# version 05
# baseline (speedup 1.0000x reference)
import numpy as np
import ml_dtypes
from contextlib import ExitStack

import concourse.bass as bass
import concourse.mybir as mybir
from concourse.bass_utils import run_bass_kernel_spmd

F32 = mybir.dt.float32
BF16 = mybir.dt.bfloat16
AF = mybir.ActivationFunctionType
ALU = mybir.AluOpType
AX = mybir.AxisListType.X
BF = ml_dtypes.bfloat16

MAGIC = 12582912.0
QMAX_R = 1.0 / 127.0
QEPS = 1e-8
LNEPS = 1e-5

T = 2048
TH = 1024
DM = 2048
KT = 16
NN = 16
TC = 512
NTC = T // TC
II = 5632
NI = 44

ENGINES = ("sync", "gpsimd", "scalar", "vector", "tensor")
RG_PAIRS = [[0, 1], [2, 3], [4, 5], [6, 7]]
SEMS = ("ld", "gp", "sv", "ss", "st", "cc")


class Plan:
    def __init__(self):
        self.ops = {e: [] for e in ENGINES}
        self.counts = {s: 0 for s in SEMS}
        self.bank_free = [[] for _ in range(8)]
        self.phase_marks = {}

    def op(self, eng, fn, waits=(), inc=None):
        self.ops[eng].append((fn, [(s, int(v)) for (s, v) in waits if v > 0],
                              inc))
        if inc is not None:
            self.counts[inc[0]] += inc[1]

    def c(self, name):
        return self.counts[name]

    def bank_wait(self, i):
        return list(self.bank_free[i])

    def bank_set(self, i, sem):
        self.bank_free[i] = [(sem, self.counts[sem])]

    def phase_mark(self, name):
        self.phase_marks[name] = {e: len(self.ops[e]) for e in ENGINES}

    def truncate(self, name):
        m = self.phase_marks[name]
        for e in ENGINES:
            self.ops[e] = self.ops[e][:m[e]]

    def emit_engine(self, ename, eng, sem_handles):
        hi = {}
        for fn, waits, inc in self.ops[ename]:
            for s, v in waits:
                if hi.get(s, 0) < v:
                    eng.wait_ge(sem_handles[s], v)
                    hi[s] = v
            ins = fn(eng)
            if inc is not None:
                ins.then_inc(sem_handles[inc[0]], inc[1])


def build_nc(cut=None, dbg=False):
    nc = bass.Bass("TRN2", target_bir_lowering=False, debug=False,
                   detect_race_conditions=False)

    def din(name, shape, dt=F32):
        return nc.dram_tensor(name, shape, dt, kind="ExternalInput").ap()

    x_full = din("x_full", [T, DM])
    x_ffn = din("x_ffn", [TH, DM])
    x_ffn_t = din("x_ffn_t", [DM, TH])
    w_in_t = din("w_in_t", [DM, 5120])
    w_dtin_t = din("w_dtin_t", [DM, 128])
    w_dtp_t = din("w_dtp_t", [128, 2048])
    w_out_t = din("w_out_t", [2048, DM])
    w_gu = din("w_gu", [2 * II, DM])
    w_dn = din("w_dn", [DM, II])
    dt_b = din("dt_b", [128, KT])
    a_log = din("a_log", [128, KT * NN])
    conv_w = din("conv_w", [128, KT * 4])
    conv_b = din("conv_b", [128, KT])
    d_vec = din("d_vec", [128, KT])
    eb_in = din("eb_in", [128, NN * 128], BF16)
    ec_in = din("ec_in", [128, 8 * 128], BF16)
    ex_in = din("ex_in", [128, 2 * 128], BF16)
    ones_in = din("ones_in", [8, 8 * 128], BF16)
    ident_bf = din("ident_bf", [128, 128], BF16)
    fconst = din("fconst", [128, 8])
    ident_f32 = din("ident_f32", [128, 128], F32)

    zs_dram = nc.dram_tensor("zs_dram", [2048, T], F32).ap()
    xact_dram = nc.dram_tensor("xact_dram", [2048, T], F32).ap()
    b_dram = nc.dram_tensor("b_dram", [1024, T], BF16).ap()
    c_dram = nc.dram_tensor("c_dram", [4096, T], BF16).ap()
    h_dram = nc.dram_tensor("h_dram", [II, TH], F32).ap()
    yg_dram = nc.dram_tensor("yg_dram", [2048, T], BF16).ap()
    cc_in = nc.dram_tensor("cc_in", [2 * DM, TH], F32).ap()
    cc_out = nc.dram_tensor("cc_out", [DM, TH], F32).ap()
    out_d = nc.dram_tensor("out", [DM, TH], F32, kind="ExternalOutput").ap()

    p = Plan()
    W, C = p.op, p.c
    es = ExitStack()
    sbuf = lambda name, shape, dt: es.enter_context(
        nc.sbuf_tensor(name, shape, dt))

    AR1 = sbuf("AR1", [128, 17408], F32)
    AR2 = sbuf("AR2", [128, 16384], F32)
    AR3 = sbuf("AR3", [128, 8192], F32)
    AR1b = AR1[:].bitcast(BF16)
    AR2b = AR2[:].bitcast(BF16)
    AR3b = AR3[:].bitcast(BF16)

    dtm_sb = sbuf("dtm_sb", [128, 2048], BF16)
    wdtp_sb = sbuf("wdtp_sb", [128, 2048], BF16)
    b_sb = sbuf("b_sb", [128, 2 * 2048], BF16)
    a_sb = sbuf("a_sb", [128, KT * NN], F32)
    eb_sb = sbuf("eb_sb", [128, NN * 128], BF16)
    ec_sb = sbuf("ec_sb", [128, 8 * 128], BF16)
    ex_sb = sbuf("ex_sb", [128, 2 * 128], BF16)
    ones_sb = sbuf("ones_sb", [8, 8 * 128], BF16)
    idb_sb = sbuf("idb_sb", [128, 128], BF16)
    idf_sb = sbuf("idf_sb", [128, 128], F32)
    cw_sb = sbuf("cw_sb", [128, KT * 4], F32)
    cb_sb = sbuf("cb_sb", [128, KT], F32)
    db_sb = sbuf("db_sb", [128, KT], F32)
    dv_sb = sbuf("dv_sb", [128, KT], F32)
    cols = sbuf("cols", [128, 64], F32)
    mxmn = sbuf("mxmn", [128, 2 * TH], F32)
    sgh = sbuf("sgh", [128, 2048], F32)
    fc_sb = sbuf("fc_sb", [128, 8], F32)

    ps = [es.enter_context(nc.psum_tensor(f"ps{i}", [128, TC], F32))
          for i in range(8)]

    xn_fm = AR1b
    win_bf = [AR2b[:, 0:2048], AR2b[:, 2048:4096]]
    feat_f = AR2[:, 2048:4096]
    fhi = AR2b[:, 8192:10240]
    flo = AR2b[:, 10240:12288]
    zst2 = [AR2[:, 6144:6656], AR2[:, 6656:7168]]
    xln = AR2[:, 7168:9216]
    sqscr = AR2[:, 9216:11264]
    xnt_f = AR2[:, 11264:13312]
    xep = AR2[:, 2048:4099]
    cacc = AR2[:, 4608:6656]
    xastage = AR2[:, 0:2048]
    x_ilv = AR3b
    xact_k = [AR1[:, 0:2048], AR1[:, 2048:4096]]
    dt_kv = AR1[:, 4096:6144]
    dtx_kv = AR1[:, 6144:8192]
    dA_k = [AR1[:, 8192:10240], AR1[:, 10240:12288]]
    t1_k = dA_k[0]
    zs_one = AR1[:, 12288:14336]
    qTg_bf = AR1b[:, 28672:30720]
    qTu_bf = AR1b[:, 30720:32768]
    ygst = AR1b[:, 32768:34816]
    h_sb = AR3[:, 0:2048]
    u_sb = [AR3[:, 2048:2560], AR3[:, 2560:3072]]
    yb3 = [AR3b[:, 6144:6656], AR3b[:, 6656:7168], AR3b[:, 15360:15872]]
    c_sb = [[AR3b[:, 7168 + (2 * par + ti) * 2048:
                  7168 + (2 * par + ti + 1) * 2048]
             for ti in range(2)] for par in range(2)]
    wout_bf = [AR1b[:, 0:2048], AR1b[:, 2048:4096]]
    att_st = [AR1[:, 4096:4608], AR1[:, 4608:5120]]
    xq_fm = AR2b[:, 0:16384]
    hq_a = AR2b
    hq_b = AR3b[:, 0:12288]

    def hq_ap(kI, c0, c1):
        if kI < 32:
            return hq_a[:, kI * 1024 + c0: kI * 1024 + c1]
        return hq_b[:, (kI - 32) * 1024 + c0: (kI - 32) * 1024 + c1]

    wgu_g = [AR2[:, 8192:10240], AR2[:, 10240:12288]]
    wgu_u = [AR2[:, 12288:14336], AR2[:, 14336:16384]]
    sg2 = [sgh[:, 0:512], sgh[:, 512:1024]]
    hst2 = [sgh[:, 1024:1536], sgh[:, 1536:2048]]
    mx_sb = mxmn[:, 0:TH]
    mn_sb = mxmn[:, TH:2 * TH]
    wdn_f = AR1[:, 0:5632]
    q2n_bf = AR1b[:, 11264:16896]
    q2T_bf = AR1b[:, 16896:22528]
    shb_sb = AR3[:, 6144:7168]
    rhb_sb = AR3[:, 7168:8192]
    dnst2 = [mxmn[:, 0:512], mxmn[:, 512:1024]]
    att_rl = [mxmn[:, 1024:1536], mxmn[:, 1536:2048]]
    xres2 = [AR1[:, 5120:5632], AR1[:, 5632:6144]]
    hax = AR3[:, 6144:7168]
    shrow = AR1[0:8, 6144:6272]
    rhrow = AR1[0:8, 6272:6400]
    shrh = AR1b[0:8, 12800:12928]
    shrl = AR1b[0:8, 12928:13056]
    rhrh = AR1b[0:8, 13056:13184]
    rhrl = AR1b[0:8, 13184:13312]

    mean_c = cols[:, 0:1]
    sum_c = cols[:, 1:2]
    m2_c = cols[:, 2:3]
    var_c = cols[:, 3:4]
    rstd_c = cols[:, 4:5]
    aw_c = cols[:, 5:6]
    sw_g2 = [cols[:, 6:7], cols[:, 32:33]]
    sw_u2 = [cols[:, 7:8], cols[:, 33:34]]
    rw_g2 = [cols[:, 8:9], cols[:, 34:35]]
    rw_u2 = [cols[:, 9:10], cols[:, 35:36]]
    sx_c = cols[:, 10:11]
    rx_c = cols[:, 11:12]
    sw2_c = cols[:, 12:13]
    rw2_c = cols[:, 13:14]
    shcol = cols[:, 16:24]
    rhcol = cols[:, 24:32]

    for dst, src in ((a_sb[:], a_log), (cw_sb[:], conv_w), (cb_sb[:], conv_b),
                     (db_sb[:], dt_b), (dv_sb[:], d_vec), (ec_sb[:], ec_in),
                     (ex_sb[:], ex_in), (idb_sb[:], ident_bf),
                     (idf_sb[:], ident_f32), (eb_sb[:], eb_in),
                     (ones_sb[:], ones_in), (fc_sb[:], fconst)):
        W("sync", lambda e, d=dst, s=src: e.dma_start(out=d, in_=s),
          inc=("ld", 16))
    consts_ld = C("ld") + 16
    W("gpsimd", lambda e: e.dma_start(out=wdtp_sb[:], in_=w_dtp_t),
      inc=("gp", 16))
    W("scalar", lambda e: e.activation(out=a_sb[:], in_=a_sb[:], func=AF.Exp),
      waits=[("ld", consts_ld)], inc=("ss", 1))
    W("vector", lambda e: e.tensor_scalar(out=a_sb[:], in0=a_sb[:],
                                          scalar1=-1.0, scalar2=None,
                                          op0=ALU.mult),
      waits=[("ss", C("ss"))], inc=("sv", 1))

    def emit_ln_quant(src_rows, out_f, quantize, extra_ld_waits,
                      xln=None, sqscr=None):
        W("sync", lambda e, xln=xln: e.dma_start(out=xln, in_=src_rows),
          waits=extra_ld_waits, inc=("ld", 16))
        myld = C("ld")
        W("vector", lambda e: e.tensor_reduce(out=sum_c, in_=xln, axis=AX,
                                              op=ALU.add),
          waits=[("ld", myld)], inc=("sv", 1))
        sv_red = C("sv")
        W("scalar", lambda e: e.activation(out=sqscr, in_=xln,
                                           func=AF.Square, accum_out=m2_c),
          waits=[("ld", myld)], inc=("ss", 1))
        ss_sq = C("ss")
        W("vector", lambda e: e.tensor_scalar(out=mean_c, in0=sum_c,
                                              scalar1=1.0 / DM, scalar2=None,
                                              op0=ALU.mult),
          waits=[("sv", sv_red)], inc=("sv", 1))
        sv_mean = C("sv")
        W("vector", lambda e: e.tensor_scalar(out=m2_c, in0=m2_c,
                                              scalar1=1.0 / DM, scalar2=None,
                                              op0=ALU.mult),
          waits=[("ss", ss_sq)], inc=("sv", 1))
        sv_m2 = C("sv")
        W("vector", lambda e: e.scalar_tensor_tensor(
            out=var_c, in0=mean_c, scalar=mean_c, in1=m2_c,
            op0=ALU.mult, op1=ALU.subtract),
          waits=[("sv", sv_m2)], inc=("sv", 1))
        sv_var = C("sv")
        W("scalar", lambda e: e.activation(out=rstd_c, in_=var_c, func=AF.Ln,
                                           scale=-1.0, bias=fc_sb[:, 0:1]),
          waits=[("sv", sv_var)], inc=("ss", 1))
        ss_ln = C("ss")
        W("scalar", lambda e: e.activation(out=rstd_c, in_=rstd_c,
                                           func=AF.Exp, scale=fc_sb[:, 2:3]),
          waits=[("ss", ss_ln)], inc=("ss", 1))
        ss_rstd = C("ss")
        if not quantize:
            W("vector", lambda e: e.tensor_scalar(
                out=out_f, in0=xln, scalar1=mean_c, scalar2=rstd_c,
                op0=ALU.subtract, op1=ALU.mult),
              waits=[("ss", ss_rstd)], inc=("sv", 1))
            return C("sv"), ss_rstd
        W("vector", lambda e: e.tensor_scalar(
            out=sqscr, in0=xln, scalar1=mean_c, scalar2=rstd_c,
            op0=ALU.subtract, op1=ALU.mult),
          waits=[("ss", ss_rstd)], inc=("sv", 1))
        W("vector", lambda e: e.tensor_reduce(out=aw_c, in_=sqscr, axis=AX,
                                              op=ALU.max,
                                              apply_absolute_value=True),
          inc=("sv", 1))
        sv_aw = C("sv")
        W("vector", lambda e: e.tensor_scalar(out=sx_c, in0=aw_c,
                                              scalar1=QMAX_R, scalar2=QEPS,
                                              op0=ALU.mult, op1=ALU.max),
          waits=[("sv", sv_aw)], inc=("sv", 1))
        sv_sx = C("sv")
        W("vector", lambda e: e.reciprocal(out=rx_c, in_=sx_c),
          waits=[("sv", sv_sx)], inc=("sv", 1))
        sv_q = C("sv")
        W("scalar", lambda e: e.activation(out=sqscr, in_=sqscr,
                                           func=AF.Identity, scale=rx_c,
                                           bias=fc_sb[:, 1:2]),
          waits=[("sv", sv_q)], inc=("ss", 1))
        ss_rnd = C("ss")
        W("vector", lambda e: e.tensor_scalar(out=out_f, in0=sqscr,
                                              scalar1=MAGIC, scalar2=sx_c,
                                              op0=ALU.subtract, op1=ALU.mult),
          waits=[("ss", ss_rnd)], inc=("sv", 1))
        return C("sv"), ss_rnd

    def emit_tile_transpose(src_bf, dstv, tt, nmt, sv_src):
        for msq in range((nmt + 3) // 4):
            bi = 4 + (msq % 2)
            bank = ps[bi]
            cnt = min(4, nmt - msq * 4)
            for s4 in range(cnt):
                ms = msq * 4 + s4
                wl = p.bank_wait(bi) if s4 == 0 else []
                if s4 == 0:
                    wl = wl + [("sv", sv_src)]
                W("tensor", lambda e, ms=ms, s4=s4, bank=bank, src_bf=src_bf:
                  e.transpose(bank[:, s4 * 128:(s4 + 1) * 128],
                              src_bf[:, ms * 128:(ms + 1) * 128], idf_sb[:]),
                  waits=wl, inc=("st", 1))
            st_g = C("st")
            dst = dstv[:, msq * 4: msq * 4 + cnt, tt * 128:(tt + 1) * 128]
            W("vector", lambda e, dst=dst, bank=bank, cnt=cnt: e.tensor_copy(
                out=dst,
                in_=bank[:, 0:cnt * 128].rearrange("p (s t) -> p s t", s=cnt)),
              waits=[("st", st_g)], inc=("sv", 1))
            p.bank_set(bi, "sv")
        return C("st"), C("sv")

    xnv = AR1b[:, 0:32768].rearrange("p (mt t) -> p mt t", mt=16)
    prev_users = [("sv", 0), ("ss", 0)]
    for tt in range(16):
        sv_xn, ss_u = emit_ln_quant(x_full[tt * 128:(tt + 1) * 128, :],
                                    xnt_f, False, prev_users,
                                    xln=xln[:], sqscr=sqscr[:])
        st_t, sv_t = emit_tile_transpose(xnt_f, xnv, tt, 16, sv_xn)
        prev_users = [("sv", sv_xn), ("ss", ss_u), ("st", st_t)]
    xn_sv = C("sv")
    p.phase_mark("P0")

    wdtin_v = w_dtin_t.rearrange("(mt p) o -> p mt o", p=128)
    wb0 = win_bf[0][:].rearrange("p (mt o) -> p mt o", mt=16)
    W("gpsimd", lambda e: e.dma_start(out=wb0, in_=wdtin_v), inc=("gp", 16))
    dtin_gp = C("gp")
    for tcp in range(4):
        bi = tcp % 2
        bank = ps[bi]
        wl0 = p.bank_wait(bi) + [("gp", dtin_gp), ("sv", xn_sv)]
        for mt in range(16):
            W("tensor", lambda e, tcp=tcp, mt=mt, bank=bank: e.matmul(
                bank[:], win_bf[0][:, mt * 128:(mt + 1) * 128],
                xn_fm[:, mt * 2048 + tcp * 512: mt * 2048 + (tcp + 1) * 512],
                start=(mt == 0), stop=(mt == 15)),
              waits=(wl0 if mt == 0 else []),
              inc=("st", 1) if mt == 15 else None)
        st_m = C("st")
        W("vector", lambda e, tcp=tcp, bank=bank: e.tensor_copy(
            out=dtm_sb[:, tcp * 512:(tcp + 1) * 512], in_=bank[:]),
          waits=[("st", st_m)], inc=("sv", 1))
        p.bank_set(bi, "sv")
    dtm_sv = C("sv")
    p.phase_mark("P1")

    win_v = w_in_t.rearrange("(mt p) o -> p mt o", p=128)
    gp_scatter_mark = 0
    zst_used = [0, 0]
    win_used = [[("sv", 0)], [("gp", dtin_gp), ("sv", dtm_sv), ("st", 0)]]
    win_used[1] = [("st", C("st"))]
    win_free = [[("st", C("st"))], []]
    for ot in range(40):
        par = ot % 2
        wb = win_bf[par][:].rearrange("p (mt o) -> p mt o", mt=16)
        W("gpsimd", lambda e, ot=ot, wb=wb: e.dma_start(
            out=wb, in_=win_v[:, :, ot * 128:(ot + 1) * 128]),
          waits=win_free[par], inc=("gp", 16))
        my_gp = C("gp")
        for tc in range(4):
            bi = (ot * 4 + tc) % 4
            bank = ps[bi]
            wl0 = p.bank_wait(bi) + [("gp", my_gp)]
            if ot == 0 and tc == 0:
                wl0.append(("sv", xn_sv))
            for mt in range(16):
                W("tensor", lambda e, ot=ot, tc=tc, mt=mt, bank=bank, par=par:
                  e.matmul(bank[:], win_bf[par][:, mt * 128:(mt + 1) * 128],
                           xn_fm[:, mt * 2048 + tc * 512:
                                 mt * 2048 + (tc + 1) * 512],
                           start=(mt == 0), stop=(mt == 15)),
                  waits=(wl0 if mt == 0 else []),
                  inc=("st", 1) if mt == 15 else None)
            st_m = C("st")
            if ot < 16:
                zpar = (ot * 4 + tc) % 2
                W("scalar", lambda e, bank=bank, zpar=zpar: e.activation(
                    out=zst2[zpar][:], in_=bank[:], func=AF.Silu),
                  waits=[("st", st_m), ("gp", zst_used[zpar])], inc=("ss", 1))
                ss_z = C("ss")
                p.bank_set(bi, "ss")
                W("gpsimd", lambda e, ot=ot, tc=tc, zpar=zpar: e.dma_start(
                    out=zs_dram[ot * 128:(ot + 1) * 128,
                                tc * 512:(tc + 1) * 512], in_=zst2[zpar][:]),
                  waits=[("ss", ss_z)], inc=("gp", 16))
                zst_used[zpar] = C("gp")
            else:
                W("vector", lambda e, bank=bank, tc=tc: e.tensor_copy(
                    out=feat_f[:, tc * 512:(tc + 1) * 512], in_=bank[:]),
                  waits=[("st", st_m), ("gp", gp_scatter_mark)],
                  inc=("sv", 1))
                p.bank_set(bi, "sv")
        win_free[par] = [("st", C("st"))]
        if ot >= 16:
            W("vector", lambda e: e.tensor_copy(out=fhi[:], in_=feat_f[:]),
              inc=("sv", 1))
            W("vector", lambda e: e.tensor_tensor(out=flo[:], in0=feat_f[:],
                                                  in1=fhi[:],
                                                  op=ALU.subtract),
              inc=("sv", 1))
            sv_hl = C("sv")
            if ot < 20:
                oti = ot - 16
                for kk in range(4):
                    k = oti * 4 + kk
                    for s, srcb in ((0, fhi), (1, flo)):
                        dst = x_ilv[(k % 2) * 64 + s * 32:
                                    (k % 2) * 64 + s * 32 + 32,
                                    (k // 2) * 2048:(k // 2 + 1) * 2048]
                        W("gpsimd", lambda e, dst=dst, srcb=srcb, kk=kk:
                          e.dma_start(out=dst,
                                      in_=srcb[kk * 32:(kk + 1) * 32, :]),
                          waits=[("sv", sv_hl)], inc=("gp", 16))
            elif ot < 24:
                oti = ot - 20
                for kk in range(4):
                    k = oti * 4 + kk
                    for s, srcb in ((0, fhi), (1, flo)):
                        dst = b_dram[k * 64 + s * 32: k * 64 + s * 32 + 32, :]
                        W("gpsimd", lambda e, dst=dst, srcb=srcb, kk=kk:
                          e.dma_start(out=dst,
                                      in_=srcb[kk * 32:(kk + 1) * 32, :]),
                          waits=[("sv", sv_hl)], inc=("gp", 16))
            else:
                k = ot - 24
                cv = c_dram.rearrange("(k n s h) t -> k n s h t",
                                      k=16, n=16, s=2, h=8)
                for s, srcb in ((0, fhi), (1, flo)):
                    dst = cv[k, :, s, :, :]
                    W("gpsimd", lambda e, dst=dst, srcb=srcb: e.dma_start(
                        out=dst, in_=srcb[:]),
                      waits=[("sv", sv_hl)], inc=("gp", 16))
            gp_scatter_mark = C("gp")
    p2_gp = C("gp")
    p.phase_mark("P2")

    xa_gp = 0
    for k in range(KT):
        bi = 4 + (k % 2)
        bank = ps[bi]
        W("vector", lambda e: e.memset(xep[:, 0:3], 0.0), inc=("sv", 1))
        for tcp in range(4):
            wl = p.bank_wait(bi)
            if k == 0 and tcp == 0:
                wl = wl + [("gp", p2_gp), ("ld", consts_ld)]
            W("tensor", lambda e, k=k, tcp=tcp, bank=bank: e.matmul(
                bank[:], ex_sb[(k % 2) * 64:(k % 2) * 64 + 64,
                               (k % 2) * 128:(k % 2) * 128 + 128],
                x_ilv[(k % 2) * 64:(k % 2) * 64 + 64,
                                         (k // 2) * 2048 + tcp * 512:
                                         (k // 2) * 2048 + (tcp + 1) * 512],
                start=True, stop=True),
              waits=wl, inc=("st", 1))
            st_m = C("st")
            W("vector", lambda e, tcp=tcp, bank=bank: e.tensor_copy(
                out=xep[:, 3 + tcp * 512: 3 + (tcp + 1) * 512], in_=bank[:]),
              waits=[("st", st_m)], inc=("sv", 1))
            p.bank_set(bi, "sv")
        W("vector", lambda e, k=k: e.tensor_scalar(
            out=cacc[:], in0=xep[:, 3:3 + T],
            scalar1=cw_sb[:, k * 4 + 3: k * 4 + 4],
            scalar2=None, op0=ALU.mult), inc=("sv", 1))
        for tap in range(3):
            W("vector", lambda e, k=k, tap=tap: e.scalar_tensor_tensor(
                out=cacc[:], in0=xep[:, tap:tap + T],
                scalar=cw_sb[:, k * 4 + tap: k * 4 + tap + 1], in1=cacc[:],
                op0=ALU.mult, op1=ALU.add), inc=("sv", 1))
        sv_cv = C("sv")
        W("scalar", lambda e, k=k: e.activation(
            out=xastage[:], in_=cacc[:], func=AF.Silu,
            bias=cb_sb[:, k:k + 1]),
          waits=[("sv", sv_cv), ("gp", xa_gp)], inc=("ss", 1))
        ss_xa = C("ss")
        W("gpsimd", lambda e, k=k: e.dma_start(
            out=xact_dram[k * 128:(k + 1) * 128, :], in_=xastage[:]),
          waits=[("ss", ss_xa)], inc=("gp", 16))
        xa_gp = C("gp")
    p2b_gp = C("gp")
    p2b_sv = C("sv")
    p.phase_mark("P2b")

    xqv = AR2b[:, 0:16384].rearrange("p (mt t) -> p mt t", mt=16)
    xlnq = AR1[:, 0:2048]
    sqscrq = AR1[:, 2048:4096]
    xntq = AR1[:, 4096:6144]
    prev_users = [("sv", p2b_sv), ("st", C("st")), ("gp", p2_gp)]
    for tt in range(8):
        sv_xq, ss_u = emit_ln_quant(x_ffn[tt * 128:(tt + 1) * 128, :],
                                    xntq, True, prev_users,
                                    xln=xlnq, sqscr=sqscrq)
        st_t, sv_t = emit_tile_transpose(xntq, xqv, tt, 16, sv_xq)
        prev_users = [("sv", sv_xq), ("ss", ss_u), ("st", st_t)]
    xq_sv = C("sv")
    p5a_st = C("st")
    p5a_ss = C("ss")
    p.phase_mark("P5A")

    def load_w_tile(wdst, src, rows0, wl_extra):
        W("sync", lambda e, wdst=wdst: e.dma_start(
            out=wdst, in_=src[rows0:rows0 + 128, :]),
          waits=wl_extra, inc=("ld", 16))
        return C("ld")

    def quant_w_tile(wdst, myld, s_col, r_col):
        W("vector", lambda e, wdst=wdst: e.tensor_reduce(
            out=aw_c, in_=wdst, axis=AX, op=ALU.max,
            apply_absolute_value=True),
          waits=[("ld", myld)], inc=("sv", 1))
        sv_aw = C("sv")
        W("vector", lambda e, s_col=s_col: e.tensor_scalar(
            out=s_col, in0=aw_c, scalar1=QMAX_R, scalar2=QEPS,
            op0=ALU.mult, op1=ALU.max),
          waits=[("sv", sv_aw)], inc=("sv", 1))
        sv_sc = C("sv")
        W("vector", lambda e, s_col=s_col, r_col=r_col: e.reciprocal(
            out=r_col, in_=s_col),
          waits=[("sv", sv_sc)], inc=("sv", 1))
        svq = C("sv")
        W("scalar", lambda e, wdst=wdst, r_col=r_col: e.activation(
            out=wdst, in_=wdst, func=AF.Identity, scale=r_col, bias=fc_sb[:, 1:2]),
          waits=[("sv", svq)], inc=("ss", 1))
        ssr = C("ss")
        W("vector", lambda e, wdst=wdst: e.tensor_scalar(
            out=wdst, in0=wdst, scalar1=MAGIC, scalar2=None,
            op0=ALU.subtract),
          waits=[("ss", ssr)], inc=("sv", 1))
        return C("sv"), ssr

    def transpose_q(qsrc, qTdst, nmt, sv_src, evac_eng):
        qTv = qTdst.rearrange("p (mt o) -> p mt o", mt=nmt)
        last = 0
        for msq in range((nmt + 3) // 4):
            bi = 4 + (msq % 2)
            bank = ps[bi]
            cnt = min(4, nmt - msq * 4)
            for s4 in range(cnt):
                ms = msq * 4 + s4
                wl = (p.bank_wait(bi) + [("sv", sv_src)]) if s4 == 0 else []
                W("tensor", lambda e, ms=ms, s4=s4, bank=bank, qsrc=qsrc:
                  e.transpose(bank[:, s4 * 128:(s4 + 1) * 128],
                              qsrc[:, ms * 128:(ms + 1) * 128], idf_sb[:]),
                  waits=wl, inc=("st", 1))
            st_g = C("st")
            dst = qTv[:, msq * 4: msq * 4 + cnt, :]
            src = lambda bank=bank, cnt=cnt: bank[:, 0:cnt * 128].rearrange(
                "p (s t) -> p s t", s=cnt)
            if evac_eng == "vector":
                W("vector", lambda e, dst=dst, bank=bank, cnt=cnt:
                  e.tensor_copy(out=dst, in_=bank[:, 0:cnt * 128].rearrange(
                      "p (s t) -> p s t", s=cnt)),
                  waits=[("st", st_g)], inc=("sv", 1))
                p.bank_set(bi, "sv")
                last = ("sv", C("sv"))
            else:
                W("scalar", lambda e, dst=dst, bank=bank, cnt=cnt: e.copy(
                    out=dst, in_=bank[:, 0:cnt * 128].rearrange(
                        "p (s t) -> p s t", s=cnt)),
                  waits=[("st", st_g)], inc=("ss", 1))
                p.bank_set(bi, "ss")
                last = ("ss", C("ss"))
        return last

    ffn_state = {"h_gp": 0, "prev_q_gp": 0}
    pair_st = {}
    sg_free = [0, 0]
    ffn_q = []
    ffn_qi = [0]
    ffn_bank = [6]

    def drip(nops):
        done = 0
        while done < nops and ffn_qi[0] < len(ffn_q):
            ffn_q[ffn_qi[0]]()
            ffn_qi[0] += 1
            done += 1

    def next_fb():
        fb = ffn_bank[0]
        ffn_bank[0] = 13 - fb
        return fb

    def queue_ffn_pair(i):
        par = i % 2
        m = {}
        qT2 = {"g": qTg_bf, "u": qTu_bf}
        wt2 = {"g": wgu_g[par], "u": wgu_u[par]}
        sw2 = {"g": sw_g2[par], "u": sw_u2[par]}
        rw2 = {"g": rw_g2[par], "u": rw_u2[par]}

        def c_load():
            wlg = [("sv", xq_sv), ("gp", ffn_state["prev_q_gp"])]
            if i >= 2:
                wlg.append(("st", pair_st[i - 2]))
            W("sync", lambda e: e.dma_start(
                out=wgu_g[par], in_=w_gu[i * 128:(i + 1) * 128, :]),
              waits=wlg, inc=("ld", 16))
            W("sync", lambda e: e.dma_start(
                out=wgu_u[par],
                in_=w_gu[(NI + i) * 128:(NI + i + 1) * 128, :]),
              inc=("ld", 16))
            m["ld"] = C("ld")
        ffn_q.append(c_load)

        def c_scale(gu):
            W("vector", lambda e: e.tensor_reduce(
                out=sw2[gu], in_=wt2[gu], axis=AX, op=ALU.max,
                apply_absolute_value=True),
              waits=[("ld", m["ld"])], inc=("sv", 1))
            hop = C("sv")
            W("vector", lambda e: e.tensor_scalar(
                out=sw2[gu], in0=sw2[gu], scalar1=QMAX_R, scalar2=QEPS,
                op0=ALU.mult, op1=ALU.max),
              waits=[("sv", hop)], inc=("sv", 1))
            hop2 = C("sv")
            W("vector", lambda e: e.reciprocal(out=rw2[gu], in_=sw2[gu]),
              waits=[("sv", hop2)], inc=("sv", 1))
            m["rec"] = C("sv")
        ffn_q.append(lambda: c_scale("g"))
        ffn_q.append(lambda: c_scale("u"))

        def c_round(gu):
            W("scalar", lambda e: e.activation(
                out=wt2[gu], in_=wt2[gu], func=AF.Identity, scale=rw2[gu],
                bias=fc_sb[:, 1:2]),
              waits=[("sv", m["rec"])], inc=("ss", 1))
            ssr = C("ss")
            W("gpsimd", lambda e: e.tensor_scalar(
                out=wt2[gu], in0=wt2[gu], scalar1=MAGIC, scalar2=None,
                op0=ALU.subtract),
              waits=[("ss", ssr)], inc=("gp", 1))
            m["q_" + gu] = C("gp")
        ffn_q.append(lambda: c_round("g"))
        ffn_q.append(lambda: c_round("u"))

        def c_tr(gu, grp):
            fb = next_fb()
            bank = ps[fb]
            qTv = qT2[gu].rearrange("p (mt o) -> p mt o", mt=16)
            for s4 in range(4):
                ms = grp * 4 + s4
                wl = (p.bank_wait(fb) + [("gp", m["q_" + gu])]
                      if s4 == 0 else [])
                W("tensor", lambda e, ms=ms, s4=s4, bank=bank, gu=gu:
                  e.transpose(bank[:, s4 * 128:(s4 + 1) * 128],
                              wt2[gu][:, ms * 128:(ms + 1) * 128],
                              idf_sb[:]),
                  waits=wl, inc=("st", 1))
            st_g = C("st")
            W("scalar", lambda e, grp=grp, bank=bank, qTv=qTv: e.copy(
                out=qTv[:, grp * 4: grp * 4 + 4, :],
                in_=bank[:, 0:512].rearrange("p (s t) -> p s t", s=4)),
              waits=[("st", st_g)], inc=("ss", 1))
            p.bank_set(fb, "ss")
            m["tr_" + gu] = C("ss")
        for gu in ("g", "u"):
            for grp in range(4):
                ffn_q.append(lambda gu=gu, grp=grp: c_tr(gu, grp))

        def c_chain(gu, tc, q4):
            if q4 == 0:
                fb = next_fb()
                m["fb_" + gu + str(tc)] = fb
                wl0 = (p.bank_wait(fb)
                       + [("ss", m["tr_g"]), ("ss", m["tr_u"]),
                          ("sv", xq_sv)])
            else:
                fb = m["fb_" + gu + str(tc)]
                wl0 = []
            bank = ps[fb]
            for mt in range(q4 * 4, q4 * 4 + 4):
                W("tensor", lambda e, mt=mt, tc=tc, bank=bank, gu=gu:
                  e.matmul(bank[:], qT2[gu][:, mt * 128:(mt + 1) * 128],
                           xq_fm[:, mt * 1024 + tc * 512:
                                 mt * 1024 + (tc + 1) * 512],
                           start=(mt == 0), stop=(mt == 15)),
                  waits=(wl0 if mt == q4 * 4 else []),
                  inc=("st", 1) if mt == 15 else None)
            if mt == 15:
                m["mm_" + gu + str(tc)] = C("st")

        def c_silu(tc):
            hpar = (i * 2 + tc) % 2
            fbG = m["fb_g" + str(tc)]
            W("scalar", lambda e, hpar=hpar, fbG=fbG: e.activation(
                out=sg2[hpar], in_=ps[fbG][:], func=AF.Silu,
                scale=sw2["g"]),
              waits=[("st", m["mm_g" + str(tc)]), ("sv", sg_free[hpar])],
              inc=("ss", 1))
            m["sg" + str(tc)] = C("ss")
            p.bank_set(fbG, "ss")

        def c_h(tc):
            hpar = (i * 2 + tc) % 2
            fbU = m["fb_u" + str(tc)]
            wl_h = [("ss", m["sg" + str(tc)]),
                    ("st", m["mm_u" + str(tc)]),
                    ("gp", ffn_state["h_gp"])]
            if i == 0:
                wl_h.append(("sv", mxmn_init_sv))
            W("vector", lambda e, hpar=hpar, fbU=fbU:
              e.scalar_tensor_tensor(
                  out=hst2[hpar], in0=ps[fbU][:], scalar=sw2["u"],
                  in1=sg2[hpar], op0=ALU.mult, op1=ALU.mult),
              waits=wl_h, inc=("sv", 1))
            sv_h = C("sv")
            sg_free[hpar] = sv_h
            p.bank_set(fbU, "sv")
            W("vector", lambda e, tc=tc, hpar=hpar: e.tensor_tensor(
                out=mx_sb[:, tc * 512:(tc + 1) * 512],
                in0=mx_sb[:, tc * 512:(tc + 1) * 512],
                in1=hst2[hpar], op=ALU.max), inc=("sv", 1))
            W("vector", lambda e, tc=tc, hpar=hpar: e.tensor_tensor(
                out=mn_sb[:, tc * 512:(tc + 1) * 512],
                in0=mn_sb[:, tc * 512:(tc + 1) * 512],
                in1=hst2[hpar], op=ALU.min), inc=("sv", 1))
            m["h" + str(tc)] = sv_h

        def c_spill(tc):
            hpar = (i * 2 + tc) % 2
            W("gpsimd", lambda e, i=i, tc=tc, hpar=hpar: e.dma_start(
                out=h_dram[i * 128:(i + 1) * 128,
                           tc * 512:(tc + 1) * 512],
                in_=hst2[hpar]),
              waits=[("sv", m["h" + str(tc)])], inc=("gp", 16))
            ffn_state["h_gp"] = C("gp")

        for tc in range(2):
            for q4 in range(4):
                ffn_q.append(lambda tc=tc, q4=q4: c_chain("g", tc, q4))
            for q4 in range(4):
                ffn_q.append(lambda tc=tc, q4=q4: c_chain("u", tc, q4))
            ffn_q.append(lambda tc=tc: c_silu(tc))
            ffn_q.append(lambda tc=tc: c_h(tc))
            ffn_q.append(lambda tc=tc: c_spill(tc))

        def c_fin():
            pair_st[i] = C("st")
            ffn_state["prev_q_gp"] = m["q_u"]
        ffn_q.append(c_fin)
    ld_marks = {}

    def reload_k(k, extra_waits):
        par = k % 2
        W("sync", lambda e, k=k, par=par: e.dma_start(
            out=xact_k[par], in_=xact_dram[k * 128:(k + 1) * 128, :]),
          waits=extra_waits, inc=("ld", 16))
        for ti in range(2):
            W("sync", lambda e, k=k, par=par, ti=ti: e.dma_start(
                out=c_sb[par][ti],
                in_=c_dram[k * 256 + ti * 128: k * 256 + (ti + 1) * 128, :]),
              inc=("ld", 16))
        if k % 2 == 0:
            kp = k // 2
            W("sync", lambda e, kp=kp: e.dma_start(
                out=b_sb[:, (kp % 2) * 2048:(kp % 2 + 1) * 2048],
                in_=b_dram[kp * 128:(kp + 1) * 128, :]),
              inc=("ld", 16))
        ld_marks[k] = C("ld")

    W("vector", lambda e: e.memset(mx_sb, -3.0e38), inc=("sv", 1))
    W("vector", lambda e: e.memset(mn_sb, 3.0e38), inc=("sv", 1))
    mxmn_init_sv = C("sv")
    reload_k(0, [("gp", p2b_gp), ("sv", p2b_sv), ("ss", p5a_ss),
                 ("st", p5a_st)])
    for i in range(NI):
        queue_ffn_pair(i)
    p.phase_mark("P3r")

    kuse_st = {}
    gate_sv_hist = {}
    yg_spill_prev = 0
    zs_ld = {}
    prev_gate_sv = p2b_sv
    dA_sv_done = {}
    u_marks = {}
    yb_marks = {}
    for k in range(KT):
        par = k % 2
        myld = ld_marks[k]
        dtu = {}
        for tcp in range(4):
            bi = 4 + (tcp % 2)
            bank = ps[bi]
            wl = p.bank_wait(bi)
            if tcp == 0:
                wl = wl + [("sv", max(prev_gate_sv, dtm_sv)),
                           ("st", p5a_st)]
            W("tensor", lambda e, k=k, tcp=tcp, bank=bank: e.matmul(
                bank[:], wdtp_sb[:, k * 128:(k + 1) * 128],
                dtm_sb[:, tcp * 512:(tcp + 1) * 512],
                start=True, stop=True),
              waits=wl, inc=("st", 1))
            st_dt = C("st")
            sl = slice(tcp * 512, (tcp + 1) * 512)
            W("scalar", lambda e, k=k, bank=bank, sl=sl: e.activation(
                out=t1_k[:, sl], in_=bank[:], func=AF.Abs,
                bias=db_sb[:, k:k + 1]),
              waits=[("st", st_dt)], inc=("ss", 1))
            ss_a = C("ss")
            W("vector", lambda e, k=k, bank=bank, sl=sl: e.tensor_scalar(
                out=dt_kv[:, sl], in0=bank[:], scalar1=db_sb[:, k:k + 1],
                scalar2=0.0, op0=ALU.add, op1=ALU.max),
              waits=[("st", st_dt), ("ss", ss_a)], inc=("sv", 1))
            sv_d = C("sv")
            p.bank_free[bi] = [("ss", ss_a), ("sv", sv_d)]
        if k == 0:
            p.phase_mark("P3m")
        W("scalar", lambda e: e.activation(out=t1_k[:], in_=t1_k[:],
                                           func=AF.Exp, scale=-1.0),
          inc=("ss", 1))
        W("scalar", lambda e: e.activation(out=t1_k[:], in_=t1_k[:],
                                           func=AF.Ln, bias=1.0),
          inc=("ss", 1))
        ss_sp = C("ss")
        W("vector", lambda e: e.tensor_tensor(out=dt_kv[:], in0=dt_kv[:],
                                              in1=t1_k[:], op=ALU.add),
          waits=[("ss", ss_sp)], inc=("sv", 1))
        if k == 0:
            p.phase_mark("P3sp")
        W("vector", lambda e, par=par: e.tensor_tensor(
            out=dtx_kv[:], in0=dt_kv[:], in1=xact_k[par], op=ALU.mult),
          waits=[("ld", myld)], inc=("sv", 1))
        sv_dt = C("sv")
        if k == 0:
            p.phase_mark("P3dt")
        ss_dA = {}

        def emit_dA(n, k=k, sv_dt=sv_dt):
            npar = n % 2
            wl_dA = [("sv", sv_dt)]
            if (n - 2) in dA_sv_done:
                wl_dA.append(("sv", dA_sv_done[n - 2]))
            W("scalar", lambda e, k=k, n=n, npar=npar: e.activation(
                out=dA_k[npar][:], in_=dt_kv[:], func=AF.Exp,
                scale=a_sb[:, k * NN + n: k * NN + n + 1]),
              waits=wl_dA, inc=("ss", 1))
            ss_dA[n] = C("ss")

        emit_dA(0)
        emit_dA(1)
        sv_yb = {}
        ysum_st = {}

        def emit_ysum(jt):
            nj, tcj = jt // NTC, jt % NTC
            wl = [("sv", sv_yb[jt])]
            if nj == 0:
                wl = wl + p.bank_wait(tcj)
            W("tensor", lambda e, nj=nj, tcj=tcj, jt=jt: e.matmul(
                ps[tcj][:], idb_sb[:], yb3[jt % 3],
                start=(nj == 0), stop=(nj == NN - 1)),
              waits=wl, inc=("st", 1))
            ysum_st[jt] = C("st")

        for n in range(NN):
            npar = n % 2
            for tc in range(NTC):
                it = n * NTC + tc
                tsl = slice(tc * 512, (tc + 1) * 512)
                if tc == 0 and 1 <= n < NN - 1:
                    emit_dA(n + 1)
                wlB = p.bank_wait(4) + ([("ld", myld)] if it == 0 else [])
                W("tensor", lambda e, k=k, n=n, tsl=tsl: e.matmul(
                    ps[4][:], eb_sb[(k % 2) * 64:(k % 2) * 64 + 64,
                                    n * 128:(n + 1) * 128],
                    b_sb[(k % 2) * 64:(k % 2) * 64 + 64,
                         ((k // 2) % 2) * 2048 + tsl.start:
                         ((k // 2) % 2) * 2048 + tsl.stop],
                    start=True, stop=True),
                  waits=wlB, inc=("st", 1))
                st_B = C("st")
                W("vector", lambda e, tsl=tsl, it=it:
                  e.tensor_tensor(out=u_sb[it % 2], in0=dtx_kv[:, tsl],
                                  in1=ps[4][:], op=ALU.mult),
                  waits=[("st", st_B), ("ss", ss_dA[n])], inc=("sv", 1))
                p.bank_set(4, "sv")
                init = 0.0 if tc == 0 else h_sb[:, tc * 512 - 1: tc * 512]
                W("vector", lambda e, tsl=tsl, it=it, npar=npar,
                  init=init: e.tensor_tensor_scan(
                      out=h_sb[:, tsl], data0=dA_k[npar][:, tsl],
                      data1=u_sb[it % 2], initial=init, op0=ALU.mult,
                      op1=ALU.add), inc=("sv", 1))
                if tc == NTC - 1:
                    dA_sv_done[n] = C("sv")
                wlC = p.bank_wait(5) + ([("ld", myld)] if it == 0 else [])
                W("tensor", lambda e, k=k, n=n, tsl=tsl: e.matmul(
                    ps[5][:], ec_sb[:, (n % 8) * 128:((n % 8) + 1) * 128],
                    c_sb[k % 2][n // 8][:, tsl], start=True, stop=True),
                  waits=wlC, inc=("st", 1))
                st_C = C("st")
                wlY = [("st", st_C)]
                if (it - 3) in ysum_st:
                    wlY.append(("st", ysum_st[it - 3]))
                W("vector", lambda e, tsl=tsl, it=it:
                  e.tensor_tensor(out=yb3[it % 3], in0=h_sb[:, tsl],
                                  in1=ps[5][:], op=ALU.mult),
                  waits=wlY, inc=("sv", 1))
                p.bank_set(5, "sv")
                sv_yb[it] = C("sv")
                if it >= 1:
                    emit_ysum(it - 1)
                drip(2)
            if n == 8:
                W("sync", lambda e, k=k: e.dma_start(
                    out=zs_one, in_=zs_dram[k * 128:(k + 1) * 128, :]),
                  waits=[("sv", prev_gate_sv)], inc=("ld", 16))
                zs_ld[k] = C("ld")
            if n == 0 and k + 1 < KT:
                gates = [("sv", C("sv")), ("st", C("st"))]
                if k >= 1:
                    gates += [("sv", gate_sv_hist.get(k - 1, 0)),
                              ("st", kuse_st.get(k - 1, 0))]
                reload_k(k + 1, gates)
            if k == 0 and n == 0:
                p.phase_mark("P3n0")
        emit_ysum(NN * NTC - 1)
        st_ysum = C("st")
        for tc in range(NTC):
            tsl = slice(tc * 512, (tc + 1) * 512)
            W("vector", lambda e, k=k, tsl=tsl, tc=tc, par=par:
              e.scalar_tensor_tensor(
                  out=u_sb[0], in0=xact_k[par][:, tsl],
                  scalar=dv_sb[:, k:k + 1], in1=ps[tc][:],
                  op0=ALU.mult, op1=ALU.add),
              waits=[("st", st_ysum)], inc=("sv", 1))
            p.bank_set(tc, "sv")
            W("vector", lambda e, k=k, tsl=tsl: e.tensor_tensor(
                out=ygst[:, tsl], in0=u_sb[0], in1=zs_one[:, tsl],
                op=ALU.mult),
              waits=(([("gp", yg_spill_prev), ("ld", zs_ld[k])])
                     if tsl.start == 0 else []),
              inc=("sv", 1))
        prev_gate_sv = C("sv")
        gate_sv_hist[k] = prev_gate_sv
        W("gpsimd", lambda e, k=k: e.dma_start(
            out=yg_dram[k * 128:(k + 1) * 128, :], in_=ygst[:]),
          waits=[("sv", prev_gate_sv)], inc=("gp", 16))
        yg_spill_prev = C("gp")
        kuse_st[k] = C("st")
        drip(4)
        if k == 0:
            p.phase_mark("P3k0")
    drip(10 ** 9)
    p3_sv = C("sv")
    p3_st = C("st")
    p.phase_mark("P3")

    yg_spill_gp = C("gp")
    wo_v = w_out_t.rearrange("(mt p) o -> p mt o", p=128)
    wout_og = AR1b[:, 0:4096]
    wog_v = wout_og.rearrange("p (mt o) -> p mt o", mt=16)
    ygb = [AR1b[:, 4096:6144], AR1b[:, 6144:8192]]
    att_st2 = [AR1[:, 5120:5632], AR1[:, 5632:6144]]
    att_gp = [0, 0]
    prev_og_load_gate = [("sv", p3_sv), ("st", p3_st)]
    for og in range(8):
        W("gpsimd", lambda e, og=og: e.dma_start(
            out=wog_v, in_=wo_v[:, :, og * 256:(og + 1) * 256]),
          waits=prev_og_load_gate, inc=("gp", 16))
        my_wgp = C("gp")
        ykst = {}
        for k in range(16):
            par = k % 2
            wl_yg = [("gp", yg_spill_gp)]
            if k >= 1:
                wl_yg.append(("st", ykst.get(k - 1, 0)))
            W("sync", lambda e, k=k, par=par: e.dma_start(
                out=ygb[par], in_=yg_dram[k * 128:(k + 1) * 128, :]),
              waits=wl_yg, inc=("ld", 16))
            myld = C("ld")
            for o2 in range(2):
                for tc in range(4):
                    bi = o2 * 4 + tc
                    wl0 = [("gp", my_wgp), ("ld", myld)]
                    if k == 0:
                        wl0 = wl0 + p.bank_wait(bi)
                    W("tensor", lambda e, k=k, o2=o2, tc=tc, par=par, bi=bi:
                      e.matmul(ps[bi][:],
                               wout_og[:, k * 256 + o2 * 128:
                                       k * 256 + o2 * 128 + 128],
                               ygb[par][:, tc * 512:(tc + 1) * 512],
                               start=(k == 0), stop=(k == 15)),
                      waits=(wl0 if (o2 == 0 and tc == 0) else []),
                      inc=("st", 1) if (o2 == 1 and tc == 3) else None)
            ykst[k] = C("st")
        st_og = C("st")
        for o2 in range(2):
            for tc in range(4):
                bi = o2 * 4 + tc
                apar = (og * 8 + bi) % 2
                W("vector", lambda e, bi=bi, apar=apar: e.tensor_copy(
                    out=att_st2[apar], in_=ps[bi][:]),
                  waits=[("st", st_og), ("gp", att_gp[apar])], inc=("sv", 1))
                p.bank_set(bi, "sv")
                sv_e = C("sv")
                ot = og * 2 + o2
                tb = tc // 2
                W("gpsimd", lambda e, ot=ot, tc=tc, tb=tb, apar=apar:
                  e.dma_start(out=cc_in[tb * 2048 + ot * 128:
                                        tb * 2048 + (ot + 1) * 128,
                                        (tc % 2) * 512:((tc % 2) + 1) * 512],
                              in_=att_st2[apar]),
                  waits=[("sv", sv_e)], inc=("gp", 16))
                att_gp[apar] = C("gp")
        prev_og_load_gate = [("st", st_og)]
    p4_gp = C("gp")
    p4_st = C("st")
    p4_sv = C("sv")
    W("gpsimd", lambda e: e.collective_compute(
        "ReduceScatter", ALU.add, replica_groups=RG_PAIRS,
        ins=[cc_in.opt()], outs=[cc_out.opt()]),
      waits=[("gp", p4_gp)], inc=("cc", 1))

    p.phase_mark("P4")


    p.phase_mark("P5a")

    p5b_sv = C("sv")
    p5b_gp = C("gp")

    p.phase_mark("P5b")

    W("vector", lambda e: e.tensor_scalar(out=hax, in0=mn_sb, scalar1=-1.0,
                                          scalar2=None, op0=ALU.mult),
      waits=[("sv", p5b_sv)], inc=("sv", 1))
    W("vector", lambda e: e.tensor_tensor(out=hax, in0=hax, in1=mx_sb,
                                          op=ALU.max), inc=("sv", 1))
    sv_hax = C("sv")
    for ti in range(8):
        bi = 4 + (ti % 2)
        bank = ps[bi]
        W("tensor", lambda e, ti=ti, bank=bank: e.transpose(
            bank[:, 0:128], hax[:, ti * 128:(ti + 1) * 128], idf_sb[:]),
          waits=p.bank_wait(bi) + [("sv", sv_hax)], inc=("st", 1))
        st_t = C("st")
        W("vector", lambda e, ti=ti, bank=bank: e.tensor_reduce(
            out=shcol[:, ti:ti + 1], in_=bank[:, 0:128], axis=AX,
            op=ALU.max), waits=[("st", st_t)], inc=("sv", 1))
        p.bank_set(bi, "sv")
    sv_red8 = C("sv")
    W("vector", lambda e: e.tensor_scalar(out=shcol, in0=shcol,
                                          scalar1=QMAX_R, scalar2=QEPS,
                                          op0=ALU.mult, op1=ALU.max),
      waits=[("sv", sv_red8)], inc=("sv", 1))
    sv_shc = C("sv")
    W("vector", lambda e: e.reciprocal(out=rhcol, in_=shcol),
      waits=[("sv", sv_shc)], inc=("sv", 1))
    sv_sh = C("sv")
    W("tensor", lambda e: e.transpose(ps[6][0:8, 0:128], shcol, idf_sb[:]),
      waits=p.bank_wait(6) + [("sv", sv_sh)], inc=("st", 1))
    W("tensor", lambda e: e.transpose(ps[7][0:8, 0:128], rhcol, idf_sb[:]),
      waits=p.bank_wait(7), inc=("st", 1))
    st_r = C("st")
    W("vector", lambda e: e.tensor_copy(out=shrow, in_=ps[6][0:8, 0:128]),
      waits=[("st", st_r)], inc=("sv", 1))
    W("vector", lambda e: e.tensor_copy(out=rhrow, in_=ps[7][0:8, 0:128]),
      inc=("sv", 1))
    p.bank_set(6, "sv")
    p.bank_set(7, "sv")
    sv_rw = C("sv")
    W("vector", lambda e: e.tensor_copy(out=shrh, in_=shrow),
      waits=[("sv", sv_rw)], inc=("sv", 1))
    sv_h1 = C("sv")
    W("vector", lambda e: e.tensor_tensor(out=shrl, in0=shrow, in1=shrh,
                                          op=ALU.subtract),
      waits=[("sv", sv_h1)], inc=("sv", 1))
    W("vector", lambda e: e.tensor_copy(out=rhrh, in_=rhrow), inc=("sv", 1))
    sv_h2 = C("sv")
    W("vector", lambda e: e.tensor_tensor(out=rhrl, in0=rhrow, in1=rhrh,
                                          op=ALU.subtract),
      waits=[("sv", sv_h2)], inc=("sv", 1))
    sv_rows = C("sv")
    for dst, rh, rl in ((shb_sb, shrh, shrl), (rhb_sb, rhrh, rhrl)):
        for ti in range(8):
            bi = 4 + (ti % 2)
            bank = ps[bi]
            W("tensor", lambda e, bank=bank, rh=rh, ti=ti: e.matmul(
                bank[:, 0:128], ones_sb[:, ti * 128:(ti + 1) * 128],
                rh[0:8, :], start=True, stop=False),
              waits=p.bank_wait(bi) + [("sv", sv_rows)])
            W("tensor", lambda e, bank=bank, rl=rl, ti=ti: e.matmul(
                bank[:, 0:128], ones_sb[:, ti * 128:(ti + 1) * 128],
                rl[0:8, :], start=False, stop=True), inc=("st", 1))
            st_e = C("st")
            W("vector", lambda e, dst=dst, ti=ti, bank=bank: e.tensor_copy(
                out=dst[:, ti * 128:(ti + 1) * 128], in_=bank[:, 0:128]),
              waits=[("st", st_e)], inc=("sv", 1))
            p.bank_set(bi, "sv")
    sv_bcast = C("sv")

    p.phase_mark("P5c")

    hq_prev = {}
    hq_mult = {}
    for kI in range(NI):
        for tc in range(2):
            j = kI * 2 + tc
            hpar = j % 2
            wl = [("gp", p5b_gp)] if j == 0 else []
            if j >= 1:
                wl = wl + [("sv", hq_mult.get(j - 1, 0))]
            if j >= 2:
                wl = wl + list(hq_prev[j - 2])
            W("sync", lambda e, kI=kI, tc=tc, hpar=hpar: e.dma_start(
                out=hst2[hpar],
                in_=h_dram[kI * 128:(kI + 1) * 128,
                           tc * 512:(tc + 1) * 512]),
              waits=wl + [("sv", sv_bcast)], inc=("ld", 16))
            myld = C("ld")
            W("vector", lambda e, hpar=hpar, tc=tc: e.tensor_tensor(
                out=hst2[hpar], in0=hst2[hpar],
                in1=rhb_sb[:, tc * 512:(tc + 1) * 512], op=ALU.mult),
              waits=[("ld", myld)], inc=("sv", 1))
            svm = C("sv")
            hq_mult[j] = svm
            W("scalar", lambda e, hpar=hpar: e.activation(
                out=hst2[hpar], in_=hst2[hpar], func=AF.Identity,
                bias=fc_sb[:, 1:2]),
              waits=[("sv", svm)], inc=("ss", 1))
            ssm = C("ss")
            W("vector", lambda e, kI=kI, tc=tc, hpar=hpar: e.tensor_scalar(
                out=hq_ap(kI, tc * 512, (tc + 1) * 512), in0=hst2[hpar],
                scalar1=MAGIC, scalar2=None, op0=ALU.subtract),
              waits=[("ss", ssm)], inc=("sv", 1))
            hq_prev[j] = [("sv", C("sv")), ("ss", ssm)]
    hq_sv = C("sv")

    p.phase_mark("P5d")

    prev_dn_st = 0
    dn_prev = {}
    out_gp_prev = {}
    for dm in range(16):
        wl0 = ([("sv", hq_sv)] if dm == 0 else
               [("st", prev_dn_st)] + list(dn_prev.get(2 * (dm - 1), [])))
        dld = load_w_tile(wdn_f, w_dn, dm * 128, wl0)
        svq, _ssq = quant_w_tile(wdn_f, dld, sw2_c, rw2_c)
        mT = transpose_q(wdn_f, q2T_bf, NI, svq, "vector")
        for tc in range(2):
            bi = tc
            bank = ps[bi]
            wlD = p.bank_wait(bi) + [mT, ("sv", hq_sv)]
            for kI in range(NI):
                W("tensor", lambda e, kI=kI, tc=tc, bank=bank: e.matmul(
                    bank[:], q2T_bf[:, kI * 128:(kI + 1) * 128],
                    hq_ap(kI, tc * 512, (tc + 1) * 512),
                    start=(kI == 0), stop=(kI == NI - 1)),
                  waits=(wlD if kI == 0 else []),
                  inc=("st", 1) if kI == NI - 1 else None)
            st_mm = C("st")
            j5 = dm * 2 + tc
            dpar = j5 % 2
            wl_att = ([("cc", 1), ("sv", svq), mT]
                      + list(dn_prev.get(j5 - 1, []))
                      + list(dn_prev.get(j5 - 2, [])))
            W("sync", lambda e, dm=dm, tc=tc, dpar=dpar: e.dma_start(
                out=att_rl[dpar],
                in_=cc_out[dm * 128:(dm + 1) * 128,
                           tc * 512:(tc + 1) * 512]),
              waits=wl_att, inc=("ld", 16))
            W("sync", lambda e, dm=dm, tc=tc, dpar=dpar: e.dma_start(
                out=xres2[dpar],
                in_=x_ffn_t[dm * 128:(dm + 1) * 128,
                            tc * 512:(tc + 1) * 512]),
              inc=("ld", 16))
            myld = C("ld")
            W("vector", lambda e, tc=tc, bank=bank, dpar=dpar:
              e.scalar_tensor_tensor(
                  out=dnst2[dpar], in0=bank[:], scalar=sw2_c,
                  in1=shb_sb[:, tc * 512:(tc + 1) * 512],
                  op0=ALU.mult, op1=ALU.mult),
              waits=[("st", st_mm), ("gp", out_gp_prev.get(j5 - 2, 0))],
              inc=("sv", 1))
            p.bank_set(bi, "sv")
            W("vector", lambda e, dpar=dpar: e.tensor_tensor(
                out=dnst2[dpar], in0=dnst2[dpar], in1=att_rl[dpar],
                op=ALU.add),
              waits=[("ld", myld)], inc=("sv", 1))
            W("vector", lambda e, dpar=dpar: e.tensor_tensor(
                out=dnst2[dpar], in0=dnst2[dpar], in1=xres2[dpar],
                op=ALU.add), inc=("sv", 1))
            sv_o = C("sv")
            dn_prev[j5] = [("sv", sv_o)]
            W("gpsimd", lambda e, dm=dm, tc=tc, dpar=dpar: e.dma_start(
                out=out_d[dm * 128:(dm + 1) * 128,
                          tc * 512:(tc + 1) * 512],
                in_=dnst2[dpar]),
              waits=[("sv", sv_o)], inc=("gp", 16))
            out_gp_prev[j5] = C("gp")
        prev_dn_st = C("st")
    if dbg:
        for nm, t, shp in (("d_zs", zs_dram, [2048, T]),
                           ("d_xact", xact_dram, [2048, T]),
                           ("d_b", b_dram, [1024, T]),
                           ("d_c", c_dram, [4096, T]),
                           ("d_h", h_dram, [II, TH]),
                           ("d_ccin", cc_in, [2 * DM, TH]),
                           ("d_ccout", cc_out, [DM, TH])):
            dt_ = BF16 if nm in ("d_b", "d_c") else F32
            od = nc.dram_tensor(nm, shp, dt_, kind="ExternalOutput").ap()
            W("gpsimd", lambda e, od=od, t=t: e.dma_start(out=od, in_=t),
              waits=[("gp", C("gp")), ("cc", 1), ("sv", C("sv")),
                     ("ss", C("ss")), ("st", C("st")), ("ld", C("ld"))],
              inc=("gp", 16))
    out_gp_total = C("gp")
    W("gpsimd", lambda e: e.nop(), waits=[("gp", out_gp_total)])

    if cut is not None:
        p.truncate(cut)

    with es:
        sem_handles = {}
        for s in SEMS:
            sem_handles[s] = es.enter_context(nc.semaphore(s))
        block = es.enter_context(nc.Block())

        @block.sync
        def _(sync):
            p.emit_engine("sync", sync, sem_handles)

        @block.gpsimd
        def _(gpsimd):
            p.emit_engine("gpsimd", gpsimd, sem_handles)

        @block.scalar
        def _(scalar):
            p.emit_engine("scalar", scalar, sem_handles)

        @block.vector
        def _(vector):
            p.emit_engine("vector", vector, sem_handles)

        @block.tensor
        def _(tensor):
            p.emit_engine("tensor", tensor, sem_handles)
    return nc


def _prep_core_inputs(b, j, x, Wi, Wdti, Wdtp, dtb, cwi, cbi, alogi, dvi, Wo,
                      Wgu, Wdn, consts):
    DI, DXB = 4096, 1024
    ch0 = j * 2048
    xb = np.ascontiguousarray(x[b])
    d = {}
    d["x_full"] = xb
    d["x_ffn"] = np.ascontiguousarray(xb[j * TH:(j + 1) * TH])
    d["x_ffn_t"] = np.ascontiguousarray(d["x_ffn"].T)
    rows = [np.arange(ch0, ch0 + 2048)]
    xr = []
    for k in range(16):
        for g2 in range(2):
            g = j * 32 + 2 * k + g2
            for pp in range(16):
                xr.append(DI + g * 16 + pp)
    rows.append(np.array(xr))
    br = []
    for k in range(16):
        for n in range(16):
            for g2 in range(2):
                g = j * 32 + 2 * k + g2
                br.append(DI + DXB + g * 16 + n)
    rows.append(np.array(br))
    cr = []
    for k in range(16):
        for n in range(16):
            for hh in range(8):
                h = j * 128 + k * 8 + hh
                cr.append(DI + 2 * DXB + h * 16 + n)
    rows.append(np.array(cr))
    rows = np.concatenate(rows)
    d["w_in_t"] = np.ascontiguousarray(Wi[rows].T)
    d["w_dtin_t"] = np.ascontiguousarray(Wdti.T)
    d["w_dtp_t"] = np.ascontiguousarray(Wdtp[ch0:ch0 + 2048].T)
    d["w_out_t"] = np.ascontiguousarray(Wo[:, ch0:ch0 + 2048].T)
    d["w_gu"] = Wgu
    d["w_dn"] = Wdn
    d["dt_b"] = np.ascontiguousarray(dtb[ch0:ch0 + 2048].reshape(16, 128).T)
    d["a_log"] = np.ascontiguousarray(
        alogi[ch0:ch0 + 2048].reshape(16, 128, 16).transpose(1, 0, 2)
        .reshape(128, 256))
    d["conv_w"] = np.ascontiguousarray(
        cwi[ch0:ch0 + 2048].reshape(16, 128, 4).transpose(1, 0, 2)
        .reshape(128, 64))
    d["conv_b"] = np.ascontiguousarray(cbi[ch0:ch0 + 2048].reshape(16, 128).T)
    d["d_vec"] = np.ascontiguousarray(dvi[ch0:ch0 + 2048].reshape(16, 128).T)
    d.update(consts)
    return d


def _mk_esel8():
    e8 = np.zeros((8, 8, 128), BF)
    for ti in range(8):
        e8[ti, ti, :] = 1
    return np.ascontiguousarray(e8.reshape(8, 8 * 128))


def _make_consts():
    eb = np.zeros((128, NN, 128), BF)
    for kpar in range(2):
        for n in range(NN):
            for q in range(128):
                g = q // 64
                for s in range(2):
                    eb[kpar * 64 + s * 32 + n * 2 + g, n, q] = 1
    ec = np.zeros((128, 8, 128), BF)
    for nm in range(8):
        for q in range(128):
            hh = q // 16
            for s in range(2):
                ec[nm * 16 + s * 8 + hh, nm, q] = 1
    ex = np.zeros((128, 2, 128), BF)
    for kpar in range(2):
        for q in range(128):
            g = q // 64
            pp = q % 16
            for s in range(2):
                ex[kpar * 64 + s * 32 + g * 16 + pp, kpar, q] = 1
    return {
        "eb_in": np.ascontiguousarray(eb.reshape(128, NN * 128)),
        "ec_in": np.ascontiguousarray(ec.reshape(128, 8 * 128)),
        "ex_in": np.ascontiguousarray(ex.reshape(128, 2 * 128)),
        "ones_in": _mk_esel8(),
        "ident_bf": np.eye(128).astype(BF),
        "fconst": np.tile(np.array([[LNEPS, MAGIC, -0.5, 0, 0, 0, 0, 0]],
                                   np.float32), (128, 1)),
        "ident_f32": np.eye(128, dtype=np.float32),
    }


_NC_CACHE = {}


def kernel(L0_attn_hidden_states, L0_attn_in_proj_w, L0_attn_dt_in_proj_w,
           L0_attn_dt_proj_w, L0_attn_dt_proj_b, L0_attn_conv_w,
           L0_attn_conv_b, L0_attn_A_log, L0_attn_D, L0_attn_out_proj_w,
           L0_ffn_w_gate_up, L0_ffn_w_down):
    f = np.float32
    x = np.asarray(L0_attn_hidden_states, f)
    Wi = np.asarray(L0_attn_in_proj_w, f)
    Wdti = np.asarray(L0_attn_dt_in_proj_w, f)
    Wdtp = np.asarray(L0_attn_dt_proj_w, f)
    dtb = np.asarray(L0_attn_dt_proj_b, f)
    cwi = np.asarray(L0_attn_conv_w, f)
    cbi = np.asarray(L0_attn_conv_b, f)
    alogi = np.asarray(L0_attn_A_log, f)
    dvi = np.asarray(L0_attn_D, f)
    Wo = np.asarray(L0_attn_out_proj_w, f)
    Wgu = np.asarray(L0_ffn_w_gate_up, f)
    Wdn = np.asarray(L0_ffn_w_down, f)

    consts = _make_consts()
    in_maps = []
    for core in range(8):
        b, j = core // 2, core % 2
        in_maps.append(_prep_core_inputs(b, j, x, Wi, Wdti, Wdtp, dtb, cwi,
                                         cbi, alogi, dvi, Wo, Wgu, Wdn,
                                         consts))
    if "nc" not in _NC_CACHE:
        _NC_CACHE["nc"] = build_nc()
    nc = _NC_CACHE["nc"]
    res = run_bass_kernel_spmd(nc, in_maps, core_ids=list(range(8)))
    out = np.empty((4, T, DM), np.float32)
    for core in range(8):
        b, j = core // 2, core % 2
        out[b, j * TH:(j + 1) * TH, :] = np.asarray(
            res.results[core]["out"], np.float32).T
    return out


# version 06
# speedup vs baseline: 1.2647x; 1.2647x over previous
import numpy as np
import ml_dtypes
from contextlib import ExitStack

import concourse.bass as bass
import concourse.mybir as mybir
from concourse.bass_utils import run_bass_kernel_spmd

F32 = mybir.dt.float32
BF16 = mybir.dt.bfloat16
AF = mybir.ActivationFunctionType
ALU = mybir.AluOpType
AX = mybir.AxisListType.X
BF = ml_dtypes.bfloat16

MAGIC = 12582912.0
QMAX_R = 1.0 / 127.0
QEPS = 1e-8
LNEPS = 1e-5

T = 2048
TH = 1024
DM = 2048
KT = 16
NN = 16
TC = 512
NTC = T // TC
II = 5632
NI = 44

ENGINES = ("sync", "gpsimd", "scalar", "vector", "tensor")
RG_PAIRS = [[0, 1], [2, 3], [4, 5], [6, 7]]
SEMS = ("ld", "gp", "sv", "ss", "st", "cc")


class Plan:
    def __init__(self):
        self.ops = {e: [] for e in ENGINES}
        self.counts = {s: 0 for s in SEMS}
        self.bank_free = [[] for _ in range(8)]
        self.phase_marks = {}

    def op(self, eng, fn, waits=(), inc=None):
        self.ops[eng].append((fn, [(s, int(v)) for (s, v) in waits if v > 0],
                              inc))
        if inc is not None:
            self.counts[inc[0]] += inc[1]

    def c(self, name):
        return self.counts[name]

    def bank_wait(self, i):
        return list(self.bank_free[i])

    def bank_set(self, i, sem):
        self.bank_free[i] = [(sem, self.counts[sem])]

    def phase_mark(self, name):
        self.phase_marks[name] = {e: len(self.ops[e]) for e in ENGINES}

    def truncate(self, name):
        m = self.phase_marks[name]
        for e in ENGINES:
            self.ops[e] = self.ops[e][:m[e]]

    def emit_engine(self, ename, eng, sem_handles):
        hi = {}
        for fn, waits, inc in self.ops[ename]:
            for s, v in waits:
                if hi.get(s, 0) < v:
                    eng.wait_ge(sem_handles[s], v)
                    hi[s] = v
            ins = fn(eng)
            if inc is not None:
                ins.then_inc(sem_handles[inc[0]], inc[1])


def build_nc(cut=None, dbg=False):
    nc = bass.Bass("TRN2", target_bir_lowering=False, debug=False,
                   detect_race_conditions=False)

    def din(name, shape, dt=F32):
        return nc.dram_tensor(name, shape, dt, kind="ExternalInput").ap()

    x_full = din("x_full", [T, DM])
    x_ffn = din("x_ffn", [TH, DM])
    x_ffn_t = din("x_ffn_t", [DM, TH])
    w_in_t = din("w_in_t", [DM, 5120])
    w_dtin_t = din("w_dtin_t", [DM, 128])
    w_dtp_t = din("w_dtp_t", [128, 2048])
    w_out_t = din("w_out_t", [2048, DM])
    w_gu = din("w_gu", [2 * II, DM])
    w_dn = din("w_dn", [DM, II])
    dt_b = din("dt_b", [128, KT])
    a_log = din("a_log", [128, KT * NN])
    conv_w = din("conv_w", [128, KT * 4])
    conv_b = din("conv_b", [128, KT])
    d_vec = din("d_vec", [128, KT])
    eb_in = din("eb_in", [128, NN * 128], BF16)
    ec_in = din("ec_in", [128, 8 * 128], BF16)
    ex_in = din("ex_in", [128, 2 * 128], BF16)
    ones_in = din("ones_in", [8, 8 * 128], BF16)
    ident_bf = din("ident_bf", [128, 128], BF16)
    fconst = din("fconst", [128, 8])
    ident_f32 = din("ident_f32", [128, 128], F32)

    zs_dram = nc.dram_tensor("zs_dram", [2048, T], F32).ap()
    xact_dram = nc.dram_tensor("xact_dram", [2048, T], F32).ap()
    b_dram = nc.dram_tensor("b_dram", [1024, T], BF16).ap()
    c_dram = nc.dram_tensor("c_dram", [4096, T], BF16).ap()
    h_dram = nc.dram_tensor("h_dram", [II, TH], F32).ap()
    yg_dram = nc.dram_tensor("yg_dram", [2048, T], BF16).ap()
    cc_in = nc.dram_tensor("cc_in", [2 * DM, TH], F32).ap()
    cc_out = nc.dram_tensor("cc_out", [DM, TH], F32).ap()
    out_d = nc.dram_tensor("out", [DM, TH], F32, kind="ExternalOutput").ap()

    p = Plan()
    W, C = p.op, p.c
    es = ExitStack()
    sbuf = lambda name, shape, dt: es.enter_context(
        nc.sbuf_tensor(name, shape, dt))

    AR1 = sbuf("AR1", [128, 17408], F32)
    AR2 = sbuf("AR2", [128, 16384], F32)
    AR3 = sbuf("AR3", [128, 8192], F32)
    AR1b = AR1[:].bitcast(BF16)
    AR2b = AR2[:].bitcast(BF16)
    AR3b = AR3[:].bitcast(BF16)

    dtm_sb = sbuf("dtm_sb", [128, 2048], BF16)
    wdtp_sb = sbuf("wdtp_sb", [128, 2048], BF16)
    b_sb = sbuf("b_sb", [128, 2 * 2048], BF16)
    a_sb = sbuf("a_sb", [128, KT * NN], F32)
    eb_sb = sbuf("eb_sb", [128, NN * 128], BF16)
    ec_sb = sbuf("ec_sb", [128, 8 * 128], BF16)
    ex_sb = sbuf("ex_sb", [128, 2 * 128], BF16)
    ones_sb = sbuf("ones_sb", [8, 8 * 128], BF16)
    idb_sb = sbuf("idb_sb", [128, 128], BF16)
    idf_sb = sbuf("idf_sb", [128, 128], F32)
    cw_sb = sbuf("cw_sb", [128, KT * 4], F32)
    cb_sb = sbuf("cb_sb", [128, KT], F32)
    db_sb = sbuf("db_sb", [128, KT], F32)
    dv_sb = sbuf("dv_sb", [128, KT], F32)
    cols = sbuf("cols", [128, 64], F32)
    mxmn = sbuf("mxmn", [128, 2 * TH], F32)
    sgh = sbuf("sgh", [128, 2048], F32)
    fc_sb = sbuf("fc_sb", [128, 8], F32)

    ps = [es.enter_context(nc.psum_tensor(f"ps{i}", [128, TC], F32))
          for i in range(8)]

    xn_fm = AR1b
    win_bf = [AR2b[:, 0:2048], AR2b[:, 2048:4096]]
    feat_f = AR2[:, 2048:4096]
    fhi = AR2b[:, 8192:10240]
    flo = AR2b[:, 10240:12288]
    zst2 = [AR2[:, 6144:6656], AR2[:, 6656:7168]]
    xln = AR2[:, 7168:9216]
    sqscr = AR2[:, 9216:11264]
    xnt_f = AR2[:, 11264:13312]
    xep = AR2[:, 2048:4099]
    cacc = AR2[:, 4608:6656]
    xastage = AR2[:, 0:2048]
    x_ilv = AR3b
    xact_k = [AR1[:, 0:2048], AR1[:, 2048:4096]]
    dt_kv = AR1[:, 4096:6144]
    dtx_kv = AR1[:, 6144:8192]
    dA_k = [AR1[:, 8192:10240], AR1[:, 10240:12288]]
    t1_k = dA_k[0]
    zs_one = AR1[:, 12288:14336]
    qTg_bf = AR1b[:, 28672:30720]
    qTu_bf = AR1b[:, 30720:32768]
    ygst = AR1b[:, 32768:34816]
    h_sb = AR3[:, 0:2048]
    u_sb = [AR3[:, 2048:2560], AR3[:, 2560:3072]]
    yb3 = [AR3b[:, 6144:6656], AR3b[:, 6656:7168], AR3b[:, 15360:15872]]
    c_sb = [[AR3b[:, 7168 + (2 * par + ti) * 2048:
                  7168 + (2 * par + ti + 1) * 2048]
             for ti in range(2)] for par in range(2)]
    wout_bf = [AR1b[:, 0:2048], AR1b[:, 2048:4096]]
    att_st = [AR1[:, 4096:4608], AR1[:, 4608:5120]]
    xq_fm = AR2b[:, 0:16384]
    hq_a = AR2b
    hq_b = AR3b[:, 0:12288]

    def hq_ap(kI, c0, c1):
        if kI < 32:
            return hq_a[:, kI * 1024 + c0: kI * 1024 + c1]
        return hq_b[:, (kI - 32) * 1024 + c0: (kI - 32) * 1024 + c1]

    wgu_g = [AR2[:, 8192:10240], AR2[:, 10240:12288]]
    wgu_u = [AR2[:, 12288:14336], AR2[:, 14336:16384]]
    sg2 = [sgh[:, 0:512], sgh[:, 512:1024]]
    hst2 = [sgh[:, 1024:1536], sgh[:, 1536:2048]]
    mx_sb = mxmn[:, 0:TH]
    mn_sb = mxmn[:, TH:2 * TH]
    wdn_f = AR1[:, 0:5632]
    q2n_bf = AR1b[:, 11264:16896]
    q2T_bf = AR1b[:, 16896:22528]
    shb_sb = AR3[:, 6144:7168]
    rhb_sb = AR3[:, 7168:8192]
    dnst2 = [mxmn[:, 0:512], mxmn[:, 512:1024]]
    att_rl = [mxmn[:, 1024:1536], mxmn[:, 1536:2048]]
    xres2 = [AR1[:, 5120:5632], AR1[:, 5632:6144]]
    hax = AR3[:, 6144:7168]
    shrow = AR1[0:8, 6144:6272]
    rhrow = AR1[0:8, 6272:6400]
    shrh = AR1b[0:8, 12800:12928]
    shrl = AR1b[0:8, 12928:13056]
    rhrh = AR1b[0:8, 13056:13184]
    rhrl = AR1b[0:8, 13184:13312]

    mean_c = cols[:, 0:1]
    sum_c = cols[:, 1:2]
    m2_c = cols[:, 2:3]
    var_c = cols[:, 3:4]
    rstd_c = cols[:, 4:5]
    aw_c = cols[:, 5:6]
    sw_g2 = [cols[:, 6:7], cols[:, 32:33]]
    sw_u2 = [cols[:, 7:8], cols[:, 33:34]]
    rw_g2 = [cols[:, 8:9], cols[:, 34:35]]
    rw_u2 = [cols[:, 9:10], cols[:, 35:36]]
    sx_c = cols[:, 10:11]
    rx_c = cols[:, 11:12]
    sw2_c = cols[:, 12:13]
    rw2_c = cols[:, 13:14]
    shcol = cols[:, 16:24]
    rhcol = cols[:, 24:32]

    for dst, src in ((a_sb[:], a_log), (cw_sb[:], conv_w), (cb_sb[:], conv_b),
                     (db_sb[:], dt_b), (dv_sb[:], d_vec), (ec_sb[:], ec_in),
                     (ex_sb[:], ex_in), (idb_sb[:], ident_bf),
                     (idf_sb[:], ident_f32), (eb_sb[:], eb_in),
                     (ones_sb[:], ones_in), (fc_sb[:], fconst)):
        W("sync", lambda e, d=dst, s=src: e.dma_start(out=d, in_=s),
          inc=("ld", 16))
    consts_ld = C("ld") + 16
    W("gpsimd", lambda e: e.dma_start(out=wdtp_sb[:], in_=w_dtp_t),
      inc=("gp", 16))
    W("scalar", lambda e: e.activation(out=a_sb[:], in_=a_sb[:], func=AF.Exp),
      waits=[("ld", consts_ld)], inc=("ss", 1))
    W("vector", lambda e: e.tensor_scalar(out=a_sb[:], in0=a_sb[:],
                                          scalar1=-1.0, scalar2=None,
                                          op0=ALU.mult),
      waits=[("ss", C("ss"))], inc=("sv", 1))

    def emit_ln_quant(src_rows, out_f, quantize, extra_ld_waits,
                      xln=None, sqscr=None):
        W("sync", lambda e, xln=xln: e.dma_start(out=xln, in_=src_rows),
          waits=extra_ld_waits, inc=("ld", 16))
        myld = C("ld")
        W("vector", lambda e: e.tensor_reduce(out=sum_c, in_=xln, axis=AX,
                                              op=ALU.add),
          waits=[("ld", myld)], inc=("sv", 1))
        sv_red = C("sv")
        W("scalar", lambda e: e.activation(out=sqscr, in_=xln,
                                           func=AF.Square, accum_out=m2_c),
          waits=[("ld", myld)], inc=("ss", 1))
        ss_sq = C("ss")
        W("vector", lambda e: e.tensor_scalar(out=mean_c, in0=sum_c,
                                              scalar1=1.0 / DM, scalar2=None,
                                              op0=ALU.mult),
          waits=[("sv", sv_red)], inc=("sv", 1))
        sv_mean = C("sv")
        W("vector", lambda e: e.tensor_scalar(out=m2_c, in0=m2_c,
                                              scalar1=1.0 / DM, scalar2=None,
                                              op0=ALU.mult),
          waits=[("ss", ss_sq)], inc=("sv", 1))
        sv_m2 = C("sv")
        W("vector", lambda e: e.scalar_tensor_tensor(
            out=var_c, in0=mean_c, scalar=mean_c, in1=m2_c,
            op0=ALU.mult, op1=ALU.subtract),
          waits=[("sv", sv_m2)], inc=("sv", 1))
        sv_var = C("sv")
        W("scalar", lambda e: e.activation(out=rstd_c, in_=var_c, func=AF.Ln,
                                           scale=-1.0, bias=fc_sb[:, 0:1]),
          waits=[("sv", sv_var)], inc=("ss", 1))
        ss_ln = C("ss")
        W("scalar", lambda e: e.activation(out=rstd_c, in_=rstd_c,
                                           func=AF.Exp, scale=fc_sb[:, 2:3]),
          waits=[("ss", ss_ln)], inc=("ss", 1))
        ss_rstd = C("ss")
        if not quantize:
            W("vector", lambda e: e.tensor_scalar(
                out=out_f, in0=xln, scalar1=mean_c, scalar2=rstd_c,
                op0=ALU.subtract, op1=ALU.mult),
              waits=[("ss", ss_rstd)], inc=("sv", 1))
            return C("sv"), ss_rstd
        W("vector", lambda e: e.tensor_scalar(
            out=sqscr, in0=xln, scalar1=mean_c, scalar2=rstd_c,
            op0=ALU.subtract, op1=ALU.mult),
          waits=[("ss", ss_rstd)], inc=("sv", 1))
        W("vector", lambda e: e.tensor_reduce(out=aw_c, in_=sqscr, axis=AX,
                                              op=ALU.max,
                                              apply_absolute_value=True),
          inc=("sv", 1))
        sv_aw = C("sv")
        W("vector", lambda e: e.tensor_scalar(out=sx_c, in0=aw_c,
                                              scalar1=QMAX_R, scalar2=QEPS,
                                              op0=ALU.mult, op1=ALU.max),
          waits=[("sv", sv_aw)], inc=("sv", 1))
        sv_sx = C("sv")
        W("vector", lambda e: e.reciprocal(out=rx_c, in_=sx_c),
          waits=[("sv", sv_sx)], inc=("sv", 1))
        sv_q = C("sv")
        W("scalar", lambda e: e.activation(out=sqscr, in_=sqscr,
                                           func=AF.Identity, scale=rx_c,
                                           bias=fc_sb[:, 1:2]),
          waits=[("sv", sv_q)], inc=("ss", 1))
        ss_rnd = C("ss")
        W("vector", lambda e: e.tensor_scalar(out=out_f, in0=sqscr,
                                              scalar1=MAGIC, scalar2=sx_c,
                                              op0=ALU.subtract, op1=ALU.mult),
          waits=[("ss", ss_rnd)], inc=("sv", 1))
        return C("sv"), ss_rnd

    def emit_tile_transpose(src_bf, dstv, tt, nmt, sv_src):
        for msq in range((nmt + 3) // 4):
            bi = 4 + (msq % 2)
            bank = ps[bi]
            cnt = min(4, nmt - msq * 4)
            for s4 in range(cnt):
                ms = msq * 4 + s4
                wl = p.bank_wait(bi) if s4 == 0 else []
                if s4 == 0:
                    wl = wl + [("sv", sv_src)]
                W("tensor", lambda e, ms=ms, s4=s4, bank=bank, src_bf=src_bf:
                  e.transpose(bank[:, s4 * 128:(s4 + 1) * 128],
                              src_bf[:, ms * 128:(ms + 1) * 128], idf_sb[:]),
                  waits=wl, inc=("st", 1))
            st_g = C("st")
            dst = dstv[:, msq * 4: msq * 4 + cnt, tt * 128:(tt + 1) * 128]
            W("vector", lambda e, dst=dst, bank=bank, cnt=cnt: e.tensor_copy(
                out=dst,
                in_=bank[:, 0:cnt * 128].rearrange("p (s t) -> p s t", s=cnt)),
              waits=[("st", st_g)], inc=("sv", 1))
            p.bank_set(bi, "sv")
        return C("st"), C("sv")

    xnv = AR1b[:, 0:32768].rearrange("p (mt t) -> p mt t", mt=16)
    prev_users = [("sv", 0), ("ss", 0)]
    for tt in range(16):
        sv_xn, ss_u = emit_ln_quant(x_full[tt * 128:(tt + 1) * 128, :],
                                    xnt_f, False, prev_users,
                                    xln=xln[:], sqscr=sqscr[:])
        st_t, sv_t = emit_tile_transpose(xnt_f, xnv, tt, 16, sv_xn)
        prev_users = [("sv", sv_xn), ("ss", ss_u), ("st", st_t)]
    xn_sv = C("sv")
    p.phase_mark("P0")

    wdtin_v = w_dtin_t.rearrange("(mt p) o -> p mt o", p=128)
    wb0 = win_bf[0][:].rearrange("p (mt o) -> p mt o", mt=16)
    W("gpsimd", lambda e: e.dma_start(out=wb0, in_=wdtin_v), inc=("gp", 16))
    dtin_gp = C("gp")
    for tcp in range(4):
        bi = tcp % 2
        bank = ps[bi]
        wl0 = p.bank_wait(bi) + [("gp", dtin_gp), ("sv", xn_sv)]
        for mt in range(16):
            W("tensor", lambda e, tcp=tcp, mt=mt, bank=bank: e.matmul(
                bank[:], win_bf[0][:, mt * 128:(mt + 1) * 128],
                xn_fm[:, mt * 2048 + tcp * 512: mt * 2048 + (tcp + 1) * 512],
                start=(mt == 0), stop=(mt == 15)),
              waits=(wl0 if mt == 0 else []),
              inc=("st", 1) if mt == 15 else None)
        st_m = C("st")
        W("vector", lambda e, tcp=tcp, bank=bank: e.tensor_copy(
            out=dtm_sb[:, tcp * 512:(tcp + 1) * 512], in_=bank[:]),
          waits=[("st", st_m)], inc=("sv", 1))
        p.bank_set(bi, "sv")
    dtm_sv = C("sv")
    p.phase_mark("P1")

    win_v = w_in_t.rearrange("(mt p) o -> p mt o", p=128)
    gp_scatter_mark = 0
    zst_used = [0, 0]
    win_used = [[("sv", 0)], [("gp", dtin_gp), ("sv", dtm_sv), ("st", 0)]]
    win_used[1] = [("st", C("st"))]
    win_free = [[("st", C("st"))], []]
    for ot in range(40):
        par = ot % 2
        wb = win_bf[par][:].rearrange("p (mt o) -> p mt o", mt=16)
        W("gpsimd", lambda e, ot=ot, wb=wb: e.dma_start(
            out=wb, in_=win_v[:, :, ot * 128:(ot + 1) * 128]),
          waits=win_free[par], inc=("gp", 16))
        my_gp = C("gp")
        for tc in range(4):
            bi = (ot * 4 + tc) % 4
            bank = ps[bi]
            wl0 = p.bank_wait(bi) + [("gp", my_gp)]
            if ot == 0 and tc == 0:
                wl0.append(("sv", xn_sv))
            for mt in range(16):
                W("tensor", lambda e, ot=ot, tc=tc, mt=mt, bank=bank, par=par:
                  e.matmul(bank[:], win_bf[par][:, mt * 128:(mt + 1) * 128],
                           xn_fm[:, mt * 2048 + tc * 512:
                                 mt * 2048 + (tc + 1) * 512],
                           start=(mt == 0), stop=(mt == 15)),
                  waits=(wl0 if mt == 0 else []),
                  inc=("st", 1) if mt == 15 else None)
            st_m = C("st")
            if ot < 16:
                zpar = (ot * 4 + tc) % 2
                W("scalar", lambda e, bank=bank, zpar=zpar: e.activation(
                    out=zst2[zpar][:], in_=bank[:], func=AF.Silu),
                  waits=[("st", st_m), ("gp", zst_used[zpar])], inc=("ss", 1))
                ss_z = C("ss")
                p.bank_set(bi, "ss")
                W("gpsimd", lambda e, ot=ot, tc=tc, zpar=zpar: e.dma_start(
                    out=zs_dram[ot * 128:(ot + 1) * 128,
                                tc * 512:(tc + 1) * 512], in_=zst2[zpar][:]),
                  waits=[("ss", ss_z)], inc=("gp", 16))
                zst_used[zpar] = C("gp")
            else:
                W("vector", lambda e, bank=bank, tc=tc: e.tensor_copy(
                    out=feat_f[:, tc * 512:(tc + 1) * 512], in_=bank[:]),
                  waits=[("st", st_m), ("gp", gp_scatter_mark)],
                  inc=("sv", 1))
                p.bank_set(bi, "sv")
        win_free[par] = [("st", C("st"))]
        if ot >= 16:
            W("vector", lambda e: e.tensor_copy(out=fhi[:], in_=feat_f[:]),
              inc=("sv", 1))
            W("vector", lambda e: e.tensor_tensor(out=flo[:], in0=feat_f[:],
                                                  in1=fhi[:],
                                                  op=ALU.subtract),
              inc=("sv", 1))
            sv_hl = C("sv")
            if ot < 20:
                oti = ot - 16
                for kk in range(4):
                    k = oti * 4 + kk
                    for s, srcb in ((0, fhi), (1, flo)):
                        dst = x_ilv[(k % 2) * 64 + s * 32:
                                    (k % 2) * 64 + s * 32 + 32,
                                    (k // 2) * 2048:(k // 2 + 1) * 2048]
                        W("gpsimd", lambda e, dst=dst, srcb=srcb, kk=kk:
                          e.dma_start(out=dst,
                                      in_=srcb[kk * 32:(kk + 1) * 32, :]),
                          waits=[("sv", sv_hl)], inc=("gp", 16))
            elif ot < 24:
                oti = ot - 20
                for kk in range(4):
                    k = oti * 4 + kk
                    for s, srcb in ((0, fhi), (1, flo)):
                        dst = b_dram[k * 64 + s * 32: k * 64 + s * 32 + 32, :]
                        W("gpsimd", lambda e, dst=dst, srcb=srcb, kk=kk:
                          e.dma_start(out=dst,
                                      in_=srcb[kk * 32:(kk + 1) * 32, :]),
                          waits=[("sv", sv_hl)], inc=("gp", 16))
            else:
                k = ot - 24
                cv = c_dram.rearrange("(k n s h) t -> k n s h t",
                                      k=16, n=16, s=2, h=8)
                for s, srcb in ((0, fhi), (1, flo)):
                    dst = cv[k, :, s, :, :]
                    W("gpsimd", lambda e, dst=dst, srcb=srcb: e.dma_start(
                        out=dst, in_=srcb[:]),
                      waits=[("sv", sv_hl)], inc=("gp", 16))
            gp_scatter_mark = C("gp")
    p2_gp = C("gp")
    p.phase_mark("P2")

    xa_gp = 0
    for k in range(KT):
        bi = 4 + (k % 2)
        bank = ps[bi]
        W("vector", lambda e: e.memset(xep[:, 0:3], 0.0), inc=("sv", 1))
        for tcp in range(4):
            wl = p.bank_wait(bi)
            if k == 0 and tcp == 0:
                wl = wl + [("gp", p2_gp), ("ld", consts_ld)]
            W("tensor", lambda e, k=k, tcp=tcp, bank=bank: e.matmul(
                bank[:], ex_sb[(k % 2) * 64:(k % 2) * 64 + 64,
                               (k % 2) * 128:(k % 2) * 128 + 128],
                x_ilv[(k % 2) * 64:(k % 2) * 64 + 64,
                                         (k // 2) * 2048 + tcp * 512:
                                         (k // 2) * 2048 + (tcp + 1) * 512],
                start=True, stop=True),
              waits=wl, inc=("st", 1))
            st_m = C("st")
            W("vector", lambda e, tcp=tcp, bank=bank: e.tensor_copy(
                out=xep[:, 3 + tcp * 512: 3 + (tcp + 1) * 512], in_=bank[:]),
              waits=[("st", st_m)], inc=("sv", 1))
            p.bank_set(bi, "sv")
        W("vector", lambda e, k=k: e.tensor_scalar(
            out=cacc[:], in0=xep[:, 3:3 + T],
            scalar1=cw_sb[:, k * 4 + 3: k * 4 + 4],
            scalar2=None, op0=ALU.mult), inc=("sv", 1))
        for tap in range(3):
            W("vector", lambda e, k=k, tap=tap: e.scalar_tensor_tensor(
                out=cacc[:], in0=xep[:, tap:tap + T],
                scalar=cw_sb[:, k * 4 + tap: k * 4 + tap + 1], in1=cacc[:],
                op0=ALU.mult, op1=ALU.add), inc=("sv", 1))
        sv_cv = C("sv")
        W("scalar", lambda e, k=k: e.activation(
            out=xastage[:], in_=cacc[:], func=AF.Silu,
            bias=cb_sb[:, k:k + 1]),
          waits=[("sv", sv_cv), ("gp", xa_gp)], inc=("ss", 1))
        ss_xa = C("ss")
        W("gpsimd", lambda e, k=k: e.dma_start(
            out=xact_dram[k * 128:(k + 1) * 128, :], in_=xastage[:]),
          waits=[("ss", ss_xa)], inc=("gp", 16))
        xa_gp = C("gp")
    p2b_gp = C("gp")
    p2b_sv = C("sv")
    p.phase_mark("P2b")

    xqv = AR2b[:, 0:16384].rearrange("p (mt t) -> p mt t", mt=16)
    xlnq = AR1[:, 0:2048]
    sqscrq = AR1[:, 2048:4096]
    xntq = AR1[:, 4096:6144]
    prev_users = [("sv", p2b_sv), ("st", C("st")), ("gp", p2_gp)]
    for tt in range(8):
        sv_xq, ss_u = emit_ln_quant(x_ffn[tt * 128:(tt + 1) * 128, :],
                                    xntq, True, prev_users,
                                    xln=xlnq, sqscr=sqscrq)
        st_t, sv_t = emit_tile_transpose(xntq, xqv, tt, 16, sv_xq)
        prev_users = [("sv", sv_xq), ("ss", ss_u), ("st", st_t)]
    xq_sv = C("sv")
    p5a_st = C("st")
    p5a_ss = C("ss")
    p.phase_mark("P5A")

    def load_w_tile(wdst, src, rows0, wl_extra):
        W("sync", lambda e, wdst=wdst: e.dma_start(
            out=wdst, in_=src[rows0:rows0 + 128, :]),
          waits=wl_extra, inc=("ld", 16))
        return C("ld")

    def quant_w_tile(wdst, myld, s_col, r_col):
        W("vector", lambda e, wdst=wdst: e.tensor_reduce(
            out=aw_c, in_=wdst, axis=AX, op=ALU.max,
            apply_absolute_value=True),
          waits=[("ld", myld)], inc=("sv", 1))
        sv_aw = C("sv")
        W("vector", lambda e, s_col=s_col: e.tensor_scalar(
            out=s_col, in0=aw_c, scalar1=QMAX_R, scalar2=QEPS,
            op0=ALU.mult, op1=ALU.max),
          waits=[("sv", sv_aw)], inc=("sv", 1))
        sv_sc = C("sv")
        W("vector", lambda e, s_col=s_col, r_col=r_col: e.reciprocal(
            out=r_col, in_=s_col),
          waits=[("sv", sv_sc)], inc=("sv", 1))
        svq = C("sv")
        W("scalar", lambda e, wdst=wdst, r_col=r_col: e.activation(
            out=wdst, in_=wdst, func=AF.Identity, scale=r_col, bias=fc_sb[:, 1:2]),
          waits=[("sv", svq)], inc=("ss", 1))
        ssr = C("ss")
        W("vector", lambda e, wdst=wdst: e.tensor_scalar(
            out=wdst, in0=wdst, scalar1=MAGIC, scalar2=None,
            op0=ALU.subtract),
          waits=[("ss", ssr)], inc=("sv", 1))
        return C("sv"), ssr

    def transpose_q(qsrc, qTdst, nmt, sv_src, evac_eng):
        qTv = qTdst.rearrange("p (mt o) -> p mt o", mt=nmt)
        last = 0
        for msq in range((nmt + 3) // 4):
            bi = 4 + (msq % 2)
            bank = ps[bi]
            cnt = min(4, nmt - msq * 4)
            for s4 in range(cnt):
                ms = msq * 4 + s4
                wl = (p.bank_wait(bi) + [("sv", sv_src)]) if s4 == 0 else []
                W("tensor", lambda e, ms=ms, s4=s4, bank=bank, qsrc=qsrc:
                  e.transpose(bank[:, s4 * 128:(s4 + 1) * 128],
                              qsrc[:, ms * 128:(ms + 1) * 128], idf_sb[:]),
                  waits=wl, inc=("st", 1))
            st_g = C("st")
            dst = qTv[:, msq * 4: msq * 4 + cnt, :]
            src = lambda bank=bank, cnt=cnt: bank[:, 0:cnt * 128].rearrange(
                "p (s t) -> p s t", s=cnt)
            if evac_eng == "vector":
                W("vector", lambda e, dst=dst, bank=bank, cnt=cnt:
                  e.tensor_copy(out=dst, in_=bank[:, 0:cnt * 128].rearrange(
                      "p (s t) -> p s t", s=cnt)),
                  waits=[("st", st_g)], inc=("sv", 1))
                p.bank_set(bi, "sv")
                last = ("sv", C("sv"))
            else:
                W("scalar", lambda e, dst=dst, bank=bank, cnt=cnt: e.copy(
                    out=dst, in_=bank[:, 0:cnt * 128].rearrange(
                        "p (s t) -> p s t", s=cnt)),
                  waits=[("st", st_g)], inc=("ss", 1))
                p.bank_set(bi, "ss")
                last = ("ss", C("ss"))
        return last

    ffn_state = {"h_gp": 0, "prev_q_ss": 0}
    pair_st = {}
    sg_free = [0, 0]
    ffn_q = []
    ffn_qi = [0]
    ffn_bank = [6]

    def drip(nops):
        done = 0
        while done < nops and ffn_qi[0] < len(ffn_q):
            ffn_q[ffn_qi[0]]()
            ffn_qi[0] += 1
            done += 1

    def next_fb():
        fb = ffn_bank[0]
        ffn_bank[0] = 13 - fb
        return fb

    def queue_ffn_pair(i):
        par = i % 2
        m = {}
        qT2 = {"g": qTg_bf, "u": qTu_bf}
        wt2 = {"g": wgu_g[par], "u": wgu_u[par]}
        sw2 = {"g": sw_g2[par], "u": sw_u2[par]}
        rw2 = {"g": rw_g2[par], "u": rw_u2[par]}

        def c_load():
            wlg = [("sv", xq_sv), ("ss", ffn_state["prev_q_ss"])]
            if i >= 2:
                wlg.append(("st", pair_st[i - 2]))
            W("sync", lambda e: e.dma_start(
                out=wgu_g[par], in_=w_gu[i * 128:(i + 1) * 128, :]),
              waits=wlg, inc=("ld", 16))
            W("sync", lambda e: e.dma_start(
                out=wgu_u[par],
                in_=w_gu[(NI + i) * 128:(NI + i + 1) * 128, :]),
              inc=("ld", 16))
            m["ld"] = C("ld")
        ffn_q.append(c_load)

        def c_scale(gu):
            W("vector", lambda e: e.tensor_reduce(
                out=sw2[gu], in_=wt2[gu], axis=AX, op=ALU.max,
                apply_absolute_value=True),
              waits=[("ld", m["ld"])], inc=("sv", 1))
            hop = C("sv")
            W("vector", lambda e: e.tensor_scalar(
                out=sw2[gu], in0=sw2[gu], scalar1=QMAX_R, scalar2=QEPS,
                op0=ALU.mult, op1=ALU.max),
              waits=[("sv", hop)], inc=("sv", 1))
            hop2 = C("sv")
            W("vector", lambda e: e.reciprocal(out=rw2[gu], in_=sw2[gu]),
              waits=[("sv", hop2)], inc=("sv", 1))
            m["rec"] = C("sv")
        ffn_q.append(lambda: c_scale("g"))
        ffn_q.append(lambda: c_scale("u"))

        def c_round(gu):
            W("scalar", lambda e: e.activation(
                out=wt2[gu], in_=wt2[gu], func=AF.Identity, scale=rw2[gu],
                bias=fc_sb[:, 1:2]),
              waits=[("sv", m["rec"])], inc=("ss", 1))
            W("scalar", lambda e: e.activation(
                out=wt2[gu], in_=wt2[gu], func=AF.Identity,
                bias=fc_sb[:, 3:4]),
              inc=("ss", 1))
            m["q_" + gu] = C("ss")
        ffn_q.append(lambda: c_round("g"))
        ffn_q.append(lambda: c_round("u"))

        def c_tr(gu, grp):
            fb = next_fb()
            bank = ps[fb]
            qTv = qT2[gu].rearrange("p (mt o) -> p mt o", mt=16)
            for s4 in range(4):
                ms = grp * 4 + s4
                wl = (p.bank_wait(fb) + [("ss", m["q_" + gu])]
                      if s4 == 0 else [])
                W("tensor", lambda e, ms=ms, s4=s4, bank=bank, gu=gu:
                  e.transpose(bank[:, s4 * 128:(s4 + 1) * 128],
                              wt2[gu][:, ms * 128:(ms + 1) * 128],
                              idf_sb[:]),
                  waits=wl, inc=("st", 1))
            st_g = C("st")
            W("scalar", lambda e, grp=grp, bank=bank, qTv=qTv: e.copy(
                out=qTv[:, grp * 4: grp * 4 + 4, :],
                in_=bank[:, 0:512].rearrange("p (s t) -> p s t", s=4)),
              waits=[("st", st_g)], inc=("ss", 1))
            p.bank_set(fb, "ss")
            m["tr_" + gu] = C("ss")
        for gu in ("g", "u"):
            for grp in range(4):
                ffn_q.append(lambda gu=gu, grp=grp: c_tr(gu, grp))

        def c_chain(gu, tc, q4):
            if q4 == 0:
                fb = next_fb()
                m["fb_" + gu + str(tc)] = fb
                wl0 = (p.bank_wait(fb)
                       + [("ss", m["tr_g"]), ("ss", m["tr_u"]),
                          ("sv", xq_sv)])
            else:
                fb = m["fb_" + gu + str(tc)]
                wl0 = []
            bank = ps[fb]
            for mt in range(q4 * 4, q4 * 4 + 4):
                W("tensor", lambda e, mt=mt, tc=tc, bank=bank, gu=gu:
                  e.matmul(bank[:], qT2[gu][:, mt * 128:(mt + 1) * 128],
                           xq_fm[:, mt * 1024 + tc * 512:
                                 mt * 1024 + (tc + 1) * 512],
                           start=(mt == 0), stop=(mt == 15)),
                  waits=(wl0 if mt == q4 * 4 else []),
                  inc=("st", 1) if mt == 15 else None)
            if mt == 15:
                m["mm_" + gu + str(tc)] = C("st")

        def c_silu(tc):
            hpar = (i * 2 + tc) % 2
            fbG = m["fb_g" + str(tc)]
            W("scalar", lambda e, hpar=hpar, fbG=fbG: e.activation(
                out=sg2[hpar], in_=ps[fbG][:], func=AF.Silu,
                scale=sw2["g"]),
              waits=[("st", m["mm_g" + str(tc)]), ("sv", sg_free[hpar])],
              inc=("ss", 1))
            m["sg" + str(tc)] = C("ss")
            p.bank_set(fbG, "ss")

        def c_h(tc):
            hpar = (i * 2 + tc) % 2
            fbU = m["fb_u" + str(tc)]
            wl_h = [("ss", m["sg" + str(tc)]),
                    ("st", m["mm_u" + str(tc)]),
                    ("gp", ffn_state["h_gp"])]
            if i == 0:
                wl_h.append(("sv", mxmn_init_sv))
            W("vector", lambda e, hpar=hpar, fbU=fbU:
              e.scalar_tensor_tensor(
                  out=hst2[hpar], in0=ps[fbU][:], scalar=sw2["u"],
                  in1=sg2[hpar], op0=ALU.mult, op1=ALU.mult),
              waits=wl_h, inc=("sv", 1))
            sv_h = C("sv")
            sg_free[hpar] = sv_h
            p.bank_set(fbU, "sv")
            W("vector", lambda e, tc=tc, hpar=hpar: e.tensor_tensor(
                out=mx_sb[:, tc * 512:(tc + 1) * 512],
                in0=mx_sb[:, tc * 512:(tc + 1) * 512],
                in1=hst2[hpar], op=ALU.max), inc=("sv", 1))
            W("vector", lambda e, tc=tc, hpar=hpar: e.tensor_tensor(
                out=mn_sb[:, tc * 512:(tc + 1) * 512],
                in0=mn_sb[:, tc * 512:(tc + 1) * 512],
                in1=hst2[hpar], op=ALU.min), inc=("sv", 1))
            m["h" + str(tc)] = sv_h

        def c_spill(tc):
            hpar = (i * 2 + tc) % 2
            W("gpsimd", lambda e, i=i, tc=tc, hpar=hpar: e.dma_start(
                out=h_dram[i * 128:(i + 1) * 128,
                           tc * 512:(tc + 1) * 512],
                in_=hst2[hpar]),
              waits=[("sv", m["h" + str(tc)])], inc=("gp", 16))
            ffn_state["h_gp"] = C("gp")

        for tc in range(2):
            for q4 in range(4):
                ffn_q.append(lambda tc=tc, q4=q4: c_chain("g", tc, q4))
            for q4 in range(4):
                ffn_q.append(lambda tc=tc, q4=q4: c_chain("u", tc, q4))
            ffn_q.append(lambda tc=tc: c_silu(tc))
            ffn_q.append(lambda tc=tc: c_h(tc))
            ffn_q.append(lambda tc=tc: c_spill(tc))

        def c_fin():
            pair_st[i] = C("st")
            ffn_state["prev_q_ss"] = m["q_u"]
        ffn_q.append(c_fin)
    ld_marks = {}

    def reload_k(k, extra_waits):
        par = k % 2
        W("sync", lambda e, k=k, par=par: e.dma_start(
            out=xact_k[par], in_=xact_dram[k * 128:(k + 1) * 128, :]),
          waits=extra_waits, inc=("ld", 16))
        for ti in range(2):
            W("sync", lambda e, k=k, par=par, ti=ti: e.dma_start(
                out=c_sb[par][ti],
                in_=c_dram[k * 256 + ti * 128: k * 256 + (ti + 1) * 128, :]),
              inc=("ld", 16))
        if k % 2 == 0:
            kp = k // 2
            W("sync", lambda e, kp=kp: e.dma_start(
                out=b_sb[:, (kp % 2) * 2048:(kp % 2 + 1) * 2048],
                in_=b_dram[kp * 128:(kp + 1) * 128, :]),
              inc=("ld", 16))
        ld_marks[k] = C("ld")

    W("vector", lambda e: e.memset(mx_sb, -3.0e38), inc=("sv", 1))
    W("vector", lambda e: e.memset(mn_sb, 3.0e38), inc=("sv", 1))
    mxmn_init_sv = C("sv")
    reload_k(0, [("gp", p2b_gp), ("sv", p2b_sv), ("ss", p5a_ss),
                 ("st", p5a_st)])
    for i in range(NI):
        queue_ffn_pair(i)
    p.phase_mark("P3r")

    kuse_st = {}
    gate_sv_hist = {}
    yg_spill_prev = 0
    zs_ld = {}
    prev_gate_sv = p2b_sv
    dA_sv_done = {}
    u_marks = {}
    yb_marks = {}
    for k in range(KT):
        par = k % 2
        myld = ld_marks[k]
        dtu = {}
        for tcp in range(4):
            bi = 4 + (tcp % 2)
            bank = ps[bi]
            wl = p.bank_wait(bi)
            if tcp == 0:
                wl = wl + [("sv", max(prev_gate_sv, dtm_sv)),
                           ("st", p5a_st)]
            W("tensor", lambda e, k=k, tcp=tcp, bank=bank: e.matmul(
                bank[:], wdtp_sb[:, k * 128:(k + 1) * 128],
                dtm_sb[:, tcp * 512:(tcp + 1) * 512],
                start=True, stop=True),
              waits=wl, inc=("st", 1))
            st_dt = C("st")
            sl = slice(tcp * 512, (tcp + 1) * 512)
            W("scalar", lambda e, k=k, bank=bank, sl=sl: e.activation(
                out=t1_k[:, sl], in_=bank[:], func=AF.Abs,
                bias=db_sb[:, k:k + 1]),
              waits=[("st", st_dt)], inc=("ss", 1))
            ss_a = C("ss")
            W("vector", lambda e, k=k, bank=bank, sl=sl: e.tensor_scalar(
                out=dt_kv[:, sl], in0=bank[:], scalar1=db_sb[:, k:k + 1],
                scalar2=0.0, op0=ALU.add, op1=ALU.max),
              waits=[("st", st_dt), ("ss", ss_a)], inc=("sv", 1))
            sv_d = C("sv")
            p.bank_free[bi] = [("ss", ss_a), ("sv", sv_d)]
        if k == 0:
            p.phase_mark("P3m")
        W("scalar", lambda e: e.activation(out=t1_k[:], in_=t1_k[:],
                                           func=AF.Exp, scale=-1.0),
          inc=("ss", 1))
        W("scalar", lambda e: e.activation(out=t1_k[:], in_=t1_k[:],
                                           func=AF.Ln, bias=1.0),
          inc=("ss", 1))
        ss_sp = C("ss")
        W("vector", lambda e: e.tensor_tensor(out=dt_kv[:], in0=dt_kv[:],
                                              in1=t1_k[:], op=ALU.add),
          waits=[("ss", ss_sp)], inc=("sv", 1))
        if k == 0:
            p.phase_mark("P3sp")
        W("vector", lambda e, par=par: e.tensor_tensor(
            out=dtx_kv[:], in0=dt_kv[:], in1=xact_k[par], op=ALU.mult),
          waits=[("ld", myld)], inc=("sv", 1))
        sv_dt = C("sv")
        if k == 0:
            p.phase_mark("P3dt")
        ss_dA = {}

        def emit_dA(n, k=k, sv_dt=sv_dt):
            npar = n % 2
            wl_dA = [("sv", sv_dt)]
            if (n - 2) in dA_sv_done:
                wl_dA.append(("sv", dA_sv_done[n - 2]))
            W("scalar", lambda e, k=k, n=n, npar=npar: e.activation(
                out=dA_k[npar][:], in_=dt_kv[:], func=AF.Exp,
                scale=a_sb[:, k * NN + n: k * NN + n + 1]),
              waits=wl_dA, inc=("ss", 1))
            ss_dA[n] = C("ss")

        emit_dA(0)
        emit_dA(1)
        sv_yb = {}
        ysum_st = {}

        def emit_ysum(jt):
            nj, tcj = jt // NTC, jt % NTC
            wl = [("sv", sv_yb[jt])]
            if nj == 0:
                wl = wl + p.bank_wait(tcj)
            W("tensor", lambda e, nj=nj, tcj=tcj, jt=jt: e.matmul(
                ps[tcj][:], idb_sb[:], yb3[jt % 3],
                start=(nj == 0), stop=(nj == NN - 1)),
              waits=wl, inc=("st", 1))
            ysum_st[jt] = C("st")

        for n in range(NN):
            npar = n % 2
            for tc in range(NTC):
                it = n * NTC + tc
                tsl = slice(tc * 512, (tc + 1) * 512)
                if tc == 0 and 1 <= n < NN - 1:
                    emit_dA(n + 1)
                wlB = p.bank_wait(4) + ([("ld", myld)] if it == 0 else [])
                W("tensor", lambda e, k=k, n=n, tsl=tsl: e.matmul(
                    ps[4][:], eb_sb[(k % 2) * 64:(k % 2) * 64 + 64,
                                    n * 128:(n + 1) * 128],
                    b_sb[(k % 2) * 64:(k % 2) * 64 + 64,
                         ((k // 2) % 2) * 2048 + tsl.start:
                         ((k // 2) % 2) * 2048 + tsl.stop],
                    start=True, stop=True),
                  waits=wlB, inc=("st", 1))
                st_B = C("st")
                W("vector", lambda e, tsl=tsl, it=it:
                  e.tensor_tensor(out=u_sb[it % 2], in0=dtx_kv[:, tsl],
                                  in1=ps[4][:], op=ALU.mult),
                  waits=[("st", st_B), ("ss", ss_dA[n])], inc=("sv", 1))
                p.bank_set(4, "sv")
                init = 0.0 if tc == 0 else h_sb[:, tc * 512 - 1: tc * 512]
                W("vector", lambda e, tsl=tsl, it=it, npar=npar,
                  init=init: e.tensor_tensor_scan(
                      out=h_sb[:, tsl], data0=dA_k[npar][:, tsl],
                      data1=u_sb[it % 2], initial=init, op0=ALU.mult,
                      op1=ALU.add), inc=("sv", 1))
                if tc == NTC - 1:
                    dA_sv_done[n] = C("sv")
                wlC = p.bank_wait(5) + ([("ld", myld)] if it == 0 else [])
                W("tensor", lambda e, k=k, n=n, tsl=tsl: e.matmul(
                    ps[5][:], ec_sb[:, (n % 8) * 128:((n % 8) + 1) * 128],
                    c_sb[k % 2][n // 8][:, tsl], start=True, stop=True),
                  waits=wlC, inc=("st", 1))
                st_C = C("st")
                wlY = [("st", st_C)]
                if (it - 3) in ysum_st:
                    wlY.append(("st", ysum_st[it - 3]))
                W("vector", lambda e, tsl=tsl, it=it:
                  e.tensor_tensor(out=yb3[it % 3], in0=h_sb[:, tsl],
                                  in1=ps[5][:], op=ALU.mult),
                  waits=wlY, inc=("sv", 1))
                p.bank_set(5, "sv")
                sv_yb[it] = C("sv")
                if it >= 1:
                    emit_ysum(it - 1)
                drip(2)
            if n == 8:
                W("sync", lambda e, k=k: e.dma_start(
                    out=zs_one, in_=zs_dram[k * 128:(k + 1) * 128, :]),
                  waits=[("sv", prev_gate_sv)], inc=("ld", 16))
                zs_ld[k] = C("ld")
            if n == 0 and k + 1 < KT:
                gates = [("sv", C("sv")), ("st", C("st"))]
                if k >= 1:
                    gates += [("sv", gate_sv_hist.get(k - 1, 0)),
                              ("st", kuse_st.get(k - 1, 0))]
                reload_k(k + 1, gates)
            if k == 0 and n == 0:
                p.phase_mark("P3n0")
        emit_ysum(NN * NTC - 1)
        st_ysum = C("st")
        for tc in range(NTC):
            tsl = slice(tc * 512, (tc + 1) * 512)
            W("vector", lambda e, k=k, tsl=tsl, tc=tc, par=par:
              e.scalar_tensor_tensor(
                  out=u_sb[0], in0=xact_k[par][:, tsl],
                  scalar=dv_sb[:, k:k + 1], in1=ps[tc][:],
                  op0=ALU.mult, op1=ALU.add),
              waits=[("st", st_ysum)], inc=("sv", 1))
            p.bank_set(tc, "sv")
            W("vector", lambda e, k=k, tsl=tsl: e.tensor_tensor(
                out=ygst[:, tsl], in0=u_sb[0], in1=zs_one[:, tsl],
                op=ALU.mult),
              waits=(([("gp", yg_spill_prev), ("ld", zs_ld[k])])
                     if tsl.start == 0 else []),
              inc=("sv", 1))
        prev_gate_sv = C("sv")
        gate_sv_hist[k] = prev_gate_sv
        W("gpsimd", lambda e, k=k: e.dma_start(
            out=yg_dram[k * 128:(k + 1) * 128, :], in_=ygst[:]),
          waits=[("sv", prev_gate_sv)], inc=("gp", 16))
        yg_spill_prev = C("gp")
        kuse_st[k] = C("st")
        drip(4)
        if k == 0:
            p.phase_mark("P3k0")
    drip(10 ** 9)
    p3_sv = C("sv")
    p3_st = C("st")
    p.phase_mark("P3")

    yg_spill_gp = C("gp")
    wo_v = w_out_t.rearrange("(mt p) o -> p mt o", p=128)
    wout_og = AR1b[:, 0:4096]
    wog_v = wout_og.rearrange("p (mt o) -> p mt o", mt=16)
    ygb = [AR1b[:, 4096:6144], AR1b[:, 6144:8192]]
    att_st2 = [AR1[:, 5120:5632], AR1[:, 5632:6144]]
    att_gp = [0, 0]
    prev_og_load_gate = [("sv", p3_sv), ("st", p3_st)]
    for og in range(8):
        W("gpsimd", lambda e, og=og: e.dma_start(
            out=wog_v, in_=wo_v[:, :, og * 256:(og + 1) * 256]),
          waits=prev_og_load_gate, inc=("gp", 16))
        my_wgp = C("gp")
        ykst = {}
        for k in range(16):
            par = k % 2
            wl_yg = [("gp", yg_spill_gp)]
            if k >= 1:
                wl_yg.append(("st", ykst.get(k - 1, 0)))
            W("sync", lambda e, k=k, par=par: e.dma_start(
                out=ygb[par], in_=yg_dram[k * 128:(k + 1) * 128, :]),
              waits=wl_yg, inc=("ld", 16))
            myld = C("ld")
            for o2 in range(2):
                for tc in range(4):
                    bi = o2 * 4 + tc
                    wl0 = [("gp", my_wgp), ("ld", myld)]
                    if k == 0:
                        wl0 = wl0 + p.bank_wait(bi)
                    W("tensor", lambda e, k=k, o2=o2, tc=tc, par=par, bi=bi:
                      e.matmul(ps[bi][:],
                               wout_og[:, k * 256 + o2 * 128:
                                       k * 256 + o2 * 128 + 128],
                               ygb[par][:, tc * 512:(tc + 1) * 512],
                               start=(k == 0), stop=(k == 15)),
                      waits=(wl0 if (o2 == 0 and tc == 0) else []),
                      inc=("st", 1) if (o2 == 1 and tc == 3) else None)
            ykst[k] = C("st")
        st_og = C("st")
        for o2 in range(2):
            for tc in range(4):
                bi = o2 * 4 + tc
                apar = (og * 8 + bi) % 2
                W("vector", lambda e, bi=bi, apar=apar: e.tensor_copy(
                    out=att_st2[apar], in_=ps[bi][:]),
                  waits=[("st", st_og), ("gp", att_gp[apar])], inc=("sv", 1))
                p.bank_set(bi, "sv")
                sv_e = C("sv")
                ot = og * 2 + o2
                tb = tc // 2
                W("gpsimd", lambda e, ot=ot, tc=tc, tb=tb, apar=apar:
                  e.dma_start(out=cc_in[tb * 2048 + ot * 128:
                                        tb * 2048 + (ot + 1) * 128,
                                        (tc % 2) * 512:((tc % 2) + 1) * 512],
                              in_=att_st2[apar]),
                  waits=[("sv", sv_e)], inc=("gp", 16))
                att_gp[apar] = C("gp")
        prev_og_load_gate = [("st", st_og)]
    p4_gp = C("gp")
    p4_st = C("st")
    p4_sv = C("sv")
    W("gpsimd", lambda e: e.collective_compute(
        "ReduceScatter", ALU.add, replica_groups=RG_PAIRS,
        ins=[cc_in.opt()], outs=[cc_out.opt()]),
      waits=[("gp", p4_gp)], inc=("cc", 1))

    p.phase_mark("P4")


    p.phase_mark("P5a")

    p5b_sv = C("sv")
    p5b_gp = C("gp")

    p.phase_mark("P5b")

    W("vector", lambda e: e.tensor_scalar(out=hax, in0=mn_sb, scalar1=-1.0,
                                          scalar2=None, op0=ALU.mult),
      waits=[("sv", p5b_sv)], inc=("sv", 1))
    W("vector", lambda e: e.tensor_tensor(out=hax, in0=hax, in1=mx_sb,
                                          op=ALU.max), inc=("sv", 1))
    sv_hax = C("sv")
    for ti in range(8):
        bi = 4 + (ti % 2)
        bank = ps[bi]
        W("tensor", lambda e, ti=ti, bank=bank: e.transpose(
            bank[:, 0:128], hax[:, ti * 128:(ti + 1) * 128], idf_sb[:]),
          waits=p.bank_wait(bi) + [("sv", sv_hax)], inc=("st", 1))
        st_t = C("st")
        W("vector", lambda e, ti=ti, bank=bank: e.tensor_reduce(
            out=shcol[:, ti:ti + 1], in_=bank[:, 0:128], axis=AX,
            op=ALU.max), waits=[("st", st_t)], inc=("sv", 1))
        p.bank_set(bi, "sv")
    sv_red8 = C("sv")
    W("vector", lambda e: e.tensor_scalar(out=shcol, in0=shcol,
                                          scalar1=QMAX_R, scalar2=QEPS,
                                          op0=ALU.mult, op1=ALU.max),
      waits=[("sv", sv_red8)], inc=("sv", 1))
    sv_shc = C("sv")
    W("vector", lambda e: e.reciprocal(out=rhcol, in_=shcol),
      waits=[("sv", sv_shc)], inc=("sv", 1))
    sv_sh = C("sv")
    W("tensor", lambda e: e.transpose(ps[6][0:8, 0:128], shcol, idf_sb[:]),
      waits=p.bank_wait(6) + [("sv", sv_sh)], inc=("st", 1))
    W("tensor", lambda e: e.transpose(ps[7][0:8, 0:128], rhcol, idf_sb[:]),
      waits=p.bank_wait(7), inc=("st", 1))
    st_r = C("st")
    W("vector", lambda e: e.tensor_copy(out=shrow, in_=ps[6][0:8, 0:128]),
      waits=[("st", st_r)], inc=("sv", 1))
    W("vector", lambda e: e.tensor_copy(out=rhrow, in_=ps[7][0:8, 0:128]),
      inc=("sv", 1))
    p.bank_set(6, "sv")
    p.bank_set(7, "sv")
    sv_rw = C("sv")
    W("vector", lambda e: e.tensor_copy(out=shrh, in_=shrow),
      waits=[("sv", sv_rw)], inc=("sv", 1))
    sv_h1 = C("sv")
    W("vector", lambda e: e.tensor_tensor(out=shrl, in0=shrow, in1=shrh,
                                          op=ALU.subtract),
      waits=[("sv", sv_h1)], inc=("sv", 1))
    W("vector", lambda e: e.tensor_copy(out=rhrh, in_=rhrow), inc=("sv", 1))
    sv_h2 = C("sv")
    W("vector", lambda e: e.tensor_tensor(out=rhrl, in0=rhrow, in1=rhrh,
                                          op=ALU.subtract),
      waits=[("sv", sv_h2)], inc=("sv", 1))
    sv_rows = C("sv")
    for dst, rh, rl in ((shb_sb, shrh, shrl), (rhb_sb, rhrh, rhrl)):
        for ti in range(8):
            bi = 4 + (ti % 2)
            bank = ps[bi]
            W("tensor", lambda e, bank=bank, rh=rh, ti=ti: e.matmul(
                bank[:, 0:128], ones_sb[:, ti * 128:(ti + 1) * 128],
                rh[0:8, :], start=True, stop=False),
              waits=p.bank_wait(bi) + [("sv", sv_rows)])
            W("tensor", lambda e, bank=bank, rl=rl, ti=ti: e.matmul(
                bank[:, 0:128], ones_sb[:, ti * 128:(ti + 1) * 128],
                rl[0:8, :], start=False, stop=True), inc=("st", 1))
            st_e = C("st")
            W("vector", lambda e, dst=dst, ti=ti, bank=bank: e.tensor_copy(
                out=dst[:, ti * 128:(ti + 1) * 128], in_=bank[:, 0:128]),
              waits=[("st", st_e)], inc=("sv", 1))
            p.bank_set(bi, "sv")
    sv_bcast = C("sv")

    p.phase_mark("P5c")

    hq_prev = {}
    hq_mult = {}
    for kI in range(NI):
        for tc in range(2):
            j = kI * 2 + tc
            hpar = j % 2
            wl = [("gp", p5b_gp)] if j == 0 else []
            if j >= 1:
                wl = wl + [("sv", hq_mult.get(j - 1, 0))]
            if j >= 2:
                wl = wl + list(hq_prev[j - 2])
            W("sync", lambda e, kI=kI, tc=tc, hpar=hpar: e.dma_start(
                out=hst2[hpar],
                in_=h_dram[kI * 128:(kI + 1) * 128,
                           tc * 512:(tc + 1) * 512]),
              waits=wl + [("sv", sv_bcast)], inc=("ld", 16))
            myld = C("ld")
            W("vector", lambda e, hpar=hpar, tc=tc: e.tensor_tensor(
                out=hst2[hpar], in0=hst2[hpar],
                in1=rhb_sb[:, tc * 512:(tc + 1) * 512], op=ALU.mult),
              waits=[("ld", myld)], inc=("sv", 1))
            svm = C("sv")
            hq_mult[j] = svm
            W("scalar", lambda e, hpar=hpar: e.activation(
                out=hst2[hpar], in_=hst2[hpar], func=AF.Identity,
                bias=fc_sb[:, 1:2]),
              waits=[("sv", svm)], inc=("ss", 1))
            ssm = C("ss")
            W("vector", lambda e, kI=kI, tc=tc, hpar=hpar: e.tensor_scalar(
                out=hq_ap(kI, tc * 512, (tc + 1) * 512), in0=hst2[hpar],
                scalar1=MAGIC, scalar2=None, op0=ALU.subtract),
              waits=[("ss", ssm)], inc=("sv", 1))
            hq_prev[j] = [("sv", C("sv")), ("ss", ssm)]
    hq_sv = C("sv")

    p.phase_mark("P5d")

    prev_dn_st = 0
    dn_prev = {}
    out_gp_prev = {}
    for dm in range(16):
        wl0 = ([("sv", hq_sv)] if dm == 0 else
               [("st", prev_dn_st)] + list(dn_prev.get(2 * (dm - 1), [])))
        dld = load_w_tile(wdn_f, w_dn, dm * 128, wl0)
        svq, _ssq = quant_w_tile(wdn_f, dld, sw2_c, rw2_c)
        mT = transpose_q(wdn_f, q2T_bf, NI, svq, "vector")
        for tc in range(2):
            bi = tc
            bank = ps[bi]
            wlD = p.bank_wait(bi) + [mT, ("sv", hq_sv)]
            for kI in range(NI):
                W("tensor", lambda e, kI=kI, tc=tc, bank=bank: e.matmul(
                    bank[:], q2T_bf[:, kI * 128:(kI + 1) * 128],
                    hq_ap(kI, tc * 512, (tc + 1) * 512),
                    start=(kI == 0), stop=(kI == NI - 1)),
                  waits=(wlD if kI == 0 else []),
                  inc=("st", 1) if kI == NI - 1 else None)
            st_mm = C("st")
            j5 = dm * 2 + tc
            dpar = j5 % 2
            wl_att = ([("cc", 1), ("sv", svq), mT]
                      + list(dn_prev.get(j5 - 1, []))
                      + list(dn_prev.get(j5 - 2, [])))
            W("sync", lambda e, dm=dm, tc=tc, dpar=dpar: e.dma_start(
                out=att_rl[dpar],
                in_=cc_out[dm * 128:(dm + 1) * 128,
                           tc * 512:(tc + 1) * 512]),
              waits=wl_att, inc=("ld", 16))
            W("sync", lambda e, dm=dm, tc=tc, dpar=dpar: e.dma_start(
                out=xres2[dpar],
                in_=x_ffn_t[dm * 128:(dm + 1) * 128,
                            tc * 512:(tc + 1) * 512]),
              inc=("ld", 16))
            myld = C("ld")
            W("vector", lambda e, tc=tc, bank=bank, dpar=dpar:
              e.scalar_tensor_tensor(
                  out=dnst2[dpar], in0=bank[:], scalar=sw2_c,
                  in1=shb_sb[:, tc * 512:(tc + 1) * 512],
                  op0=ALU.mult, op1=ALU.mult),
              waits=[("st", st_mm), ("gp", out_gp_prev.get(j5 - 2, 0))],
              inc=("sv", 1))
            p.bank_set(bi, "sv")
            W("vector", lambda e, dpar=dpar: e.tensor_tensor(
                out=dnst2[dpar], in0=dnst2[dpar], in1=att_rl[dpar],
                op=ALU.add),
              waits=[("ld", myld)], inc=("sv", 1))
            W("vector", lambda e, dpar=dpar: e.tensor_tensor(
                out=dnst2[dpar], in0=dnst2[dpar], in1=xres2[dpar],
                op=ALU.add), inc=("sv", 1))
            sv_o = C("sv")
            dn_prev[j5] = [("sv", sv_o)]
            W("gpsimd", lambda e, dm=dm, tc=tc, dpar=dpar: e.dma_start(
                out=out_d[dm * 128:(dm + 1) * 128,
                          tc * 512:(tc + 1) * 512],
                in_=dnst2[dpar]),
              waits=[("sv", sv_o)], inc=("gp", 16))
            out_gp_prev[j5] = C("gp")
        prev_dn_st = C("st")
    if dbg:
        for nm, t, shp in (("d_zs", zs_dram, [2048, T]),
                           ("d_xact", xact_dram, [2048, T]),
                           ("d_b", b_dram, [1024, T]),
                           ("d_c", c_dram, [4096, T]),
                           ("d_h", h_dram, [II, TH]),
                           ("d_ccin", cc_in, [2 * DM, TH]),
                           ("d_ccout", cc_out, [DM, TH])):
            dt_ = BF16 if nm in ("d_b", "d_c") else F32
            od = nc.dram_tensor(nm, shp, dt_, kind="ExternalOutput").ap()
            W("gpsimd", lambda e, od=od, t=t: e.dma_start(out=od, in_=t),
              waits=[("gp", C("gp")), ("cc", 1), ("sv", C("sv")),
                     ("ss", C("ss")), ("st", C("st")), ("ld", C("ld"))],
              inc=("gp", 16))
    out_gp_total = C("gp")
    W("gpsimd", lambda e: e.nop(), waits=[("gp", out_gp_total)])

    if cut is not None:
        p.truncate(cut)

    with es:
        sem_handles = {}
        for s in SEMS:
            sem_handles[s] = es.enter_context(nc.semaphore(s))
        block = es.enter_context(nc.Block())

        @block.sync
        def _(sync):
            p.emit_engine("sync", sync, sem_handles)

        @block.gpsimd
        def _(gpsimd):
            p.emit_engine("gpsimd", gpsimd, sem_handles)

        @block.scalar
        def _(scalar):
            p.emit_engine("scalar", scalar, sem_handles)

        @block.vector
        def _(vector):
            p.emit_engine("vector", vector, sem_handles)

        @block.tensor
        def _(tensor):
            p.emit_engine("tensor", tensor, sem_handles)
    return nc


def _prep_core_inputs(b, j, x, Wi, Wdti, Wdtp, dtb, cwi, cbi, alogi, dvi, Wo,
                      Wgu, Wdn, consts):
    DI, DXB = 4096, 1024
    ch0 = j * 2048
    xb = np.ascontiguousarray(x[b])
    d = {}
    d["x_full"] = xb
    d["x_ffn"] = np.ascontiguousarray(xb[j * TH:(j + 1) * TH])
    d["x_ffn_t"] = np.ascontiguousarray(d["x_ffn"].T)
    rows = [np.arange(ch0, ch0 + 2048)]
    xr = []
    for k in range(16):
        for g2 in range(2):
            g = j * 32 + 2 * k + g2
            for pp in range(16):
                xr.append(DI + g * 16 + pp)
    rows.append(np.array(xr))
    br = []
    for k in range(16):
        for n in range(16):
            for g2 in range(2):
                g = j * 32 + 2 * k + g2
                br.append(DI + DXB + g * 16 + n)
    rows.append(np.array(br))
    cr = []
    for k in range(16):
        for n in range(16):
            for hh in range(8):
                h = j * 128 + k * 8 + hh
                cr.append(DI + 2 * DXB + h * 16 + n)
    rows.append(np.array(cr))
    rows = np.concatenate(rows)
    d["w_in_t"] = np.ascontiguousarray(Wi[rows].T)
    d["w_dtin_t"] = np.ascontiguousarray(Wdti.T)
    d["w_dtp_t"] = np.ascontiguousarray(Wdtp[ch0:ch0 + 2048].T)
    d["w_out_t"] = np.ascontiguousarray(Wo[:, ch0:ch0 + 2048].T)
    d["w_gu"] = Wgu
    d["w_dn"] = Wdn
    d["dt_b"] = np.ascontiguousarray(dtb[ch0:ch0 + 2048].reshape(16, 128).T)
    d["a_log"] = np.ascontiguousarray(
        alogi[ch0:ch0 + 2048].reshape(16, 128, 16).transpose(1, 0, 2)
        .reshape(128, 256))
    d["conv_w"] = np.ascontiguousarray(
        cwi[ch0:ch0 + 2048].reshape(16, 128, 4).transpose(1, 0, 2)
        .reshape(128, 64))
    d["conv_b"] = np.ascontiguousarray(cbi[ch0:ch0 + 2048].reshape(16, 128).T)
    d["d_vec"] = np.ascontiguousarray(dvi[ch0:ch0 + 2048].reshape(16, 128).T)
    d.update(consts)
    return d


def _mk_esel8():
    e8 = np.zeros((8, 8, 128), BF)
    for ti in range(8):
        e8[ti, ti, :] = 1
    return np.ascontiguousarray(e8.reshape(8, 8 * 128))


def _make_consts():
    eb = np.zeros((128, NN, 128), BF)
    for kpar in range(2):
        for n in range(NN):
            for q in range(128):
                g = q // 64
                for s in range(2):
                    eb[kpar * 64 + s * 32 + n * 2 + g, n, q] = 1
    ec = np.zeros((128, 8, 128), BF)
    for nm in range(8):
        for q in range(128):
            hh = q // 16
            for s in range(2):
                ec[nm * 16 + s * 8 + hh, nm, q] = 1
    ex = np.zeros((128, 2, 128), BF)
    for kpar in range(2):
        for q in range(128):
            g = q // 64
            pp = q % 16
            for s in range(2):
                ex[kpar * 64 + s * 32 + g * 16 + pp, kpar, q] = 1
    return {
        "eb_in": np.ascontiguousarray(eb.reshape(128, NN * 128)),
        "ec_in": np.ascontiguousarray(ec.reshape(128, 8 * 128)),
        "ex_in": np.ascontiguousarray(ex.reshape(128, 2 * 128)),
        "ones_in": _mk_esel8(),
        "ident_bf": np.eye(128).astype(BF),
        "fconst": np.tile(np.array([[LNEPS, MAGIC, -0.5, -MAGIC, 0, 0, 0, 0]],
                                   np.float32), (128, 1)),
        "ident_f32": np.eye(128, dtype=np.float32),
    }


_NC_CACHE = {}


def kernel(L0_attn_hidden_states, L0_attn_in_proj_w, L0_attn_dt_in_proj_w,
           L0_attn_dt_proj_w, L0_attn_dt_proj_b, L0_attn_conv_w,
           L0_attn_conv_b, L0_attn_A_log, L0_attn_D, L0_attn_out_proj_w,
           L0_ffn_w_gate_up, L0_ffn_w_down):
    f = np.float32
    x = np.asarray(L0_attn_hidden_states, f)
    Wi = np.asarray(L0_attn_in_proj_w, f)
    Wdti = np.asarray(L0_attn_dt_in_proj_w, f)
    Wdtp = np.asarray(L0_attn_dt_proj_w, f)
    dtb = np.asarray(L0_attn_dt_proj_b, f)
    cwi = np.asarray(L0_attn_conv_w, f)
    cbi = np.asarray(L0_attn_conv_b, f)
    alogi = np.asarray(L0_attn_A_log, f)
    dvi = np.asarray(L0_attn_D, f)
    Wo = np.asarray(L0_attn_out_proj_w, f)
    Wgu = np.asarray(L0_ffn_w_gate_up, f)
    Wdn = np.asarray(L0_ffn_w_down, f)

    consts = _make_consts()
    in_maps = []
    for core in range(8):
        b, j = core // 2, core % 2
        in_maps.append(_prep_core_inputs(b, j, x, Wi, Wdti, Wdtp, dtb, cwi,
                                         cbi, alogi, dvi, Wo, Wgu, Wdn,
                                         consts))
    if "nc" not in _NC_CACHE:
        _NC_CACHE["nc"] = build_nc()
    nc = _NC_CACHE["nc"]
    res = run_bass_kernel_spmd(nc, in_maps, core_ids=list(range(8)))
    out = np.empty((4, T, DM), np.float32)
    for core in range(8):
        b, j = core // 2, core % 2
        out[b, j * TH:(j + 1) * TH, :] = np.asarray(
            res.results[core]["out"], np.float32).T
    return out


# version 07 (final)
# speedup vs baseline: 1.3777x; 1.0893x over previous
import numpy as np
import ml_dtypes
from contextlib import ExitStack

import concourse.bass as bass
import concourse.mybir as mybir
from concourse.bass_utils import run_bass_kernel_spmd

F32 = mybir.dt.float32
BF16 = mybir.dt.bfloat16
AF = mybir.ActivationFunctionType
ALU = mybir.AluOpType
AX = mybir.AxisListType.X
BF = ml_dtypes.bfloat16

MAGIC = 12582912.0
QMAX_R = 1.0 / 127.0
QEPS = 1e-8
LNEPS = 1e-5

T = 2048
TH = 1024
DM = 2048
KT = 16
NN = 16
TC = 512
NTC = T // TC
II = 5632
NI = 44

ENGINES = ("sync", "gpsimd", "scalar", "vector", "tensor")
RG_PAIRS = [[0, 1], [2, 3], [4, 5], [6, 7]]
SEMS = ("ld", "gp", "sv", "ss", "st", "cc",
        "y0", "y1", "y2", "w0", "w1")


class Plan:
    def __init__(self):
        self.ops = {e: [] for e in ENGINES}
        self.counts = {s: 0 for s in SEMS}
        self.bank_free = [[] for _ in range(8)]
        self.phase_marks = {}

    def op(self, eng, fn, waits=(), inc=None):
        self.ops[eng].append((fn, [(s, int(v)) for (s, v) in waits if v > 0],
                              inc))
        if inc is not None:
            self.counts[inc[0]] += inc[1]

    def c(self, name):
        return self.counts[name]

    def bank_wait(self, i):
        return list(self.bank_free[i])

    def bank_set(self, i, sem):
        self.bank_free[i] = [(sem, self.counts[sem])]

    def phase_mark(self, name):
        self.phase_marks[name] = {e: len(self.ops[e]) for e in ENGINES}

    def truncate(self, name):
        m = self.phase_marks[name]
        for e in ENGINES:
            self.ops[e] = self.ops[e][:m[e]]

    def emit_engine(self, ename, eng, sem_handles):
        hi = {}
        for fn, waits, inc in self.ops[ename]:
            for s, v in waits:
                if hi.get(s, 0) < v:
                    eng.wait_ge(sem_handles[s], v)
                    hi[s] = v
            ins = fn(eng)
            if inc is not None:
                ins.then_inc(sem_handles[inc[0]], inc[1])


def build_nc(cut=None, dbg=False):
    nc = bass.Bass("TRN2", target_bir_lowering=False, debug=False,
                   detect_race_conditions=False)

    def din(name, shape, dt=F32):
        return nc.dram_tensor(name, shape, dt, kind="ExternalInput").ap()

    x_full = din("x_full", [T, DM])
    x_ffn = din("x_ffn", [TH, DM])
    x_ffn_t = din("x_ffn_t", [DM, TH])
    w_in_t = din("w_in_t", [DM, 5120])
    w_dtin_t = din("w_dtin_t", [DM, 128])
    w_dtp_t = din("w_dtp_t", [128, 2048])
    w_out_t = din("w_out_t", [2048, DM])
    w_gu = din("w_gu", [2 * II, DM])
    w_dn = din("w_dn", [DM, II])
    dt_b = din("dt_b", [128, KT])
    a_log = din("a_log", [128, KT * NN])
    conv_w = din("conv_w", [128, KT * 4])
    conv_b = din("conv_b", [128, KT])
    d_vec = din("d_vec", [128, KT])
    eb_in = din("eb_in", [128, NN * 128], BF16)
    ec_in = din("ec_in", [128, 8 * 128], BF16)
    ex_in = din("ex_in", [128, 2 * 128], BF16)
    ones_in = din("ones_in", [8, 8 * 128], BF16)
    ident_bf = din("ident_bf", [128, 128], BF16)
    fconst = din("fconst", [128, 8])
    ident_f32 = din("ident_f32", [128, 128], F32)

    zs_dram = nc.dram_tensor("zs_dram", [2048, T], F32).ap()
    xact_dram = nc.dram_tensor("xact_dram", [2048, T], F32).ap()
    b_dram = nc.dram_tensor("b_dram", [1024, T], BF16).ap()
    c_dram = nc.dram_tensor("c_dram", [4096, T], BF16).ap()
    h_dram = nc.dram_tensor("h_dram", [II, TH], F32).ap()
    yg_dram = nc.dram_tensor("yg_dram", [2048, T], BF16).ap()
    cc_in = nc.dram_tensor("cc_in", [2 * DM, TH], F32).ap()
    cc_out = nc.dram_tensor("cc_out", [DM, TH], F32).ap()
    out_d = nc.dram_tensor("out", [DM, TH], F32, kind="ExternalOutput").ap()

    p = Plan()
    W, C = p.op, p.c
    es = ExitStack()
    sbuf = lambda name, shape, dt: es.enter_context(
        nc.sbuf_tensor(name, shape, dt))

    AR1 = sbuf("AR1", [128, 17408], F32)
    AR2 = sbuf("AR2", [128, 16384], F32)
    AR3 = sbuf("AR3", [128, 8192], F32)
    AR1b = AR1[:].bitcast(BF16)
    AR2b = AR2[:].bitcast(BF16)
    AR3b = AR3[:].bitcast(BF16)

    dtm_sb = sbuf("dtm_sb", [128, 2048], BF16)
    wdtp_sb = sbuf("wdtp_sb", [128, 2048], BF16)
    b_sb = sbuf("b_sb", [128, 2 * 2048], BF16)
    a_sb = sbuf("a_sb", [128, KT * NN], F32)
    eb_sb = sbuf("eb_sb", [128, NN * 128], BF16)
    ec_sb = sbuf("ec_sb", [128, 8 * 128], BF16)
    ex_sb = sbuf("ex_sb", [128, 2 * 128], BF16)
    ones_sb = sbuf("ones_sb", [8, 8 * 128], BF16)
    idb_sb = sbuf("idb_sb", [128, 128], BF16)
    idf_sb = sbuf("idf_sb", [128, 128], F32)
    cw_sb = sbuf("cw_sb", [128, KT * 4], F32)
    cb_sb = sbuf("cb_sb", [128, KT], F32)
    db_sb = sbuf("db_sb", [128, KT], F32)
    dv_sb = sbuf("dv_sb", [128, KT], F32)
    cols = sbuf("cols", [128, 64], F32)
    mxmn = sbuf("mxmn", [128, 2 * TH], F32)
    sgh = sbuf("sgh", [128, 2048], F32)
    fc_sb = sbuf("fc_sb", [128, 8], F32)

    ps = [es.enter_context(nc.psum_tensor(f"ps{i}", [128, TC], F32))
          for i in range(8)]

    xn_fm = AR1b
    win_bf = [AR2b[:, 0:2048], AR2b[:, 2048:4096]]
    feat_f = AR2[:, 2048:4096]
    fhi = AR2b[:, 8192:10240]
    flo = AR2b[:, 10240:12288]
    zst2 = [AR2[:, 6144:6656], AR2[:, 6656:7168]]
    xln = AR2[:, 7168:9216]
    sqscr = AR2[:, 9216:11264]
    xnt_f = AR2[:, 11264:13312]
    xep = AR2[:, 2048:4099]
    cacc = AR2[:, 4608:6656]
    xastage = AR2[:, 0:2048]
    x_ilv = AR3b
    xact_k = [AR1[:, 0:2048], AR1[:, 2048:4096]]
    dt_kv = AR1[:, 4096:6144]
    dtx_kv = AR1[:, 6144:8192]
    dA_k = [AR1[:, 8192:10240], AR1[:, 10240:12288]]
    t1_k = dA_k[0]
    zs_one = AR1[:, 12288:14336]
    qTg_bf = AR1b[:, 28672:30720]
    qTu_bf = AR1b[:, 30720:32768]
    ygst = AR1b[:, 32768:34816]
    h_sb = AR3[:, 0:2048]
    u_sb = [AR3[:, 2048:2560], AR3[:, 2560:3072]]
    yb3 = [AR3b[:, 6144:6656], AR3b[:, 6656:7168], AR3b[:, 15360:15872]]
    c_sb = [[AR3b[:, 7168 + (2 * par + ti) * 2048:
                  7168 + (2 * par + ti + 1) * 2048]
             for ti in range(2)] for par in range(2)]
    wout2 = [AR1b[:, 0:4096], AR1b[:, 4096:8192]]
    ygb3 = [AR1b[:, 8192 + i * 2048: 8192 + (i + 1) * 2048] for i in range(3)]
    xq_fm = AR2b[:, 0:16384]
    hq_a = AR2b
    hq_b = AR3b[:, 0:12288]

    def hq_ap(kI, c0, c1):
        if kI < 32:
            return hq_a[:, kI * 1024 + c0: kI * 1024 + c1]
        return hq_b[:, (kI - 32) * 1024 + c0: (kI - 32) * 1024 + c1]

    wgu_g = [AR2[:, 8192:10240], AR2[:, 10240:12288]]
    wgu_u = [AR2[:, 12288:14336], AR2[:, 14336:16384]]
    sg2 = [sgh[:, 0:512], sgh[:, 512:1024]]
    hst2 = [sgh[:, 1024:1536], sgh[:, 1536:2048]]
    mx_sb = mxmn[:, 0:TH]
    mn_sb = mxmn[:, TH:2 * TH]
    wdn_f = AR1[:, 0:5632]
    q2n_bf = AR1b[:, 11264:16896]
    q2T_bf = AR1b[:, 16896:22528]
    shb_sb = AR3[:, 6144:7168]
    rhb_sb = AR3[:, 7168:8192]
    dnst2 = [mxmn[:, 0:512], mxmn[:, 512:1024]]
    att_rl = [mxmn[:, 1024:1536], mxmn[:, 1536:2048]]
    xres2 = [AR1[:, 5120:5632], AR1[:, 5632:6144]]
    hax = AR3[:, 6144:7168]
    shrow = AR1[0:8, 6144:6272]
    rhrow = AR1[0:8, 6272:6400]
    shrh = AR1b[0:8, 12800:12928]
    shrl = AR1b[0:8, 12928:13056]
    rhrh = AR1b[0:8, 13056:13184]
    rhrl = AR1b[0:8, 13184:13312]

    mean_c = cols[:, 0:1]
    sum_c = cols[:, 1:2]
    m2_c = cols[:, 2:3]
    var_c = cols[:, 3:4]
    rstd_c = cols[:, 4:5]
    aw_c = cols[:, 5:6]
    sw_g2 = [cols[:, 6:7], cols[:, 32:33]]
    sw_u2 = [cols[:, 7:8], cols[:, 33:34]]
    rw_g2 = [cols[:, 8:9], cols[:, 34:35]]
    rw_u2 = [cols[:, 9:10], cols[:, 35:36]]
    sx_c = cols[:, 10:11]
    rx_c = cols[:, 11:12]
    sw2_c = cols[:, 12:13]
    rw2_c = cols[:, 13:14]
    shcol = cols[:, 16:24]
    rhcol = cols[:, 24:32]

    for dst, src in ((a_sb[:], a_log), (cw_sb[:], conv_w), (cb_sb[:], conv_b),
                     (db_sb[:], dt_b), (dv_sb[:], d_vec), (ec_sb[:], ec_in),
                     (ex_sb[:], ex_in), (idb_sb[:], ident_bf),
                     (idf_sb[:], ident_f32), (eb_sb[:], eb_in),
                     (ones_sb[:], ones_in), (fc_sb[:], fconst)):
        W("sync", lambda e, d=dst, s=src: e.dma_start(out=d, in_=s),
          inc=("ld", 16))
    consts_ld = C("ld") + 16
    W("gpsimd", lambda e: e.dma_start(out=wdtp_sb[:], in_=w_dtp_t),
      inc=("gp", 16))
    W("scalar", lambda e: e.activation(out=a_sb[:], in_=a_sb[:], func=AF.Exp),
      waits=[("ld", consts_ld)], inc=("ss", 1))
    W("vector", lambda e: e.tensor_scalar(out=a_sb[:], in0=a_sb[:],
                                          scalar1=-1.0, scalar2=None,
                                          op0=ALU.mult),
      waits=[("ss", C("ss"))], inc=("sv", 1))

    def emit_ln_quant(src_rows, out_f, quantize, extra_ld_waits,
                      xln=None, sqscr=None):
        W("sync", lambda e, xln=xln: e.dma_start(out=xln, in_=src_rows),
          waits=extra_ld_waits, inc=("ld", 16))
        myld = C("ld")
        W("vector", lambda e: e.tensor_reduce(out=sum_c, in_=xln, axis=AX,
                                              op=ALU.add),
          waits=[("ld", myld)], inc=("sv", 1))
        sv_red = C("sv")
        W("scalar", lambda e: e.activation(out=sqscr, in_=xln,
                                           func=AF.Square, accum_out=m2_c),
          waits=[("ld", myld)], inc=("ss", 1))
        ss_sq = C("ss")
        W("vector", lambda e: e.tensor_scalar(out=mean_c, in0=sum_c,
                                              scalar1=1.0 / DM, scalar2=None,
                                              op0=ALU.mult),
          waits=[("sv", sv_red)], inc=("sv", 1))
        sv_mean = C("sv")
        W("vector", lambda e: e.tensor_scalar(out=m2_c, in0=m2_c,
                                              scalar1=1.0 / DM, scalar2=None,
                                              op0=ALU.mult),
          waits=[("ss", ss_sq)], inc=("sv", 1))
        sv_m2 = C("sv")
        W("vector", lambda e: e.scalar_tensor_tensor(
            out=var_c, in0=mean_c, scalar=mean_c, in1=m2_c,
            op0=ALU.mult, op1=ALU.subtract),
          waits=[("sv", sv_m2)], inc=("sv", 1))
        sv_var = C("sv")
        W("scalar", lambda e: e.activation(out=rstd_c, in_=var_c, func=AF.Ln,
                                           scale=-1.0, bias=fc_sb[:, 0:1]),
          waits=[("sv", sv_var)], inc=("ss", 1))
        ss_ln = C("ss")
        W("scalar", lambda e: e.activation(out=rstd_c, in_=rstd_c,
                                           func=AF.Exp, scale=fc_sb[:, 2:3]),
          waits=[("ss", ss_ln)], inc=("ss", 1))
        ss_rstd = C("ss")
        if not quantize:
            W("vector", lambda e: e.tensor_scalar(
                out=out_f, in0=xln, scalar1=mean_c, scalar2=rstd_c,
                op0=ALU.subtract, op1=ALU.mult),
              waits=[("ss", ss_rstd)], inc=("sv", 1))
            return C("sv"), ss_rstd
        W("vector", lambda e: e.tensor_scalar(
            out=sqscr, in0=xln, scalar1=mean_c, scalar2=rstd_c,
            op0=ALU.subtract, op1=ALU.mult),
          waits=[("ss", ss_rstd)], inc=("sv", 1))
        W("vector", lambda e: e.tensor_reduce(out=aw_c, in_=sqscr, axis=AX,
                                              op=ALU.max,
                                              apply_absolute_value=True),
          inc=("sv", 1))
        sv_aw = C("sv")
        W("vector", lambda e: e.tensor_scalar(out=sx_c, in0=aw_c,
                                              scalar1=QMAX_R, scalar2=QEPS,
                                              op0=ALU.mult, op1=ALU.max),
          waits=[("sv", sv_aw)], inc=("sv", 1))
        sv_sx = C("sv")
        W("vector", lambda e: e.reciprocal(out=rx_c, in_=sx_c),
          waits=[("sv", sv_sx)], inc=("sv", 1))
        sv_q = C("sv")
        W("scalar", lambda e: e.activation(out=sqscr, in_=sqscr,
                                           func=AF.Identity, scale=rx_c,
                                           bias=fc_sb[:, 1:2]),
          waits=[("sv", sv_q)], inc=("ss", 1))
        ss_rnd = C("ss")
        W("vector", lambda e: e.tensor_scalar(out=out_f, in0=sqscr,
                                              scalar1=MAGIC, scalar2=sx_c,
                                              op0=ALU.subtract, op1=ALU.mult),
          waits=[("ss", ss_rnd)], inc=("sv", 1))
        return C("sv"), ss_rnd

    def emit_tile_transpose(src_bf, dstv, tt, nmt, sv_src):
        for msq in range((nmt + 3) // 4):
            bi = 4 + (msq % 2)
            bank = ps[bi]
            cnt = min(4, nmt - msq * 4)
            for s4 in range(cnt):
                ms = msq * 4 + s4
                wl = p.bank_wait(bi) if s4 == 0 else []
                if s4 == 0:
                    wl = wl + [("sv", sv_src)]
                W("tensor", lambda e, ms=ms, s4=s4, bank=bank, src_bf=src_bf:
                  e.transpose(bank[:, s4 * 128:(s4 + 1) * 128],
                              src_bf[:, ms * 128:(ms + 1) * 128], idf_sb[:]),
                  waits=wl, inc=("st", 1))
            st_g = C("st")
            dst = dstv[:, msq * 4: msq * 4 + cnt, tt * 128:(tt + 1) * 128]
            W("vector", lambda e, dst=dst, bank=bank, cnt=cnt: e.tensor_copy(
                out=dst,
                in_=bank[:, 0:cnt * 128].rearrange("p (s t) -> p s t", s=cnt)),
              waits=[("st", st_g)], inc=("sv", 1))
            p.bank_set(bi, "sv")
        return C("st"), C("sv")

    xnv = AR1b[:, 0:32768].rearrange("p (mt t) -> p mt t", mt=16)
    prev_users = [("sv", 0), ("ss", 0)]
    for tt in range(16):
        sv_xn, ss_u = emit_ln_quant(x_full[tt * 128:(tt + 1) * 128, :],
                                    xnt_f, False, prev_users,
                                    xln=xln[:], sqscr=sqscr[:])
        st_t, sv_t = emit_tile_transpose(xnt_f, xnv, tt, 16, sv_xn)
        prev_users = [("sv", sv_xn), ("ss", ss_u), ("st", st_t)]
    xn_sv = C("sv")
    p.phase_mark("P0")

    wdtin_v = w_dtin_t.rearrange("(mt p) o -> p mt o", p=128)
    wb0 = win_bf[0][:].rearrange("p (mt o) -> p mt o", mt=16)
    W("gpsimd", lambda e: e.dma_start(out=wb0, in_=wdtin_v), inc=("gp", 16))
    dtin_gp = C("gp")
    for tcp in range(4):
        bi = tcp % 2
        bank = ps[bi]
        wl0 = p.bank_wait(bi) + [("gp", dtin_gp), ("sv", xn_sv)]
        for mt in range(16):
            W("tensor", lambda e, tcp=tcp, mt=mt, bank=bank: e.matmul(
                bank[:], win_bf[0][:, mt * 128:(mt + 1) * 128],
                xn_fm[:, mt * 2048 + tcp * 512: mt * 2048 + (tcp + 1) * 512],
                start=(mt == 0), stop=(mt == 15)),
              waits=(wl0 if mt == 0 else []),
              inc=("st", 1) if mt == 15 else None)
        st_m = C("st")
        W("vector", lambda e, tcp=tcp, bank=bank: e.tensor_copy(
            out=dtm_sb[:, tcp * 512:(tcp + 1) * 512], in_=bank[:]),
          waits=[("st", st_m)], inc=("sv", 1))
        p.bank_set(bi, "sv")
    dtm_sv = C("sv")
    p.phase_mark("P1")

    win_v = w_in_t.rearrange("(mt p) o -> p mt o", p=128)
    gp_scatter_mark = 0
    zst_used = [0, 0]
    win_used = [[("sv", 0)], [("gp", dtin_gp), ("sv", dtm_sv), ("st", 0)]]
    win_used[1] = [("st", C("st"))]
    win_free = [[("st", C("st"))], []]
    for ot in range(40):
        par = ot % 2
        wb = win_bf[par][:].rearrange("p (mt o) -> p mt o", mt=16)
        W("gpsimd", lambda e, ot=ot, wb=wb: e.dma_start(
            out=wb, in_=win_v[:, :, ot * 128:(ot + 1) * 128]),
          waits=win_free[par], inc=("gp", 16))
        my_gp = C("gp")
        for tc in range(4):
            bi = (ot * 4 + tc) % 4
            bank = ps[bi]
            wl0 = p.bank_wait(bi) + [("gp", my_gp)]
            if ot == 0 and tc == 0:
                wl0.append(("sv", xn_sv))
            for mt in range(16):
                W("tensor", lambda e, ot=ot, tc=tc, mt=mt, bank=bank, par=par:
                  e.matmul(bank[:], win_bf[par][:, mt * 128:(mt + 1) * 128],
                           xn_fm[:, mt * 2048 + tc * 512:
                                 mt * 2048 + (tc + 1) * 512],
                           start=(mt == 0), stop=(mt == 15)),
                  waits=(wl0 if mt == 0 else []),
                  inc=("st", 1) if mt == 15 else None)
            st_m = C("st")
            if ot < 16:
                zpar = (ot * 4 + tc) % 2
                W("scalar", lambda e, bank=bank, zpar=zpar: e.activation(
                    out=zst2[zpar][:], in_=bank[:], func=AF.Silu),
                  waits=[("st", st_m), ("gp", zst_used[zpar])], inc=("ss", 1))
                ss_z = C("ss")
                p.bank_set(bi, "ss")
                W("gpsimd", lambda e, ot=ot, tc=tc, zpar=zpar: e.dma_start(
                    out=zs_dram[ot * 128:(ot + 1) * 128,
                                tc * 512:(tc + 1) * 512], in_=zst2[zpar][:]),
                  waits=[("ss", ss_z)], inc=("gp", 16))
                zst_used[zpar] = C("gp")
            else:
                W("vector", lambda e, bank=bank, tc=tc: e.tensor_copy(
                    out=feat_f[:, tc * 512:(tc + 1) * 512], in_=bank[:]),
                  waits=[("st", st_m), ("gp", gp_scatter_mark)],
                  inc=("sv", 1))
                p.bank_set(bi, "sv")
        win_free[par] = [("st", C("st"))]
        if ot >= 16:
            W("vector", lambda e: e.tensor_copy(out=fhi[:], in_=feat_f[:]),
              inc=("sv", 1))
            W("vector", lambda e: e.tensor_tensor(out=flo[:], in0=feat_f[:],
                                                  in1=fhi[:],
                                                  op=ALU.subtract),
              inc=("sv", 1))
            sv_hl = C("sv")
            if ot < 20:
                oti = ot - 16
                for kk in range(4):
                    k = oti * 4 + kk
                    for s, srcb in ((0, fhi), (1, flo)):
                        dst = x_ilv[(k % 2) * 64 + s * 32:
                                    (k % 2) * 64 + s * 32 + 32,
                                    (k // 2) * 2048:(k // 2 + 1) * 2048]
                        W("gpsimd", lambda e, dst=dst, srcb=srcb, kk=kk:
                          e.dma_start(out=dst,
                                      in_=srcb[kk * 32:(kk + 1) * 32, :]),
                          waits=[("sv", sv_hl)], inc=("gp", 16))
            elif ot < 24:
                oti = ot - 20
                for kk in range(4):
                    k = oti * 4 + kk
                    for s, srcb in ((0, fhi), (1, flo)):
                        dst = b_dram[k * 64 + s * 32: k * 64 + s * 32 + 32, :]
                        W("gpsimd", lambda e, dst=dst, srcb=srcb, kk=kk:
                          e.dma_start(out=dst,
                                      in_=srcb[kk * 32:(kk + 1) * 32, :]),
                          waits=[("sv", sv_hl)], inc=("gp", 16))
            else:
                k = ot - 24
                cv = c_dram.rearrange("(k n s h) t -> k n s h t",
                                      k=16, n=16, s=2, h=8)
                for s, srcb in ((0, fhi), (1, flo)):
                    dst = cv[k, :, s, :, :]
                    W("gpsimd", lambda e, dst=dst, srcb=srcb: e.dma_start(
                        out=dst, in_=srcb[:]),
                      waits=[("sv", sv_hl)], inc=("gp", 16))
            gp_scatter_mark = C("gp")
    p2_gp = C("gp")
    p.phase_mark("P2")

    xa_gp = 0
    for k in range(KT):
        bi = 4 + (k % 2)
        bank = ps[bi]
        W("vector", lambda e: e.memset(xep[:, 0:3], 0.0), inc=("sv", 1))
        for tcp in range(4):
            wl = p.bank_wait(bi)
            if k == 0 and tcp == 0:
                wl = wl + [("gp", p2_gp), ("ld", consts_ld)]
            W("tensor", lambda e, k=k, tcp=tcp, bank=bank: e.matmul(
                bank[:], ex_sb[(k % 2) * 64:(k % 2) * 64 + 64,
                               (k % 2) * 128:(k % 2) * 128 + 128],
                x_ilv[(k % 2) * 64:(k % 2) * 64 + 64,
                                         (k // 2) * 2048 + tcp * 512:
                                         (k // 2) * 2048 + (tcp + 1) * 512],
                start=True, stop=True),
              waits=wl, inc=("st", 1))
            st_m = C("st")
            W("vector", lambda e, tcp=tcp, bank=bank: e.tensor_copy(
                out=xep[:, 3 + tcp * 512: 3 + (tcp + 1) * 512], in_=bank[:]),
              waits=[("st", st_m)], inc=("sv", 1))
            p.bank_set(bi, "sv")
        W("vector", lambda e, k=k: e.tensor_scalar(
            out=cacc[:], in0=xep[:, 3:3 + T],
            scalar1=cw_sb[:, k * 4 + 3: k * 4 + 4],
            scalar2=None, op0=ALU.mult), inc=("sv", 1))
        for tap in range(3):
            W("vector", lambda e, k=k, tap=tap: e.scalar_tensor_tensor(
                out=cacc[:], in0=xep[:, tap:tap + T],
                scalar=cw_sb[:, k * 4 + tap: k * 4 + tap + 1], in1=cacc[:],
                op0=ALU.mult, op1=ALU.add), inc=("sv", 1))
        sv_cv = C("sv")
        W("scalar", lambda e, k=k: e.activation(
            out=xastage[:], in_=cacc[:], func=AF.Silu,
            bias=cb_sb[:, k:k + 1]),
          waits=[("sv", sv_cv), ("gp", xa_gp)], inc=("ss", 1))
        ss_xa = C("ss")
        W("gpsimd", lambda e, k=k: e.dma_start(
            out=xact_dram[k * 128:(k + 1) * 128, :], in_=xastage[:]),
          waits=[("ss", ss_xa)], inc=("gp", 16))
        xa_gp = C("gp")
    p2b_gp = C("gp")
    p2b_sv = C("sv")
    p.phase_mark("P2b")

    xqv = AR2b[:, 0:16384].rearrange("p (mt t) -> p mt t", mt=16)
    xlnq = AR1[:, 0:2048]
    sqscrq = AR1[:, 2048:4096]
    xntq = AR1[:, 4096:6144]
    prev_users = [("sv", p2b_sv), ("st", C("st")), ("gp", p2_gp)]
    for tt in range(8):
        sv_xq, ss_u = emit_ln_quant(x_ffn[tt * 128:(tt + 1) * 128, :],
                                    xntq, True, prev_users,
                                    xln=xlnq, sqscr=sqscrq)
        st_t, sv_t = emit_tile_transpose(xntq, xqv, tt, 16, sv_xq)
        prev_users = [("sv", sv_xq), ("ss", ss_u), ("st", st_t)]
    xq_sv = C("sv")
    p5a_st = C("st")
    p5a_ss = C("ss")
    p.phase_mark("P5A")

    def load_w_tile(wdst, src, rows0, wl_extra):
        W("sync", lambda e, wdst=wdst: e.dma_start(
            out=wdst, in_=src[rows0:rows0 + 128, :]),
          waits=wl_extra, inc=("ld", 16))
        return C("ld")

    def quant_w_tile(wdst, myld, s_col, r_col):
        W("vector", lambda e, wdst=wdst: e.tensor_reduce(
            out=aw_c, in_=wdst, axis=AX, op=ALU.max,
            apply_absolute_value=True),
          waits=[("ld", myld)], inc=("sv", 1))
        sv_aw = C("sv")
        W("vector", lambda e, s_col=s_col: e.tensor_scalar(
            out=s_col, in0=aw_c, scalar1=QMAX_R, scalar2=QEPS,
            op0=ALU.mult, op1=ALU.max),
          waits=[("sv", sv_aw)], inc=("sv", 1))
        sv_sc = C("sv")
        W("vector", lambda e, s_col=s_col, r_col=r_col: e.reciprocal(
            out=r_col, in_=s_col),
          waits=[("sv", sv_sc)], inc=("sv", 1))
        svq = C("sv")
        W("scalar", lambda e, wdst=wdst, r_col=r_col: e.activation(
            out=wdst, in_=wdst, func=AF.Identity, scale=r_col, bias=fc_sb[:, 1:2]),
          waits=[("sv", svq)], inc=("ss", 1))
        ssr = C("ss")
        W("vector", lambda e, wdst=wdst: e.tensor_scalar(
            out=wdst, in0=wdst, scalar1=MAGIC, scalar2=None,
            op0=ALU.subtract),
          waits=[("ss", ssr)], inc=("sv", 1))
        return C("sv"), ssr

    def transpose_q(qsrc, qTdst, nmt, sv_src, evac_eng):
        qTv = qTdst.rearrange("p (mt o) -> p mt o", mt=nmt)
        last = 0
        for msq in range((nmt + 3) // 4):
            bi = 4 + (msq % 2)
            bank = ps[bi]
            cnt = min(4, nmt - msq * 4)
            for s4 in range(cnt):
                ms = msq * 4 + s4
                wl = (p.bank_wait(bi) + [("sv", sv_src)]) if s4 == 0 else []
                W("tensor", lambda e, ms=ms, s4=s4, bank=bank, qsrc=qsrc:
                  e.transpose(bank[:, s4 * 128:(s4 + 1) * 128],
                              qsrc[:, ms * 128:(ms + 1) * 128], idf_sb[:]),
                  waits=wl, inc=("st", 1))
            st_g = C("st")
            dst = qTv[:, msq * 4: msq * 4 + cnt, :]
            src = lambda bank=bank, cnt=cnt: bank[:, 0:cnt * 128].rearrange(
                "p (s t) -> p s t", s=cnt)
            if evac_eng == "vector":
                W("vector", lambda e, dst=dst, bank=bank, cnt=cnt:
                  e.tensor_copy(out=dst, in_=bank[:, 0:cnt * 128].rearrange(
                      "p (s t) -> p s t", s=cnt)),
                  waits=[("st", st_g)], inc=("sv", 1))
                p.bank_set(bi, "sv")
                last = ("sv", C("sv"))
            else:
                W("scalar", lambda e, dst=dst, bank=bank, cnt=cnt: e.copy(
                    out=dst, in_=bank[:, 0:cnt * 128].rearrange(
                        "p (s t) -> p s t", s=cnt)),
                  waits=[("st", st_g)], inc=("ss", 1))
                p.bank_set(bi, "ss")
                last = ("ss", C("ss"))
        return last

    ffn_state = {"h_gp": 0, "prev_q_ss": 0}
    pair_st = {}
    sg_free = [0, 0]
    ffn_q = []
    ffn_qi = [0]
    ffn_bank = [6]

    def drip(nops):
        done = 0
        while done < nops and ffn_qi[0] < len(ffn_q):
            ffn_q[ffn_qi[0]]()
            ffn_qi[0] += 1
            done += 1

    def next_fb():
        fb = ffn_bank[0]
        ffn_bank[0] = 13 - fb
        return fb

    def queue_ffn_pair(i):
        par = i % 2
        m = {}
        qT2 = {"g": qTg_bf, "u": qTu_bf}
        wt2 = {"g": wgu_g[par], "u": wgu_u[par]}
        sw2 = {"g": sw_g2[par], "u": sw_u2[par]}
        rw2 = {"g": rw_g2[par], "u": rw_u2[par]}

        def c_load():
            wlg = [("sv", xq_sv), ("ss", ffn_state["prev_q_ss"])]
            if i >= 2:
                wlg.append(("st", pair_st[i - 2]))
            W("sync", lambda e: e.dma_start(
                out=wgu_g[par], in_=w_gu[i * 128:(i + 1) * 128, :]),
              waits=wlg, inc=("ld", 16))
            W("sync", lambda e: e.dma_start(
                out=wgu_u[par],
                in_=w_gu[(NI + i) * 128:(NI + i + 1) * 128, :]),
              inc=("ld", 16))
            m["ld"] = C("ld")
        ffn_q.append(c_load)

        def c_scale(gu):
            W("vector", lambda e: e.tensor_reduce(
                out=sw2[gu], in_=wt2[gu], axis=AX, op=ALU.max,
                apply_absolute_value=True),
              waits=[("ld", m["ld"])], inc=("sv", 1))
            hop = C("sv")
            W("vector", lambda e: e.tensor_scalar(
                out=sw2[gu], in0=sw2[gu], scalar1=QMAX_R, scalar2=QEPS,
                op0=ALU.mult, op1=ALU.max),
              waits=[("sv", hop)], inc=("sv", 1))
            hop2 = C("sv")
            W("vector", lambda e: e.reciprocal(out=rw2[gu], in_=sw2[gu]),
              waits=[("sv", hop2)], inc=("sv", 1))
            m["rec"] = C("sv")
        ffn_q.append(lambda: c_scale("g"))
        ffn_q.append(lambda: c_scale("u"))

        def c_round(gu):
            W("scalar", lambda e: e.activation(
                out=wt2[gu], in_=wt2[gu], func=AF.Identity, scale=rw2[gu],
                bias=fc_sb[:, 1:2]),
              waits=[("sv", m["rec"])], inc=("ss", 1))
            W("scalar", lambda e: e.activation(
                out=wt2[gu], in_=wt2[gu], func=AF.Identity,
                bias=fc_sb[:, 3:4]),
              inc=("ss", 1))
            m["q_" + gu] = C("ss")
        ffn_q.append(lambda: c_round("g"))
        ffn_q.append(lambda: c_round("u"))

        def c_tr(gu, grp):
            fb = next_fb()
            bank = ps[fb]
            qTv = qT2[gu].rearrange("p (mt o) -> p mt o", mt=16)
            for s4 in range(4):
                ms = grp * 4 + s4
                wl = (p.bank_wait(fb) + [("ss", m["q_" + gu])]
                      if s4 == 0 else [])
                W("tensor", lambda e, ms=ms, s4=s4, bank=bank, gu=gu:
                  e.transpose(bank[:, s4 * 128:(s4 + 1) * 128],
                              wt2[gu][:, ms * 128:(ms + 1) * 128],
                              idf_sb[:]),
                  waits=wl, inc=("st", 1))
            st_g = C("st")
            W("scalar", lambda e, grp=grp, bank=bank, qTv=qTv: e.copy(
                out=qTv[:, grp * 4: grp * 4 + 4, :],
                in_=bank[:, 0:512].rearrange("p (s t) -> p s t", s=4)),
              waits=[("st", st_g)], inc=("ss", 1))
            p.bank_set(fb, "ss")
            m["tr_" + gu] = C("ss")
        for gu in ("g", "u"):
            for grp in range(4):
                ffn_q.append(lambda gu=gu, grp=grp: c_tr(gu, grp))

        def c_chain(gu, tc, q4):
            if q4 == 0:
                fb = next_fb()
                m["fb_" + gu + str(tc)] = fb
                wl0 = (p.bank_wait(fb)
                       + [("ss", m["tr_g"]), ("ss", m["tr_u"]),
                          ("sv", xq_sv)])
            else:
                fb = m["fb_" + gu + str(tc)]
                wl0 = []
            bank = ps[fb]
            for mt in range(q4 * 4, q4 * 4 + 4):
                W("tensor", lambda e, mt=mt, tc=tc, bank=bank, gu=gu:
                  e.matmul(bank[:], qT2[gu][:, mt * 128:(mt + 1) * 128],
                           xq_fm[:, mt * 1024 + tc * 512:
                                 mt * 1024 + (tc + 1) * 512],
                           start=(mt == 0), stop=(mt == 15)),
                  waits=(wl0 if mt == q4 * 4 else []),
                  inc=("st", 1) if mt == 15 else None)
            if mt == 15:
                m["mm_" + gu + str(tc)] = C("st")

        def c_silu(tc):
            hpar = (i * 2 + tc) % 2
            fbG = m["fb_g" + str(tc)]
            W("scalar", lambda e, hpar=hpar, fbG=fbG: e.activation(
                out=sg2[hpar], in_=ps[fbG][:], func=AF.Silu,
                scale=sw2["g"]),
              waits=[("st", m["mm_g" + str(tc)]), ("sv", sg_free[hpar])],
              inc=("ss", 1))
            m["sg" + str(tc)] = C("ss")
            p.bank_set(fbG, "ss")

        def c_h(tc):
            hpar = (i * 2 + tc) % 2
            fbU = m["fb_u" + str(tc)]
            wl_h = [("ss", m["sg" + str(tc)]),
                    ("st", m["mm_u" + str(tc)]),
                    ("gp", ffn_state["h_gp"])]
            if i == 0:
                wl_h.append(("sv", mxmn_init_sv))
            W("vector", lambda e, hpar=hpar, fbU=fbU:
              e.scalar_tensor_tensor(
                  out=hst2[hpar], in0=ps[fbU][:], scalar=sw2["u"],
                  in1=sg2[hpar], op0=ALU.mult, op1=ALU.mult),
              waits=wl_h, inc=("sv", 1))
            sv_h = C("sv")
            sg_free[hpar] = sv_h
            p.bank_set(fbU, "sv")
            W("vector", lambda e, tc=tc, hpar=hpar: e.tensor_tensor(
                out=mx_sb[:, tc * 512:(tc + 1) * 512],
                in0=mx_sb[:, tc * 512:(tc + 1) * 512],
                in1=hst2[hpar], op=ALU.max), inc=("sv", 1))
            W("vector", lambda e, tc=tc, hpar=hpar: e.tensor_tensor(
                out=mn_sb[:, tc * 512:(tc + 1) * 512],
                in0=mn_sb[:, tc * 512:(tc + 1) * 512],
                in1=hst2[hpar], op=ALU.min), inc=("sv", 1))
            m["h" + str(tc)] = sv_h

        def c_spill(tc):
            hpar = (i * 2 + tc) % 2
            W("gpsimd", lambda e, i=i, tc=tc, hpar=hpar: e.dma_start(
                out=h_dram[i * 128:(i + 1) * 128,
                           tc * 512:(tc + 1) * 512],
                in_=hst2[hpar]),
              waits=[("sv", m["h" + str(tc)])], inc=("gp", 16))
            ffn_state["h_gp"] = C("gp")

        for tc in range(2):
            for q4 in range(4):
                ffn_q.append(lambda tc=tc, q4=q4: c_chain("g", tc, q4))
            for q4 in range(4):
                ffn_q.append(lambda tc=tc, q4=q4: c_chain("u", tc, q4))
            ffn_q.append(lambda tc=tc: c_silu(tc))
            ffn_q.append(lambda tc=tc: c_h(tc))
            ffn_q.append(lambda tc=tc: c_spill(tc))

        def c_fin():
            pair_st[i] = C("st")
            ffn_state["prev_q_ss"] = m["q_u"]
        ffn_q.append(c_fin)
    ld_marks = {}

    def reload_k(k, extra_waits):
        par = k % 2
        W("sync", lambda e, k=k, par=par: e.dma_start(
            out=xact_k[par], in_=xact_dram[k * 128:(k + 1) * 128, :]),
          waits=extra_waits, inc=("ld", 16))
        for ti in range(2):
            W("sync", lambda e, k=k, par=par, ti=ti: e.dma_start(
                out=c_sb[par][ti],
                in_=c_dram[k * 256 + ti * 128: k * 256 + (ti + 1) * 128, :]),
              inc=("ld", 16))
        if k % 2 == 0:
            kp = k // 2
            W("sync", lambda e, kp=kp: e.dma_start(
                out=b_sb[:, (kp % 2) * 2048:(kp % 2 + 1) * 2048],
                in_=b_dram[kp * 128:(kp + 1) * 128, :]),
              inc=("ld", 16))
        ld_marks[k] = C("ld")

    W("vector", lambda e: e.memset(mx_sb, -3.0e38), inc=("sv", 1))
    W("vector", lambda e: e.memset(mn_sb, 3.0e38), inc=("sv", 1))
    mxmn_init_sv = C("sv")
    reload_k(0, [("gp", p2b_gp), ("sv", p2b_sv), ("ss", p5a_ss),
                 ("st", p5a_st)])
    for i in range(NI):
        queue_ffn_pair(i)
    p.phase_mark("P3r")

    kuse_st = {}
    gate_sv_hist = {}
    yg_spill_prev = 0
    zs_ld = {}
    prev_gate_sv = p2b_sv
    dA_sv_done = {}
    u_marks = {}
    yb_marks = {}
    for k in range(KT):
        par = k % 2
        myld = ld_marks[k]
        dtu = {}
        for tcp in range(4):
            bi = 4 + (tcp % 2)
            bank = ps[bi]
            wl = p.bank_wait(bi)
            if tcp == 0:
                wl = wl + [("sv", max(prev_gate_sv, dtm_sv)),
                           ("st", p5a_st)]
            W("tensor", lambda e, k=k, tcp=tcp, bank=bank: e.matmul(
                bank[:], wdtp_sb[:, k * 128:(k + 1) * 128],
                dtm_sb[:, tcp * 512:(tcp + 1) * 512],
                start=True, stop=True),
              waits=wl, inc=("st", 1))
            st_dt = C("st")
            sl = slice(tcp * 512, (tcp + 1) * 512)
            W("scalar", lambda e, k=k, bank=bank, sl=sl: e.activation(
                out=t1_k[:, sl], in_=bank[:], func=AF.Abs,
                bias=db_sb[:, k:k + 1]),
              waits=[("st", st_dt)], inc=("ss", 1))
            ss_a = C("ss")
            W("vector", lambda e, k=k, bank=bank, sl=sl: e.tensor_scalar(
                out=dt_kv[:, sl], in0=bank[:], scalar1=db_sb[:, k:k + 1],
                scalar2=0.0, op0=ALU.add, op1=ALU.max),
              waits=[("st", st_dt), ("ss", ss_a)], inc=("sv", 1))
            sv_d = C("sv")
            p.bank_free[bi] = [("ss", ss_a), ("sv", sv_d)]
        if k == 0:
            p.phase_mark("P3m")
        W("scalar", lambda e: e.activation(out=t1_k[:], in_=t1_k[:],
                                           func=AF.Exp, scale=-1.0),
          inc=("ss", 1))
        W("scalar", lambda e: e.activation(out=t1_k[:], in_=t1_k[:],
                                           func=AF.Ln, bias=1.0),
          inc=("ss", 1))
        ss_sp = C("ss")
        W("vector", lambda e: e.tensor_tensor(out=dt_kv[:], in0=dt_kv[:],
                                              in1=t1_k[:], op=ALU.add),
          waits=[("ss", ss_sp)], inc=("sv", 1))
        if k == 0:
            p.phase_mark("P3sp")
        W("vector", lambda e, par=par: e.tensor_tensor(
            out=dtx_kv[:], in0=dt_kv[:], in1=xact_k[par], op=ALU.mult),
          waits=[("ld", myld)], inc=("sv", 1))
        sv_dt = C("sv")
        if k == 0:
            p.phase_mark("P3dt")
        ss_dA = {}

        def emit_dA(n, k=k, sv_dt=sv_dt):
            npar = n % 2
            wl_dA = [("sv", sv_dt)]
            if (n - 2) in dA_sv_done:
                wl_dA.append(("sv", dA_sv_done[n - 2]))
            W("scalar", lambda e, k=k, n=n, npar=npar: e.activation(
                out=dA_k[npar][:], in_=dt_kv[:], func=AF.Exp,
                scale=a_sb[:, k * NN + n: k * NN + n + 1]),
              waits=wl_dA, inc=("ss", 1))
            ss_dA[n] = C("ss")

        emit_dA(0)
        emit_dA(1)
        sv_yb = {}
        ysum_st = {}

        def emit_ysum(jt):
            nj, tcj = jt // NTC, jt % NTC
            wl = [("sv", sv_yb[jt])]
            if nj == 0:
                wl = wl + p.bank_wait(tcj)
            W("tensor", lambda e, nj=nj, tcj=tcj, jt=jt: e.matmul(
                ps[tcj][:], idb_sb[:], yb3[jt % 3],
                start=(nj == 0), stop=(nj == NN - 1)),
              waits=wl, inc=("st", 1))
            ysum_st[jt] = C("st")

        for n in range(NN):
            npar = n % 2
            for tc in range(NTC):
                it = n * NTC + tc
                tsl = slice(tc * 512, (tc + 1) * 512)
                if tc == 0 and 1 <= n < NN - 1:
                    emit_dA(n + 1)
                wlB = p.bank_wait(4) + ([("ld", myld)] if it == 0 else [])
                W("tensor", lambda e, k=k, n=n, tsl=tsl: e.matmul(
                    ps[4][:], eb_sb[(k % 2) * 64:(k % 2) * 64 + 64,
                                    n * 128:(n + 1) * 128],
                    b_sb[(k % 2) * 64:(k % 2) * 64 + 64,
                         ((k // 2) % 2) * 2048 + tsl.start:
                         ((k // 2) % 2) * 2048 + tsl.stop],
                    start=True, stop=True),
                  waits=wlB, inc=("st", 1))
                st_B = C("st")
                W("vector", lambda e, tsl=tsl, it=it:
                  e.tensor_tensor(out=u_sb[it % 2], in0=dtx_kv[:, tsl],
                                  in1=ps[4][:], op=ALU.mult),
                  waits=[("st", st_B), ("ss", ss_dA[n])], inc=("sv", 1))
                p.bank_set(4, "sv")
                init = 0.0 if tc == 0 else h_sb[:, tc * 512 - 1: tc * 512]
                W("vector", lambda e, tsl=tsl, it=it, npar=npar,
                  init=init: e.tensor_tensor_scan(
                      out=h_sb[:, tsl], data0=dA_k[npar][:, tsl],
                      data1=u_sb[it % 2], initial=init, op0=ALU.mult,
                      op1=ALU.add), inc=("sv", 1))
                if tc == NTC - 1:
                    dA_sv_done[n] = C("sv")
                wlC = p.bank_wait(5) + ([("ld", myld)] if it == 0 else [])
                W("tensor", lambda e, k=k, n=n, tsl=tsl: e.matmul(
                    ps[5][:], ec_sb[:, (n % 8) * 128:((n % 8) + 1) * 128],
                    c_sb[k % 2][n // 8][:, tsl], start=True, stop=True),
                  waits=wlC, inc=("st", 1))
                st_C = C("st")
                wlY = [("st", st_C)]
                if (it - 3) in ysum_st:
                    wlY.append(("st", ysum_st[it - 3]))
                W("vector", lambda e, tsl=tsl, it=it:
                  e.tensor_tensor(out=yb3[it % 3], in0=h_sb[:, tsl],
                                  in1=ps[5][:], op=ALU.mult),
                  waits=wlY, inc=("sv", 1))
                p.bank_set(5, "sv")
                sv_yb[it] = C("sv")
                if it >= 1:
                    emit_ysum(it - 1)
                drip(2)
            if n == 8:
                W("sync", lambda e, k=k: e.dma_start(
                    out=zs_one, in_=zs_dram[k * 128:(k + 1) * 128, :]),
                  waits=[("sv", prev_gate_sv)], inc=("ld", 16))
                zs_ld[k] = C("ld")
            if n == 0 and k + 1 < KT:
                gates = [("sv", C("sv")), ("st", C("st"))]
                if k >= 1:
                    gates += [("sv", gate_sv_hist.get(k - 1, 0)),
                              ("st", kuse_st.get(k - 1, 0))]
                reload_k(k + 1, gates)
            if k == 0 and n == 0:
                p.phase_mark("P3n0")
        emit_ysum(NN * NTC - 1)
        st_ysum = C("st")
        for tc in range(NTC):
            tsl = slice(tc * 512, (tc + 1) * 512)
            W("vector", lambda e, k=k, tsl=tsl, tc=tc, par=par:
              e.scalar_tensor_tensor(
                  out=u_sb[0], in0=xact_k[par][:, tsl],
                  scalar=dv_sb[:, k:k + 1], in1=ps[tc][:],
                  op0=ALU.mult, op1=ALU.add),
              waits=[("st", st_ysum)], inc=("sv", 1))
            p.bank_set(tc, "sv")
            W("vector", lambda e, k=k, tsl=tsl: e.tensor_tensor(
                out=ygst[:, tsl], in0=u_sb[0], in1=zs_one[:, tsl],
                op=ALU.mult),
              waits=(([("gp", yg_spill_prev), ("ld", zs_ld[k])])
                     if tsl.start == 0 else []),
              inc=("sv", 1))
        prev_gate_sv = C("sv")
        gate_sv_hist[k] = prev_gate_sv
        W("gpsimd", lambda e, k=k: e.dma_start(
            out=yg_dram[k * 128:(k + 1) * 128, :], in_=ygst[:]),
          waits=[("sv", prev_gate_sv)], inc=("gp", 16))
        yg_spill_prev = C("gp")
        kuse_st[k] = C("st")
        drip(4)
        if k == 0:
            p.phase_mark("P3k0")
    drip(10 ** 9)
    p3_sv = C("sv")
    p3_st = C("st")
    p.phase_mark("P3")

    yg_spill_gp = C("gp")
    wo_v = w_out_t.rearrange("(mt p) o -> p mt o", p=128)
    wogv2 = [w.rearrange("p (mt o) -> p mt o", mt=16) for w in wout2]
    att_st2 = [AR1[:, 7168:7680], AR1[:, 7680:8192]]
    att_gp = [0, 0]
    wout_gp = {}

    def load_wout(og, gate):
        ws = "w" + str(og % 2)
        W("gpsimd", lambda e, og=og: e.dma_start(
            out=wogv2[og % 2], in_=wo_v[:, :, og * 256:(og + 1) * 256]),
          waits=gate, inc=(ws, 16))
        wout_gp[og] = (ws, C(ws))

    ykst = {}
    prev_st_og = 0
    load_wout(0, [("sv", p3_sv), ("st", p3_st)])
    for og in range(8):
        for k in range(16):
            slot = og * 16 + k
            s3 = slot % 3
            ys = "y" + str(s3)
            wl_yg = [("gp", yg_spill_gp)]
            if slot >= 3:
                wl_yg.append(("st", ykst[slot - 3]))
            else:
                wl_yg += [("sv", p3_sv), ("st", p3_st)]
            W("sync", lambda e, k=k, s3=s3: e.dma_start(
                out=ygb3[s3][:, 0:1024],
                in_=yg_dram[k * 128:(k + 1) * 128, 0:1024]),
              waits=wl_yg, inc=(ys, 16))
            W("scalar", lambda e, k=k, s3=s3: e.dma_start(
                out=ygb3[s3][:, 1024:2048],
                in_=yg_dram[k * 128:(k + 1) * 128, 1024:2048]),
              waits=list(wl_yg), inc=(ys, 16))
            myld = C(ys)
            for o2 in range(2):
                for tc in range(4):
                    bi = o2 * 4 + tc
                    wl0 = []
                    if o2 == 0 and tc == 0:
                        wl0 = [wout_gp[og], (ys, myld)]
                    if k == 0:
                        wl0 = wl0 + p.bank_wait(bi)
                    W("tensor", lambda e, k=k, o2=o2, tc=tc, s3=s3, bi=bi,
                      og=og: e.matmul(
                          ps[bi][:],
                          wout2[og % 2][:, k * 256 + o2 * 128:
                                        k * 256 + o2 * 128 + 128],
                          ygb3[s3][:, tc * 512:(tc + 1) * 512],
                          start=(k == 0), stop=(k == 15)),
                      waits=wl0,
                      inc=("st", 1) if (o2 == 1 and tc == 3) else None)
            ykst[slot] = C("st")
        st_og = C("st")
        if og + 1 < 8:
            load_wout(og + 1, [("sv", p3_sv), ("st", p3_st)] if og == 0
                      else [("st", prev_st_og)])
        for o2 in range(2):
            for tc in range(4):
                bi = o2 * 4 + tc
                apar = (og * 8 + bi) % 2
                W("vector", lambda e, bi=bi, apar=apar: e.tensor_copy(
                    out=att_st2[apar], in_=ps[bi][:]),
                  waits=[("st", st_og), ("gp", att_gp[apar])], inc=("sv", 1))
                p.bank_set(bi, "sv")
                sv_e = C("sv")
                ot = og * 2 + o2
                tb = tc // 2
                W("gpsimd", lambda e, ot=ot, tc=tc, tb=tb, apar=apar:
                  e.dma_start(out=cc_in[tb * 2048 + ot * 128:
                                        tb * 2048 + (ot + 1) * 128,
                                        (tc % 2) * 512:((tc % 2) + 1) * 512],
                              in_=att_st2[apar]),
                  waits=[("sv", sv_e)], inc=("gp", 16))
                att_gp[apar] = C("gp")
        prev_st_og = st_og
    p4_gp = C("gp")
    p4_st = C("st")
    p4_sv = C("sv")
    W("gpsimd", lambda e: e.collective_compute(
        "ReduceScatter", ALU.add, replica_groups=RG_PAIRS,
        ins=[cc_in.opt()], outs=[cc_out.opt()]),
      waits=[("gp", p4_gp)], inc=("cc", 1))

    p.phase_mark("P4")


    p.phase_mark("P5a")

    p5b_sv = C("sv")
    p5b_gp = C("gp")

    p.phase_mark("P5b")

    W("vector", lambda e: e.tensor_scalar(out=hax, in0=mn_sb, scalar1=-1.0,
                                          scalar2=None, op0=ALU.mult),
      waits=[("sv", p5b_sv)], inc=("sv", 1))
    W("vector", lambda e: e.tensor_tensor(out=hax, in0=hax, in1=mx_sb,
                                          op=ALU.max), inc=("sv", 1))
    sv_hax = C("sv")
    for ti in range(8):
        bi = 4 + (ti % 2)
        bank = ps[bi]
        W("tensor", lambda e, ti=ti, bank=bank: e.transpose(
            bank[:, 0:128], hax[:, ti * 128:(ti + 1) * 128], idf_sb[:]),
          waits=p.bank_wait(bi) + [("sv", sv_hax)], inc=("st", 1))
        st_t = C("st")
        W("vector", lambda e, ti=ti, bank=bank: e.tensor_reduce(
            out=shcol[:, ti:ti + 1], in_=bank[:, 0:128], axis=AX,
            op=ALU.max), waits=[("st", st_t)], inc=("sv", 1))
        p.bank_set(bi, "sv")
    sv_red8 = C("sv")
    W("vector", lambda e: e.tensor_scalar(out=shcol, in0=shcol,
                                          scalar1=QMAX_R, scalar2=QEPS,
                                          op0=ALU.mult, op1=ALU.max),
      waits=[("sv", sv_red8)], inc=("sv", 1))
    sv_shc = C("sv")
    W("vector", lambda e: e.reciprocal(out=rhcol, in_=shcol),
      waits=[("sv", sv_shc)], inc=("sv", 1))
    sv_sh = C("sv")
    W("tensor", lambda e: e.transpose(ps[6][0:8, 0:128], shcol, idf_sb[:]),
      waits=p.bank_wait(6) + [("sv", sv_sh)], inc=("st", 1))
    W("tensor", lambda e: e.transpose(ps[7][0:8, 0:128], rhcol, idf_sb[:]),
      waits=p.bank_wait(7), inc=("st", 1))
    st_r = C("st")
    W("vector", lambda e: e.tensor_copy(out=shrow, in_=ps[6][0:8, 0:128]),
      waits=[("st", st_r)], inc=("sv", 1))
    W("vector", lambda e: e.tensor_copy(out=rhrow, in_=ps[7][0:8, 0:128]),
      inc=("sv", 1))
    p.bank_set(6, "sv")
    p.bank_set(7, "sv")
    sv_rw = C("sv")
    W("vector", lambda e: e.tensor_copy(out=shrh, in_=shrow),
      waits=[("sv", sv_rw)], inc=("sv", 1))
    sv_h1 = C("sv")
    W("vector", lambda e: e.tensor_tensor(out=shrl, in0=shrow, in1=shrh,
                                          op=ALU.subtract),
      waits=[("sv", sv_h1)], inc=("sv", 1))
    W("vector", lambda e: e.tensor_copy(out=rhrh, in_=rhrow), inc=("sv", 1))
    sv_h2 = C("sv")
    W("vector", lambda e: e.tensor_tensor(out=rhrl, in0=rhrow, in1=rhrh,
                                          op=ALU.subtract),
      waits=[("sv", sv_h2)], inc=("sv", 1))
    sv_rows = C("sv")
    for dst, rh, rl in ((shb_sb, shrh, shrl), (rhb_sb, rhrh, rhrl)):
        for ti in range(8):
            bi = 4 + (ti % 2)
            bank = ps[bi]
            W("tensor", lambda e, bank=bank, rh=rh, ti=ti: e.matmul(
                bank[:, 0:128], ones_sb[:, ti * 128:(ti + 1) * 128],
                rh[0:8, :], start=True, stop=False),
              waits=p.bank_wait(bi) + [("sv", sv_rows)])
            W("tensor", lambda e, bank=bank, rl=rl, ti=ti: e.matmul(
                bank[:, 0:128], ones_sb[:, ti * 128:(ti + 1) * 128],
                rl[0:8, :], start=False, stop=True), inc=("st", 1))
            st_e = C("st")
            W("vector", lambda e, dst=dst, ti=ti, bank=bank: e.tensor_copy(
                out=dst[:, ti * 128:(ti + 1) * 128], in_=bank[:, 0:128]),
              waits=[("st", st_e)], inc=("sv", 1))
            p.bank_set(bi, "sv")
    sv_bcast = C("sv")

    p.phase_mark("P5c")

    hq_prev = {}
    hq_mult = {}
    for kI in range(NI):
        for tc in range(2):
            j = kI * 2 + tc
            hpar = j % 2
            wl = [("gp", p5b_gp)] if j == 0 else []
            if j >= 1:
                wl = wl + [("sv", hq_mult.get(j - 1, 0))]
            if j >= 2:
                wl = wl + list(hq_prev[j - 2])
            W("sync", lambda e, kI=kI, tc=tc, hpar=hpar: e.dma_start(
                out=hst2[hpar],
                in_=h_dram[kI * 128:(kI + 1) * 128,
                           tc * 512:(tc + 1) * 512]),
              waits=wl + [("sv", sv_bcast)], inc=("ld", 16))
            myld = C("ld")
            W("vector", lambda e, hpar=hpar, tc=tc: e.tensor_tensor(
                out=hst2[hpar], in0=hst2[hpar],
                in1=rhb_sb[:, tc * 512:(tc + 1) * 512], op=ALU.mult),
              waits=[("ld", myld)], inc=("sv", 1))
            svm = C("sv")
            hq_mult[j] = svm
            W("scalar", lambda e, hpar=hpar: e.activation(
                out=hst2[hpar], in_=hst2[hpar], func=AF.Identity,
                bias=fc_sb[:, 1:2]),
              waits=[("sv", svm)], inc=("ss", 1))
            ssm = C("ss")
            W("vector", lambda e, kI=kI, tc=tc, hpar=hpar: e.tensor_scalar(
                out=hq_ap(kI, tc * 512, (tc + 1) * 512), in0=hst2[hpar],
                scalar1=MAGIC, scalar2=None, op0=ALU.subtract),
              waits=[("ss", ssm)], inc=("sv", 1))
            hq_prev[j] = [("sv", C("sv")), ("ss", ssm)]
    hq_sv = C("sv")

    p.phase_mark("P5d")

    prev_dn_st = 0
    dn_prev = {}
    out_gp_prev = {}
    for dm in range(16):
        wl0 = ([("sv", hq_sv)] if dm == 0 else
               [("st", prev_dn_st)] + list(dn_prev.get(2 * (dm - 1), [])))
        dld = load_w_tile(wdn_f, w_dn, dm * 128, wl0)
        svq, _ssq = quant_w_tile(wdn_f, dld, sw2_c, rw2_c)
        mT = transpose_q(wdn_f, q2T_bf, NI, svq, "vector")
        for tc in range(2):
            bi = tc
            bank = ps[bi]
            wlD = p.bank_wait(bi) + [mT, ("sv", hq_sv)]
            for kI in range(NI):
                W("tensor", lambda e, kI=kI, tc=tc, bank=bank: e.matmul(
                    bank[:], q2T_bf[:, kI * 128:(kI + 1) * 128],
                    hq_ap(kI, tc * 512, (tc + 1) * 512),
                    start=(kI == 0), stop=(kI == NI - 1)),
                  waits=(wlD if kI == 0 else []),
                  inc=("st", 1) if kI == NI - 1 else None)
            st_mm = C("st")
            j5 = dm * 2 + tc
            dpar = j5 % 2
            wl_att = ([("cc", 1), ("sv", svq), mT]
                      + list(dn_prev.get(j5 - 1, []))
                      + list(dn_prev.get(j5 - 2, [])))
            W("sync", lambda e, dm=dm, tc=tc, dpar=dpar: e.dma_start(
                out=att_rl[dpar],
                in_=cc_out[dm * 128:(dm + 1) * 128,
                           tc * 512:(tc + 1) * 512]),
              waits=wl_att, inc=("ld", 16))
            W("sync", lambda e, dm=dm, tc=tc, dpar=dpar: e.dma_start(
                out=xres2[dpar],
                in_=x_ffn_t[dm * 128:(dm + 1) * 128,
                            tc * 512:(tc + 1) * 512]),
              inc=("ld", 16))
            myld = C("ld")
            W("vector", lambda e, tc=tc, bank=bank, dpar=dpar:
              e.scalar_tensor_tensor(
                  out=dnst2[dpar], in0=bank[:], scalar=sw2_c,
                  in1=shb_sb[:, tc * 512:(tc + 1) * 512],
                  op0=ALU.mult, op1=ALU.mult),
              waits=[("st", st_mm), ("gp", out_gp_prev.get(j5 - 2, 0))],
              inc=("sv", 1))
            p.bank_set(bi, "sv")
            W("vector", lambda e, dpar=dpar: e.tensor_tensor(
                out=dnst2[dpar], in0=dnst2[dpar], in1=att_rl[dpar],
                op=ALU.add),
              waits=[("ld", myld)], inc=("sv", 1))
            W("vector", lambda e, dpar=dpar: e.tensor_tensor(
                out=dnst2[dpar], in0=dnst2[dpar], in1=xres2[dpar],
                op=ALU.add), inc=("sv", 1))
            sv_o = C("sv")
            dn_prev[j5] = [("sv", sv_o)]
            W("gpsimd", lambda e, dm=dm, tc=tc, dpar=dpar: e.dma_start(
                out=out_d[dm * 128:(dm + 1) * 128,
                          tc * 512:(tc + 1) * 512],
                in_=dnst2[dpar]),
              waits=[("sv", sv_o)], inc=("gp", 16))
            out_gp_prev[j5] = C("gp")
        prev_dn_st = C("st")
    if dbg:
        for nm, t, shp in (("d_zs", zs_dram, [2048, T]),
                           ("d_xact", xact_dram, [2048, T]),
                           ("d_b", b_dram, [1024, T]),
                           ("d_c", c_dram, [4096, T]),
                           ("d_h", h_dram, [II, TH]),
                           ("d_ccin", cc_in, [2 * DM, TH]),
                           ("d_ccout", cc_out, [DM, TH])):
            dt_ = BF16 if nm in ("d_b", "d_c") else F32
            od = nc.dram_tensor(nm, shp, dt_, kind="ExternalOutput").ap()
            W("gpsimd", lambda e, od=od, t=t: e.dma_start(out=od, in_=t),
              waits=[("gp", C("gp")), ("cc", 1), ("sv", C("sv")),
                     ("ss", C("ss")), ("st", C("st")), ("ld", C("ld"))],
              inc=("gp", 16))
    out_gp_total = C("gp")
    W("gpsimd", lambda e: e.nop(), waits=[("gp", out_gp_total)])

    if cut is not None:
        p.truncate(cut)

    with es:
        sem_handles = {}
        for s in SEMS:
            sem_handles[s] = es.enter_context(nc.semaphore(s))
        block = es.enter_context(nc.Block())

        @block.sync
        def _(sync):
            p.emit_engine("sync", sync, sem_handles)

        @block.gpsimd
        def _(gpsimd):
            p.emit_engine("gpsimd", gpsimd, sem_handles)

        @block.scalar
        def _(scalar):
            p.emit_engine("scalar", scalar, sem_handles)

        @block.vector
        def _(vector):
            p.emit_engine("vector", vector, sem_handles)

        @block.tensor
        def _(tensor):
            p.emit_engine("tensor", tensor, sem_handles)
    return nc


def _prep_core_inputs(b, j, x, Wi, Wdti, Wdtp, dtb, cwi, cbi, alogi, dvi, Wo,
                      Wgu, Wdn, consts):
    DI, DXB = 4096, 1024
    ch0 = j * 2048
    xb = np.ascontiguousarray(x[b])
    d = {}
    d["x_full"] = xb
    d["x_ffn"] = np.ascontiguousarray(xb[j * TH:(j + 1) * TH])
    d["x_ffn_t"] = np.ascontiguousarray(d["x_ffn"].T)
    rows = [np.arange(ch0, ch0 + 2048)]
    xr = []
    for k in range(16):
        for g2 in range(2):
            g = j * 32 + 2 * k + g2
            for pp in range(16):
                xr.append(DI + g * 16 + pp)
    rows.append(np.array(xr))
    br = []
    for k in range(16):
        for n in range(16):
            for g2 in range(2):
                g = j * 32 + 2 * k + g2
                br.append(DI + DXB + g * 16 + n)
    rows.append(np.array(br))
    cr = []
    for k in range(16):
        for n in range(16):
            for hh in range(8):
                h = j * 128 + k * 8 + hh
                cr.append(DI + 2 * DXB + h * 16 + n)
    rows.append(np.array(cr))
    rows = np.concatenate(rows)
    d["w_in_t"] = np.ascontiguousarray(Wi[rows].T)
    d["w_dtin_t"] = np.ascontiguousarray(Wdti.T)
    d["w_dtp_t"] = np.ascontiguousarray(Wdtp[ch0:ch0 + 2048].T)
    d["w_out_t"] = np.ascontiguousarray(Wo[:, ch0:ch0 + 2048].T)
    d["w_gu"] = Wgu
    d["w_dn"] = Wdn
    d["dt_b"] = np.ascontiguousarray(dtb[ch0:ch0 + 2048].reshape(16, 128).T)
    d["a_log"] = np.ascontiguousarray(
        alogi[ch0:ch0 + 2048].reshape(16, 128, 16).transpose(1, 0, 2)
        .reshape(128, 256))
    d["conv_w"] = np.ascontiguousarray(
        cwi[ch0:ch0 + 2048].reshape(16, 128, 4).transpose(1, 0, 2)
        .reshape(128, 64))
    d["conv_b"] = np.ascontiguousarray(cbi[ch0:ch0 + 2048].reshape(16, 128).T)
    d["d_vec"] = np.ascontiguousarray(dvi[ch0:ch0 + 2048].reshape(16, 128).T)
    d.update(consts)
    return d


def _mk_esel8():
    e8 = np.zeros((8, 8, 128), BF)
    for ti in range(8):
        e8[ti, ti, :] = 1
    return np.ascontiguousarray(e8.reshape(8, 8 * 128))


def _make_consts():
    eb = np.zeros((128, NN, 128), BF)
    for kpar in range(2):
        for n in range(NN):
            for q in range(128):
                g = q // 64
                for s in range(2):
                    eb[kpar * 64 + s * 32 + n * 2 + g, n, q] = 1
    ec = np.zeros((128, 8, 128), BF)
    for nm in range(8):
        for q in range(128):
            hh = q // 16
            for s in range(2):
                ec[nm * 16 + s * 8 + hh, nm, q] = 1
    ex = np.zeros((128, 2, 128), BF)
    for kpar in range(2):
        for q in range(128):
            g = q // 64
            pp = q % 16
            for s in range(2):
                ex[kpar * 64 + s * 32 + g * 16 + pp, kpar, q] = 1
    return {
        "eb_in": np.ascontiguousarray(eb.reshape(128, NN * 128)),
        "ec_in": np.ascontiguousarray(ec.reshape(128, 8 * 128)),
        "ex_in": np.ascontiguousarray(ex.reshape(128, 2 * 128)),
        "ones_in": _mk_esel8(),
        "ident_bf": np.eye(128).astype(BF),
        "fconst": np.tile(np.array([[LNEPS, MAGIC, -0.5, -MAGIC, 0, 0, 0, 0]],
                                   np.float32), (128, 1)),
        "ident_f32": np.eye(128, dtype=np.float32),
    }


_NC_CACHE = {}


def kernel(L0_attn_hidden_states, L0_attn_in_proj_w, L0_attn_dt_in_proj_w,
           L0_attn_dt_proj_w, L0_attn_dt_proj_b, L0_attn_conv_w,
           L0_attn_conv_b, L0_attn_A_log, L0_attn_D, L0_attn_out_proj_w,
           L0_ffn_w_gate_up, L0_ffn_w_down):
    f = np.float32
    x = np.asarray(L0_attn_hidden_states, f)
    Wi = np.asarray(L0_attn_in_proj_w, f)
    Wdti = np.asarray(L0_attn_dt_in_proj_w, f)
    Wdtp = np.asarray(L0_attn_dt_proj_w, f)
    dtb = np.asarray(L0_attn_dt_proj_b, f)
    cwi = np.asarray(L0_attn_conv_w, f)
    cbi = np.asarray(L0_attn_conv_b, f)
    alogi = np.asarray(L0_attn_A_log, f)
    dvi = np.asarray(L0_attn_D, f)
    Wo = np.asarray(L0_attn_out_proj_w, f)
    Wgu = np.asarray(L0_ffn_w_gate_up, f)
    Wdn = np.asarray(L0_ffn_w_down, f)

    consts = _make_consts()
    in_maps = []
    for core in range(8):
        b, j = core // 2, core % 2
        in_maps.append(_prep_core_inputs(b, j, x, Wi, Wdti, Wdtp, dtb, cwi,
                                         cbi, alogi, dvi, Wo, Wgu, Wdn,
                                         consts))
    if "nc" not in _NC_CACHE:
        _NC_CACHE["nc"] = build_nc()
    nc = _NC_CACHE["nc"]
    res = run_bass_kernel_spmd(nc, in_maps, core_ids=list(range(8)))
    out = np.empty((4, T, DM), np.float32)
    for core in range(8):
        b, j = core // 2, core % 2
        out[b, j * TH:(j + 1) * TH, :] = np.asarray(
            res.results[core]["out"], np.float32).T
    return out
